# version 19
# baseline (speedup 1.0000x reference)
import numpy as np
import ml_dtypes
from contextlib import ExitStack

import concourse.bass as bass
import concourse.mybir as mybir
from concourse.bass_utils import run_bass_kernel_spmd

F32 = mybir.dt.float32
BF16 = mybir.dt.bfloat16
I32 = mybir.dt.int32
AF = mybir.ActivationFunctionType
ALU = mybir.AluOpType
AX = mybir.AxisListType
bfdt = ml_dtypes.bfloat16

NCORES = 8
D = 1024
H = 16
DH = 64
DL = 256
HID = 3072
NTOK = 4 * 8192
PER_CORE = NTOK // NCORES
NT = 512
SUB = NT // 128
TILES = PER_CORE // NT
EPS = 1e-5
QSCALE = 1.0 / 127.0
RING = 10
SLOTW = 1536

ENGINES = ("SP", "GP", "PE", "ACT", "DVE")


class Builder:

    SEMS = (["X", "P", "A", "V", "F", "G", "C0", "C1", "C2", "C3",
             "O0", "O1"]
            + ["W%d" % i for i in range(RING)]
            + ["AW%d" % i for i in range(16)])

    def __init__(self):
        self.ops = {e: [] for e in ENGINES}
        self.cnt = {s: 0 for s in self.SEMS}

    def emit(self, eng, fn, waits=(), inc=None, amt=1, extra_inc=None):
        ws = [w for w in waits if w is not None]
        self.ops[eng].append((fn, ws, inc, amt, extra_inc))
        ms = None
        if inc is not None:
            self.cnt[inc] += amt
            ms = (inc, self.cnt[inc])
        if extra_inc is not None:
            s2, a2 = extra_inc
            self.cnt[s2] += a2
        return ms


def build_program(n_tiles=TILES, taps=False, stop_after=None):
    nc = bass.Bass("TRN2", target_bir_lowering=False)

    x_in = nc.dram_tensor("x", [PER_CORE, D], F32, kind="ExternalInput").ap()
    wq_r = nc.dram_tensor("wq_r", [D, D], F32, kind="ExternalInput").ap()
    wdn_r = nc.dram_tensor("wdn_r", [D, DL], F32, kind="ExternalInput").ap()
    wup_r = nc.dram_tensor("wup_r", [DL, 3072], F32, kind="ExternalInput").ap()
    wo_r = nc.dram_tensor("wo_r", [D, D], F32, kind="ExternalInput").ap()
    wg_r = nc.dram_tensor("wg_r", [D, HID], I32, kind="ExternalInput").ap()
    wu_r = nc.dram_tensor("wu_r", [D, HID], I32, kind="ExternalInput").ap()
    wd_r = nc.dram_tensor("wd_r", [HID, D], I32, kind="ExternalInput").ap()
    cb_identb = nc.dram_tensor("cb_identb", [128, 128], BF16, kind="ExternalInput").ap()
    cb_ident3 = nc.dram_tensor("cb_ident3", [128, 128], F32, kind="ExternalInput").ap()
    cb_credc = nc.dram_tensor("cb_credc", [128, 256], BF16, kind="ExternalInput").ap()
    cb_idstk = nc.dram_tensor("cb_idstk", [128, 64], BF16, kind="ExternalInput").ap()
    cb_selc = nc.dram_tensor("cb_selc", [128, 64, 128], BF16, kind="ExternalInput").ap()
    cb_bq = nc.dram_tensor("cb_bq", [128, 8], F32, kind="ExternalInput").ap()
    cb_bdn = nc.dram_tensor("cb_bdn", [128, 2], F32, kind="ExternalInput").ap()
    cb_bup = nc.dram_tensor("cb_bup", [128, 24], F32, kind="ExternalInput").ap()
    cb_bout = nc.dram_tensor("cb_bout", [128, D], F32, kind="ExternalInput").ap()
    cb_eps = nc.dram_tensor("cb_eps", [128, 1], F32, kind="ExternalInput").ap()
    y_out = nc.dram_tensor("y", [PER_CORE, D], F32, kind="ExternalOutput").ap()

    wq_s = nc.dram_tensor("wq_s", [8, 128, 1024], BF16, kind="Internal").ap()
    wdn_s = nc.dram_tensor("wdn_s", [8, 128, 256], BF16, kind="Internal").ap()
    wup_s = nc.dram_tensor("wup_s", [6, 128, 1024], BF16, kind="Internal").ap()
    wo_s = nc.dram_tensor("wo_s", [8, 128, 1024], BF16, kind="Internal").ap()
    wg_s = nc.dram_tensor("wg_s", [16, 128, 1536], BF16, kind="Internal").ap()
    wu_s = nc.dram_tensor("wu_s", [16, 128, 1536], BF16, kind="Internal").ap()
    wd_s = nc.dram_tensor("wd_s", [24, 128, 1024], BF16, kind="Internal").ap()

    bld = Builder()

    with ExitStack() as st:
        sb = lambda name, shape, dt: st.enter_context(nc.sbuf_tensor(name, shape, dt))
        c_identb = sb("c_identb", [128, 128], BF16)
        c_ident3 = sb("c_ident3", [128, 128], F32)
        c_credc = sb("c_credc", [128, 256], BF16)
        c_idstk = sb("c_idstk", [128, 64], BF16)
        c_selc = sb("c_selc", [128, 64, 128], BF16)
        c_bq = sb("c_bq", [128, 8], F32)
        c_bdn = sb("c_bdn", [128, 2], F32)
        c_bup = sb("c_bup", [128, 24], F32)
        c_bout = sb("c_bout", [128, D], F32)
        c_eps = sb("c_eps", [128, 1], F32)
        xres = sb("xres", [128, SUB, D], F32)
        ybuf = sb("ybuf", [128, SUB, D], F32)
        xmbuf = sb("xmbuf", [128, 4, D], F32)
        stt = sb("stt", [128, 32], F32)
        xln = sb("xln", [128, SUB, D], BF16)
        xlnT = sb("xlnT", [128, 8, NT], BF16)
        qfm = sb("qfm", [128, 8, NT], BF16)
        lat = sb("lat", [128, 2, NT], BF16)
        kdup = sb("kdup", [128, 16, NT], BF16)
        vcmp = sb("vcmp", [128, 8, NT], BF16)
        scev = sb("scev", [128, 2, NT], F32)
        smax = sb("smax", [128, SUB, 16], F32)
        etm = sb("etm", [128, SUB, 256], F32)
        atts = sb("atts", [128, 2048], BF16)
        ebc = sb("ebc", [128, 4, NT], BF16)
        pbf = sb("pbf", [128, 8, NT], BF16)
        aoT = sb("aoT", [128, 8, NT], BF16)
        hbuf = sb("hbuf", [128, 24, NT], BF16)
        wring = sb("wring", [128, RING, SLOTW], BF16)
        ps = st.enter_context(nc.psum_tensor("ps", [128, 8, 512], F32))

        sems = {}
        for s in Builder.SEMS:
            sems[s] = st.enter_context(nc.semaphore(s))
        blk = st.enter_context(nc.Block())

        psap = ps.ap()
        psbf = psap.rearrange("p b f -> p (b f)").bitcast(BF16).rearrange(
            "p (b f) -> p b f", b=8)

        hflat_b = hbuf.ap().rearrange("p a b -> p (a b)")
        cin_i32 = hflat_b.bitcast(I32)
        cin_f32 = hflat_b.bitcast(F32)
        aflat_b = aoT.ap().rearrange("p a b -> p (a b)")

        E = bld.emit

        for src, dst in [(cb_identb, c_identb), (cb_ident3, c_ident3),
                         (cb_credc, c_credc), (cb_idstk, c_idstk),
                         (cb_selc, c_selc), (cb_eps, c_eps),
                         (cb_bq, c_bq), (cb_bdn, c_bdn), (cb_bup, c_bup),
                         (cb_bout, c_bout)]:
            E("GP", lambda g, s=src, d=dst: g.dma_start(d.ap(), s),
              inc="X", amt=16)

        xin_ms = [None] * (TILES + 1)

        def emit_xin(t, extra_waits=()):
            dst = (xres if t % 2 == 0 else ybuf).ap()
            srcv = x_in[t * NT:(t + 1) * NT, :].rearrange(
                "(s p) d -> p s d", p=128)
            xin_ms[t] = E("GP", lambda g, s_=srcv, d=dst: g.dma_start(d, s_),
                          waits=extra_waits, inc="X", amt=16)

        emit_xin(0)

        conv = []
        for c in range(8):
            conv.append(("wq", c, wq_r[c * 128:(c + 1) * 128, :], wq_s[c], 1024, F32))
        for c in range(8):
            conv.append(("wdn", c, wdn_r[c * 128:(c + 1) * 128, :], wdn_s[c], 256, F32))
        for i in range(6):
            k2, mh = i // 3, i % 3
            conv.append(("wup", i,
                         wup_r[k2 * 128:(k2 + 1) * 128, mh * 1024:(mh + 1) * 1024],
                         wup_s[i], 1024, F32))
        for c in range(8):
            conv.append(("wo", c, wo_r[c * 128:(c + 1) * 128, :], wo_s[c], 1024, F32))
        for i in range(16):
            hh, c = i // 8, i % 8
            conv.append(("wg", i,
                         wg_r[c * 128:(c + 1) * 128, hh * 1536:(hh + 1) * 1536],
                         wg_s[i], 1536, I32))
        for i in range(16):
            hh, c = i // 8, i % 8
            conv.append(("wu", i,
                         wu_r[c * 128:(c + 1) * 128, hh * 1536:(hh + 1) * 1536],
                         wu_s[i], 1536, I32))
        for kk in range(24):
            conv.append(("wd", kk, wd_r[kk * 128:(kk + 1) * 128, :], wd_s[kk], 1024, I32))

        scratch_ready = {}
        cast_ms = [None] * 4
        outdma_ms = [None, None]
        for i, (tag, idx, src, dst, width, sdt) in enumerate(conv):
            lpar = i % 4
            opar = i % 2
            cin = (cin_i32 if sdt == I32 else cin_f32)
            cin_v = cin[:, lpar * 1536:lpar * 1536 + width]
            cout_v = aflat_b[:, opar * 1536:opar * 1536 + width]
            ld = E("SP", lambda g, s=src, d=cin_v: g.dma_start(d, s),
                   waits=[cast_ms[lpar]], inc="C%d" % lpar, amt=16)
            if i % 2 == 0:
                cast = E("DVE", lambda v, d=cout_v, s=cin_v: v.tensor_copy(d, s),
                         waits=[ld, outdma_ms[opar]], inc="V")
            else:
                cast = E("ACT", lambda a, d=cout_v, s=cin_v: a.copy(d, s),
                         waits=[ld, outdma_ms[opar]], inc="A")
            cast_ms[lpar] = cast
            od = E("GP", lambda g, s=cout_v, d=dst: g.dma_start(d, s),
                   waits=[cast], inc="O%d" % opar, amt=16)
            outdma_ms[opar] = od
            scratch_ready[(tag, idx)] = od
        conv_end2 = [("O0", bld.cnt["O0"]), ("O1", bld.cnt["O1"])]

        tile_chunks = ([("wq", c, 1024) for c in range(8)]
                       + [("wdn", c, 256) for c in range(8)]
                       + [("wup", i, 1024) for i in range(6)]
                       + [("wo", c, 1024) for c in range(8)]
                       + [("wg", i, 1536) for i in range(8)]
                       + [("wu", i, 1536) for i in range(8)]
                       + [("wd", kk, 1024) for kk in range(24)])
        NCH = len(tile_chunks)
        alt_aps = ([("kdup", kdup, k * 1536) for k in range(5)]
                   + [("vcmp", vcmp, k * 1536) for k in range(2)]
                   + [("qfm", qfm, k * 1536) for k in range(2)]
                   + [("pbf", pbf, k * 1536) for k in range(2)]
                   + [("etm", etm, 0), ("aoT", aoT, 0), ("aoT", aoT, 1536),
                      ("scev", scev, 0), ("atts", atts, 0)])
        assert len(alt_aps) == 16

        def alt_ap(k):
            nm, tens, off = alt_aps[k]
            fl = tens.ap().rearrange("p a b -> p (a b)") if tens.ap().ndim == 3                 else tens.ap()
            if fl.dtype != BF16:
                fl = fl.bitcast(BF16)
            return fl[:, off:off + 1536]

        def alt_w(k, t):
            return ("AW%d" % k, 16 * (t + 1))
        scr = {"wq": wq_s, "wdn": wdn_s, "wup": wup_s, "wo": wo_s,
               "wg": wg_s, "wu": wu_s, "wd": wd_s}

        chunk_free_val = []

        def w_ms(gi):
            return ("W%d" % (gi % RING), 16 * (gi // RING + 1))

        def chunk_ap(gi, width):
            return wring.ap()[:, gi % RING, 0:width]

        STOP_CHUNKS = {"ln1": 0, "q": 8, "lat": 16, "kv": 22, "scores": 22,
                       "soft": 22, "attv": 22, "wout": 30, "ln2": 30,
                       "scores_noevac": 22, "gate": 38, "up": 46, "dn": 70}

        def _ring_load(gi):
            tag, idx, width = tile_chunks[gi % NCH]
            slot = gi % RING
            waits = [scratch_ready[(tag, idx)]]
            if gi >= RING:
                waits.append(("F", chunk_free_val[gi - RING]))
            E("SP", lambda e, d=wring.ap()[:, slot, 0:width],
              s=scr[tag][idx]: e.dma_start(d, s),
              waits=waits, inc="W%d" % slot, amt=16)

        def emit_tile_loads(t, alt_free_ms=None):
            base = t * NCH
            upto = STOP_CHUNKS.get(stop_after, NCH) if stop_after else NCH
            for li in range(upto):
                _ring_load(base + li)

        def emit_alt_loads(ks, free_ms):
            for k in ks:
                tag = "wg" if k < 8 else "wu"
                idx = 8 + (k % 8)
                E("GP", lambda e, d=alt_ap(k), s=scr[tag][idx]:
                  e.dma_start(d, s),
                  waits=[scratch_ready[(tag, idx)], free_ms],
                  inc="AW%d" % k, amt=16)

        bank_w = [None] * 8
        bank_r = [None] * 8

        def ln_phase(specs, fin_waits=()):
            n = len(specs)
            rsm = [None] * n; mn = [None] * n; sub = [None] * n
            sq = [None] * n; var = [None] * n; sd = [None] * n; rs = [None] * n
            fins = [None] * n
            for i, (s, src_ap, ok_, sm, sp, res_ap) in enumerate(specs):
                so = s * 8
                if sp is None:
                    rsm[i] = E("DVE", lambda v, o=stt.ap()[:, so:so + 1],
                               i_=src_ap:
                               v.reduce_sum(o, i_, AX.X), waits=list(sm),
                               inc="V")
                else:
                    rsm[i] = sp
            for i, (s, src_ap, ok_, sm, sp, res_ap) in enumerate(specs):
                so = s * 8
                sub[i] = E("DVE", lambda v, o=xmbuf.ap()[:, s, :], i_=src_ap,
                           sb2=stt.ap()[:, so:so + 1].broadcast_to([128, D]):
                           v.scalar_tensor_tensor(o, sb2, -1.0 / D, i_,
                                                  ALU.mult, ALU.add),
                           waits=[rsm[i]] + list(sm), inc="V")
            for i, (s, src_ap, ok_, sm, sp, res_ap) in enumerate(specs):
                so = s * 8
                sq[i] = E("ACT", lambda a,
                          o=ebc.ap().rearrange("p a b -> p (a b)")[:, 0:D],
                          i_=xmbuf.ap()[:, s, :],
                          ac=stt.ap()[:, so + 2:so + 3]:
                          a.activation(o, i_, AF.Square, accum_out=ac),
                          waits=[sub[i]], inc="A")
            for i, (s, src_ap, ok_, sm, sp, res_ap) in enumerate(specs):
                so = s * 8
                sd[i] = E("ACT", lambda a, o=stt.ap()[:, so + 4:so + 5],
                          i_=stt.ap()[:, so + 2:so + 3]:
                          a.activation(o, i_, AF.Ln, scale=1.0 / D,
                                       bias=c_eps.ap()),
                          waits=[sq[i]], inc="A")
            for i, (s, src_ap, ok_, sm, sp, res_ap) in enumerate(specs):
                so = s * 8
                rs[i] = E("ACT", lambda a, o=stt.ap()[:, so + 5:so + 6],
                          i_=stt.ap()[:, so + 4:so + 5]:
                          a.activation(o, i_, AF.Exp, scale=-0.5),
                          waits=[sd[i]], inc="A")
            for i, (s, src_ap, ok_, sm, sp, res_ap) in enumerate(specs):
                so = s * 8
                if ok_ == "xln":
                    fins[i] = E("DVE", lambda v, o=xln.ap()[:, s, :],
                                i_=xmbuf.ap()[:, s, :],
                                sc=stt.ap()[:, so + 5:so + 6]:
                                v.tensor_scalar_mul(o, i_, sc),
                                waits=[rs[i]] + list(fin_waits), inc="V")
                elif ok_ == "resid":
                    fins[i] = E("DVE", lambda v, o=res_ap, i_=xmbuf.ap()[:, s, :],
                                i1=res_ap, sc=stt.ap()[:, so + 5:so + 6]:
                                v.scalar_tensor_tensor(o, i_, sc, i1,
                                                       ALU.mult, ALU.add),
                                waits=[rs[i]], inc="V")
                else:
                    fins[i] = E("DVE", lambda v, o=res_ap, i_=xmbuf.ap()[:, s, :],
                                i1=res_ap, sc=stt.ap()[:, so + 5:so + 6],
                                ac=stt.ap()[:, so:so + 1]:
                                v.scalar_tensor_tensor(o, i_, sc, i1,
                                                       ALU.mult, ALU.add,
                                                       accum_out=ac),
                                waits=[rs[i]], inc="V")
            return fins, sub

        def emit_transposes(ln_ms):
            evs = []
            tms = []
            for s in range(SUB):
                w = [ln_ms[s], bank_r[s]]
                tm = None
                for c in range(8):
                    tm = E("PE", lambda e, o=psbf[:, s, c * 128:(c + 1) * 128],
                           i=xln.ap()[:, s, c * 128:(c + 1) * 128]:
                           e.transpose(o, i, c_identb.ap()),
                           waits=(w if c == 0 else []),
                           inc=("P" if c == 7 else None))
                bank_w[s] = tm
                ev = E("DVE", lambda v,
                       o=xlnT.ap()[:, :, s * 128:(s + 1) * 128],
                       i=psbf[:, s, :].rearrange("p (c q) -> p c q", c=8):
                       v.tensor_copy(o, i), waits=[tm], inc="V")
                bank_r[s] = ev
                evs.append(ev)
                tms.append(tm)
            return evs[-1], tms[-1]

        prev_out_ms = [None]
        for t in range(n_tiles):
            first_tile = (t == 0)
            base = t * NCH
            res_b, scr_b = (xres, ybuf) if t % 2 == 0 else (ybuf, xres)
            if t + 1 < n_tiles:
                emit_xin(t + 1, extra_waits=[prev_out_ms[0]])

            def gq(c): return base + c
            def gdn(c): return base + 8 + c
            def gup(i): return base + 16 + i
            def gwo(c): return base + 22 + c
            def gwg(i): return base + 30 + i
            def gwu(i): return base + 38 + i
            def gwd(kk): return base + 46 + kk

            if t == 0:
                ln1, _ = ln_phase([(s, res_b.ap()[:, s, :], "xln",
                                    [xin_ms[0]], None, None)
                                   for s in range(SUB)])
                xlnT_ready, _ = emit_transposes(ln1)
            else:
                xlnT_ready = pending_xlnT
            if taps and t == 0:
                d1 = nc.dram_tensor("d_xlnT1", [128, 8, NT], BF16,
                                    kind="ExternalOutput").ap()
                E("GP", lambda g, d=d1: g.dma_start(d, xlnT.ap()),
                  waits=[xlnT_ready], inc="X", amt=16)
                d2 = nc.dram_tensor("d_xln1", [128, SUB, D], BF16,
                                    kind="ExternalOutput").ap()
                E("GP", lambda g, d=d2: g.dma_start(d, xln.ap()),
                  waits=[ln1[3]], inc="X", amt=16)
            if stop_after == "ln1":
                for nm2, buf2 in [("d_xres", xres), ("d_stt", stt),
                                  ("d_xm", xmbuf)]:
                    shp2 = list(buf2.ap().shape)
                    dr2 = nc.dram_tensor(nm2, shp2, buf2.ap().dtype,
                                         kind="ExternalOutput").ap()
                    E("GP", lambda g, d=dr2, s=buf2: g.dma_start(d, s.ap()),
                      waits=[("P", bld.cnt["P"]), ("A", bld.cnt["A"]),
                             ("V", bld.cnt["V"])], inc="X", amt=16)
                emit_tile_loads(t, None); continue

            qfm_ev = []
            for m in range(8):
                b = m % 4
                w = [xlnT_ready, bank_r[b]]
                mm = None
                for k in range(8):
                    mm = E("PE", lambda e, o=psap[:, b, :],
                           l=chunk_ap(gq(k), 1024), m_=m, k_=k:
                           e.matmul(o, l[:, m_ * 128:(m_ + 1) * 128],
                                    xlnT.ap()[:, k_, :],
                                    start=(k_ == 0), stop=(k_ == 7)),
                           waits=(w + [w_ms(gq(k))] if k == 0
                                  else [w_ms(gq(k))]),
                           inc=(("F" if m == 7 else "P") if k == 7 else None))
                bank_w[b] = mm
                ev = E("ACT", lambda a, o=qfm.ap()[:, m, :], i=psap[:, b, :],
                       bi=c_bq.ap()[:, m:m + 1]:
                       a.activation(o, i, AF.Identity, bias=bi), waits=[mm], inc="A")
                bank_r[b] = ev
                qfm_ev.append(ev)
            chunk_free_val.extend([bld.cnt["F"]] * 8)
            if stop_after == "q":
                emit_tile_loads(t, None); continue

            lat_ev = []
            for m in range(2):
                b = 4 + m
                w = [xlnT_ready, bank_r[b]]
                mm = None
                for k in range(8):
                    mm = E("PE", lambda e, o=psap[:, b, :],
                           l=chunk_ap(gdn(k), 256), m_=m, k_=k:
                           e.matmul(o, l[:, m_ * 128:(m_ + 1) * 128],
                                    xlnT.ap()[:, k_, :],
                                    start=(k_ == 0), stop=(k_ == 7)),
                           waits=(w + [w_ms(gdn(k))] if k == 0
                                  else [w_ms(gdn(k))]),
                           inc=(("F" if m == 1 else "P") if k == 7 else None))
                bank_w[b] = mm
                ev = E("ACT", lambda a, o=lat.ap()[:, m, :], i=psap[:, b, :],
                       bi=c_bdn.ap()[:, m:m + 1]:
                       a.activation(o, i, AF.Identity, bias=bi), waits=[mm], inc="A")
                bank_r[b] = ev
                lat_ev.append(ev)
            chunk_free_val.extend([bld.cnt["F"]] * 8)
            if stop_after == "lat":
                emit_tile_loads(t, None); continue

            kv_ev = [None] * 24
            for g in range(24):
                b = g % 4
                w = [lat_ev[-1], bank_r[b]]
                mm = None
                for k2 in range(2):
                    mh = g // 8
                    mloc = g % 8
                    ci = gup(k2 * 3 + mh)
                    mm = E("PE", lambda e, o=psap[:, b, :],
                           l=chunk_ap(ci, 1024), ml=mloc, k_=k2:
                           e.matmul(o, l[:, ml * 128:(ml + 1) * 128],
                                    lat.ap()[:, k_, :],
                                    start=(k_ == 0), stop=(k_ == 1)),
                           waits=(w + [w_ms(ci)] if k2 == 0 else [w_ms(ci)]),
                           inc=(("F" if g == 23 else "P") if k2 == 1 else None))
                bank_w[b] = mm
                dst = kdup.ap()[:, g, :] if g < 16 else vcmp.ap()[:, g - 16, :]
                ev = E("ACT", lambda a, o=dst, i=psap[:, b, :],
                       bi=c_bup.ap()[:, g:g + 1]:
                       a.activation(o, i, AF.Identity, bias=bi), waits=[mm], inc="A")
                bank_r[b] = ev
                kv_ev[g] = ev
            chunk_free_val.extend([bld.cnt["F"]] * 6)
            if stop_after == "kv":
                emit_tile_loads(t, None); continue

            sred_ms = [None] * 128
            idx = 0
            pair_i = 0
            for c in range(8):
                for gp in range(8):
                    g = 2 * gp
                    w = [qfm_ev[c], kv_ev[g + 1]]
                    if idx >= 8:
                        w.append(sred_ms[idx - 7])
                    peng = "DVE"
                    sl = idx % 8
                    pm = E(peng, lambda v, o=pbf.ap()[:, sl:sl + 2, :],
                           a=qfm.ap()[:, c:c + 1, :].broadcast_to([128, 2, NT]),
                           b2=kdup.ap()[:, g:g + 2, :]:
                           v.tensor_tensor(o, a, b2, ALU.mult),
                           waits=w, inc=("V" if peng == "DVE" else "G"))
                    pair_i += 1
                    for q2 in range(2):
                        bank = 6 + c // 4
                        r1 = ((2 * c) % 8) * 16 + g + q2
                        first = (c % 4 == 0 and g + q2 == 0)
                        last = (c % 4 == 3 and g + q2 == 15)
                        w2 = [pm]
                        if first:
                            w2.append(bank_r[bank])
                        mm = E("PE", lambda e, o=psap[:, bank, :],
                               l=c_credc.ap()[:, 127 - r1:255 - r1],
                               pi=idx % 8, f=first, la=last:
                               e.matmul(o, l, pbf.ap()[:, pi, :],
                                        start=f, stop=la),
                               waits=(w2 if q2 == 0 else []), inc="P")
                        sred_ms[idx] = mm
                        if last:
                            bank_w[bank] = mm
                        idx += 1

            if stop_after == "scores_noevac":
                emit_tile_loads(t, None); continue
            sev = []
            for c2 in range(2):
                ev = E("ACT", lambda a, o=scev.ap()[:, c2, :],
                       i=psap[:, 6 + c2, :]: a.copy(o, i),
                       waits=[bank_w[6 + c2]], inc="A")
                bank_r[6 + c2] = ev
                sev.append(ev)
            strn_last = None
            for s in range(SUB):
                b = 4 + s // 2
                w = [sev[-1], bank_r[4], bank_r[5]]
                for c2 in range(2):
                    strn_last = E("PE", lambda e,
                                  o=psap[:, b, (s % 2) * 256 + c2 * 128:
                                         (s % 2) * 256 + (c2 + 1) * 128],
                                  i=scev.ap()[:, c2, s * 128:(s + 1) * 128]:
                                  e.transpose(o, i, c_ident3.ap()),
                                  waits=(w if (s == 0 and c2 == 0) else []),
                                  inc=("P" if (s == 3 and c2 == 1) else None))
            bank_w[4] = strn_last
            bank_w[5] = strn_last

            tmv_all = psap[:, 4:6, :].rearrange("p b (x h g) -> p (b x h) g",
                                                x=2, h=16)
            smax_f = smax.ap().rearrange("p s h -> p (s h)")
            etm_f = etm.ap().rearrange("p s q -> p (s q)")
            mx = E("DVE", lambda v, o=smax_f, i=tmv_all:
                   v.reduce_max(o, i, AX.X),
                   waits=[strn_last], inc="V")
            sb_ = E("DVE", lambda v,
                    o=etm_f.rearrange("p (a g) -> p a g", g=16),
                    i=tmv_all,
                    m=smax_f.unsqueeze(2).broadcast_to([128, 64, 16]):
                    v.tensor_tensor(o, i, m, ALU.subtract),
                    waits=[mx], inc="V")
            ex = E("ACT", lambda a, o=etm_f, i=etm_f:
                   a.activation(o, i, AF.Exp, scale=0.125),
                   waits=[sb_], inc="A")
            zs = E("DVE", lambda v, o=smax_f,
                   i=etm_f.rearrange("p (a g) -> p a g", g=16):
                   v.reduce_sum(o, i, AX.X), waits=[ex], inc="V")
            rc = E("DVE", lambda v, o=smax_f, i=smax_f: v.reciprocal(o, i),
                   waits=[zs], inc="V")
            nm = E("DVE", lambda v,
                   o=atts.ap()[:, 1024:2048].rearrange("p (a g) -> p a g", g=16),
                   i=etm_f.rearrange("p (a g) -> p a g", g=16),
                   m=smax_f.unsqueeze(2).broadcast_to([128, 64, 16]):
                   v.tensor_tensor(o, i, m, ALU.mult),
                   waits=[rc], inc="V")
            soft = [nm] * SUB
            bank_r[4] = nm
            bank_r[5] = nm

            atrn = None
            w0 = soft + [bank_r[6]]
            for c2 in range(2):
                for s in range(SUB):
                    atrn = E("PE", lambda e,
                             o=psbf[:, 6, c2 * 512 + s * 128:c2 * 512 + (s + 1) * 128],
                             i=atts.ap()[:, 1024 + s * 256 + c2 * 128:1024 + s * 256 + (c2 + 1) * 128]:
                             e.transpose(o, i, c_identb.ap()),
                             waits=(w0 if (c2 == 0 and s == 0) else []),
                             inc=("P" if (c2 == 1 and s == 3) else None))
            bank_w[6] = atrn
            attT_ev = []
            for c2 in range(2):
                ev = E("DVE", lambda v, o=atts.ap()[:, c2 * 512:(c2 + 1) * 512],
                       i=psbf[:, 6, c2 * 512:(c2 + 1) * 512]:
                       v.tensor_copy(o, i), waits=[atrn], inc="V")
                attT_ev.append(ev)
            bank_r[6] = attT_ev[-1]

            idred_ms = [None] * 128
            bc_ms = [None] * 8
            pp_ms = [None] * 4
            ao_ev = [None] * 8
            pidx0 = 0

            def emit_bcast(h, j):
                eb = j % 4
                si = (h % 8) * 8 + j
                w = [attT_ev[h // 8], bank_r[eb]]
                bc_ms[j] = E("PE", lambda e, o=psap[:, eb, :],
                             l=c_selc.ap()[:, si, :], ti=h // 8:
                             e.matmul(o, l, atts.ap()[:, ti * 512:(ti + 1) * 512],
                                      start=True, stop=True),
                             waits=w, inc="P")
                bank_w[eb] = bc_ms[j]

            def emit_prodpair(h, jp, pidx):
                b0 = (2 * jp) % 4
                w2 = [bc_ms[2 * jp + 1], kv_ev[16 + 2 * jp + 1]]
                if pidx >= 8:
                    w2.append(idred_ms[pidx - 7])
                else:
                    w2.append(sred_ms[120 + pidx + 1])
                sl = pidx % 8
                pp_ms[jp] = E("DVE", lambda v, o=pbf.ap()[:, sl:sl + 2, :],
                              a=psap[:, b0:b0 + 2, :],
                              b2=vcmp.ap()[:, 2 * jp:2 * jp + 2, :]:
                              v.tensor_tensor(o, a, b2, ALU.mult),
                              waits=w2, inc="V")
                bank_r[b0] = pp_ms[jp]
                bank_r[b0 + 1] = pp_ms[jp]

            def emit_idred(h, j, pidx, ob):
                w3 = [pp_ms[j // 2]]
                if j == 0:
                    w3.append(bank_r[ob])
                mm = E("PE", lambda e,
                       o=psap[(h % 2) * 64:(h % 2) * 64 + 64, ob, :],
                       pi=pidx % 8, f=(j == 0), la=(j == 7):
                       e.matmul(o, c_idstk.ap(), pbf.ap()[:, pi, :],
                                start=f, stop=la),
                       waits=w3, inc="P")
                idred_ms[pidx] = mm
                return mm

            st_bc = [[None] * 8, [None] * 8]
            st_pp = [[None] * 4, [None] * 4]

            def unitA(h, jp, pidx0):
                par = h % 2
                bc_ms[:] = st_bc[par]
                pp_ms[:] = st_pp[par]
                emit_bcast(h, 2 * jp)
                emit_bcast(h, 2 * jp + 1)
                emit_prodpair(h, jp, pidx0 + 2 * jp)
                st_bc[par][:] = bc_ms
                st_pp[par][:] = pp_ms

            def unitB(h, jp, pidx0, ob):
                par = h % 2
                pp_ms[:] = st_pp[par]
                last = emit_idred(h, 2 * jp, pidx0 + 2 * jp, ob)
                last = emit_idred(h, 2 * jp + 1, pidx0 + 2 * jp + 1, ob)
                return last

            for jp in range(4):
                unitA(0, jp, 0)
            for h in range(H):
                pair = h // 2
                ob = 7 - (pair % 2)
                last = None
                for jp in range(4):
                    last = unitB(h, jp, 8 * h, ob)
                    if h + 1 < H:
                        unitA(h + 1, jp, 8 * (h + 1))
                if h % 2 == 1:
                    bank_w[ob] = last
                    w4 = [bank_w[ob]]
                    if first_tile and pair == 0:
                        w4.extend(conv_end2)
                    ev = E("ACT", lambda a, o=aoT.ap()[:, pair, :],
                           i=psap[:, ob, :]: a.copy(o, i), waits=w4, inc="A")
                    bank_r[ob] = ev
                    ao_ev[pair] = ev

            attv_last = idred_ms[127]
            if stop_after is None:
                emit_alt_loads([k for k in range(16) if k not in (12, 13)],
                               attv_last)
            if stop_after == "attv":
                emit_tile_loads(t, None); continue
            wout_ev = [None] * SUB
            wout_mm = [None] * SUB
            attn_fin = [None] * SUB
            ln2 = [None] * SUB
            for pr in (0, 1):
                ss = (2 * pr, 2 * pr + 1)
                for s in ss:
                    sb0 = 4 + (s % 2) * 2
                    w = [bank_r[sb0], bank_r[sb0 + 1]] + ao_ev
                    mm = None
                    for c in range(8):
                        for half in range(2):
                            wk = [w_ms(gwo(c))] if half == 0 else []
                            mm = E("PE", lambda e, o=psap[:, sb0 + half, :],
                                   l=aoT.ap()[:, c, s * 128:(s + 1) * 128],
                                   wch=chunk_ap(gwo(c), 1024), hf=half, c_=c:
                                   e.matmul(o, l,
                                            wch[:, hf * 512:(hf + 1) * 512],
                                            start=(c_ == 0), stop=(c_ == 7)),
                                   waits=(w + wk if (c == 0 and half == 0)
                                          else wk),
                                   inc=(("F" if s == 3 else "P")
                                        if (c == 7 and half == 1) else None))
                    bank_w[sb0] = mm
                    bank_w[sb0 + 1] = mm
                    wout_mm[s] = mm
                for s in ss:
                    sb0 = 4 + (s % 2) * 2
                    ev = E("DVE", lambda v, o=xmbuf.ap()[:, s, :],
                           i=psap[:, sb0:sb0 + 2, :].rearrange(
                               "p b f -> p (b f)"),
                           ac=stt.ap()[:, s * 8:s * 8 + 1]:
                           v.scalar_tensor_tensor(o, i, 1.0, c_bout.ap(),
                                                  ALU.mult, ALU.add,
                                                  accum_out=ac),
                           waits=[wout_mm[s]], inc="V")
                    bank_r[sb0] = ev
                    bank_r[sb0 + 1] = ev
                    wout_ev[s] = ev
                af, _ = ln_phase(
                    [(s, xmbuf.ap()[:, s, :], "resid_accum", [wout_ev[s]],
                      wout_ev[s], res_b.ap()[:, s, :]) for s in ss])
                attn_fin[2 * pr:2 * pr + 2] = af
                l2, _ = ln_phase([(s, res_b.ap()[:, s, :], "xln",
                                   [attn_fin[s]], attn_fin[s], None)
                                  for s in ss])
                ln2[2 * pr:2 * pr + 2] = l2
            chunk_free_val.extend([bld.cnt["F"]] * 8)
            wout_last = wout_mm[3]
            if stop_after is None:
                emit_alt_loads([12, 13], wout_last)
            if stop_after == "wout":
                emit_tile_loads(t, None); continue

            xlnT_ready, t2_tm = emit_transposes(ln2)
            ln1_next = [None] * SUB
            if t + 1 < n_tiles:
                ln1_next, _ = ln_phase(
                    [(s, scr_b.ap()[:, s, :], "xln", [xin_ms[t + 1]],
                      None, None) for s in range(SUB)],
                    fin_waits=[("P", t2_tm[1])])
                pending_ln1 = ln1_next

            silu_ms = [None] * 24
            for hh in range(2):
                for mmi in range(12):
                    m = hh * 12 + mmi
                    b = 2 * (m % 2)
                    w = [xlnT_ready, bank_r[b]]
                    mm = None
                    for k in range(8):
                        if hh == 0:
                            lap = chunk_ap(gwg(k), 1536)
                            wk = [w_ms(gwg(k))]
                        else:
                            lap = alt_ap(k)
                            wk = [alt_w(k, t)]
                        mm = E("PE", lambda e, o=psap[:, b, :],
                               l=lap, ml=mmi, k_=k:
                               e.matmul(o, l[:, ml * 128:(ml + 1) * 128],
                                        xlnT.ap()[:, k_, :],
                                        start=(k_ == 0), stop=(k_ == 7)),
                               waits=(w + wk if k == 0 else wk),
                               inc=(("F" if (mmi == 11 and hh == 0) else "P")
                                    if k == 7 else None))
                    bank_w[b] = mm
                    w2 = [mm]
                    if first_tile and m == 0:
                        w2.extend(conv_end2)
                    ev = E("ACT", lambda a, o=hbuf.ap()[:, m, :], i=psap[:, b, :]:
                           a.activation(o, i, AF.Silu, scale=QSCALE),
                           waits=w2, inc="A")
                    bank_r[b] = ev
                    silu_ms[m] = ev
                if hh == 0:
                    chunk_free_val.extend([bld.cnt["F"]] * 8)

            stt_ms = [None] * 24
            for hh in range(2):
                for mmi in range(12):
                    m = hh * 12 + mmi
                    b = 2 * (m % 2) + 1
                    w = [xlnT_ready, bank_r[b]]
                    mm = None
                    for k in range(8):
                        if hh == 0:
                            lap = chunk_ap(gwu(k), 1536)
                            wk = [w_ms(gwu(k))]
                        else:
                            lap = alt_ap(8 + k)
                            wk = [alt_w(8 + k, t)]
                        mm = E("PE", lambda e, o=psap[:, b, :],
                               l=lap, ml=mmi, k_=k:
                               e.matmul(o, l[:, ml * 128:(ml + 1) * 128],
                                        xlnT.ap()[:, k_, :],
                                        start=(k_ == 0), stop=(k_ == 7)),
                               waits=(w + wk if k == 0 else wk),
                               inc=(("F" if (mmi == 11 and hh == 0) else "P")
                                    if k == 7 else None))
                    bank_w[b] = mm
                    ev = E("DVE", lambda v, o=hbuf.ap()[:, m, :], i=psap[:, b, :],
                           i1=hbuf.ap()[:, m, :]:
                           v.scalar_tensor_tensor(o, i, QSCALE, i1,
                                                  ALU.mult, ALU.mult),
                           waits=[mm, silu_ms[m]], inc="V")
                    bank_r[b] = ev
                    stt_ms[m] = ev
                if hh == 0:
                    chunk_free_val.extend([bld.cnt["F"]] * 8)

            for kk in range(24):
                for s in range(SUB):
                    for half in range(2):
                        b = s * 2 + half
                        w = [w_ms(gwd(kk)), stt_ms[kk]]
                        if kk == 0:
                            w.append(bank_r[b])
                        mm = E("PE", lambda e, o=psap[:, b, :],
                               l=hbuf.ap()[:, kk, s * 128:(s + 1) * 128],
                               wch=chunk_ap(gwd(kk), 1024), hf=half, k_=kk:
                               e.matmul(o, l, wch[:, hf * 512:(hf + 1) * 512],
                                        start=(k_ == 0), stop=(k_ == 23)),
                               waits=w,
                               inc=(("F" if (s == 3 and half == 1) else
                                     ("P" if kk == 23 else None))))
                        if kk == 23:
                            bank_w[b] = mm
                chunk_free_val.append(bld.cnt["F"])

            dn_ev = [None] * SUB
            for s in range(SUB):
                ev = E("DVE", lambda v, o=xmbuf.ap()[:, s, :],
                       i=psap[:, s * 2:s * 2 + 2, :].rearrange("p b f -> p (b f)"),
                       ac=stt.ap()[:, s * 8:s * 8 + 1]:
                       v.tensor_scalar(o, i, 1.0, 0.0, ALU.mult, ALU.add,
                                       accum_out=ac),
                       waits=[bank_w[s * 2], bank_w[s * 2 + 1],
                              ln1_next[s]], inc="V")
                bank_r[s * 2] = ev
                bank_r[s * 2 + 1] = ev
                dn_ev[s] = ev
            if t + 1 < n_tiles:
                pending_xlnT, _ = emit_transposes(ln1_next)
            dn_fin, _ = ln_phase(
                [(s, xmbuf.ap()[:, s, :], "resid", [dn_ev[s]],
                  dn_ev[s], res_b.ap()[:, s, :]) for s in range(SUB)])
            dst = y_out[t * NT:(t + 1) * NT, :].rearrange("(s p) d -> p s d", p=128)
            out_ms = E("GP", lambda g, d=dst, sb_=res_b: g.dma_start(d, sb_.ap()),
                       waits=dn_fin, inc="X", amt=16)
            prev_out_ms[0] = out_ms

            emit_tile_loads(t)

        if taps:
            tap_list = [
                ("d_xlnT", xlnT), ("d_qfm", qfm), ("d_lat", lat),
                ("d_kdup", kdup), ("d_vcmp", vcmp), ("d_scev", scev),
                ("d_atts", atts), ("d_aoT", aoT),
                ("d_hbuf", hbuf), ("d_ybuf", ybuf), ("d_etm", etm),
            ]
            for nm, dr_s, shp in [("d_wqs", wq_s, [8, 128, 1024]),
                                  ("d_wds", wd_s, [24, 128, 1024])]:
                drt = nc.dram_tensor(nm, shp, BF16, kind="ExternalOutput").ap()
                E("GP", lambda g, d=drt, s=dr_s: g.dma_start(d, s),
                  waits=[("X", bld.cnt["X"])], inc="X", amt=16)
            allw = [("P", bld.cnt["P"]), ("A", bld.cnt["A"]),
                    ("V", bld.cnt["V"]), ("X", bld.cnt["X"])]
            for nm, buf in tap_list:
                shp = list(buf.ap().shape)
                dt = buf.ap().dtype
                dr = nc.dram_tensor(nm, shp, dt, kind="ExternalOutput").ap()
                E("GP", lambda g, d=dr, s=buf: g.dma_start(d, s.ap()),
                  waits=allw, inc="X", amt=16)

        E("GP", lambda g: g.engine_nop(), waits=[("X", bld.cnt["X"])])

        def render(eng_name, deco):
            @deco
            def _(e):
                waited = {}
                for fn, waits, inc, amt, extra in bld.ops[eng_name]:
                    for (snm, val) in waits:
                        if waited.get(snm, -1) < val:
                            e.wait_ge(sems[snm], val)
                            waited[snm] = val
                    inst = fn(e)
                    if inc is not None:
                        inst = inst.then_inc(sems[inc], amt)
                    if extra is not None:
                        inst.then_inc(sems[extra[0]], extra[1])

        render("SP", blk.sync)
        render("GP", blk.gpsimd)
        render("PE", blk.tensor)
        render("ACT", blk.scalar)
        render("DVE", blk.vector)

    nc.finalize()
    return nc


_NC_CACHE = None


def _get_nc():
    global _NC_CACHE
    if _NC_CACHE is None:
        _NC_CACHE = build_program()
    return _NC_CACHE


def kernel(L0_attn_x, L0_attn_Wq, L0_attn_bq, L0_attn_Wdown, L0_attn_bdown,
           L0_attn_Wup, L0_attn_bup, L0_attn_Wout, L0_attn_bout,
           L0_ffn_w_gate, L0_ffn_w_up, L0_ffn_w_down):
    x = np.ascontiguousarray(np.asarray(L0_attn_x, dtype=np.float32).reshape(NTOK, D))
    wq = np.ascontiguousarray(np.asarray(L0_attn_Wq, dtype=np.float32))
    bq = np.asarray(L0_attn_bq, dtype=np.float32)
    wdn = np.ascontiguousarray(np.asarray(L0_attn_Wdown, dtype=np.float32))
    bdn = np.asarray(L0_attn_bdown, dtype=np.float32)
    wup = np.asarray(L0_attn_Wup, dtype=np.float32)
    bup = np.asarray(L0_attn_bup, dtype=np.float32)
    wo = np.ascontiguousarray(np.asarray(L0_attn_Wout, dtype=np.float32))
    bout = np.asarray(L0_attn_bout, dtype=np.float32)
    wg = np.asarray(L0_ffn_w_gate, dtype=np.int32)
    wu = np.asarray(L0_ffn_w_up, dtype=np.int32)
    wd = np.asarray(L0_ffn_w_down, dtype=np.int32)

    wup4 = wup.reshape(DL, H, 2, DH)
    kcols = wup4[:, :, 0, :]
    vcols = wup4[:, :, 1, :]
    wup_p = np.ascontiguousarray(np.concatenate(
        [np.repeat(kcols, 2, axis=1).reshape(DL, 2048),
         vcols.reshape(DL, 1024)], axis=1))
    bup4 = bup.reshape(H, 2, DH)
    bup_p = np.concatenate([np.repeat(bup4[:, 0, :], 2, axis=0).reshape(2048),
                            bup4[:, 1, :].reshape(1024)])
    bq_fm = np.ascontiguousarray(bq.reshape(8, 128).T)
    bdn_fm = np.ascontiguousarray(bdn.reshape(2, 128).T)
    bup_fm = np.ascontiguousarray(bup_p.reshape(24, 128).T)
    bout_bc = np.ascontiguousarray(np.broadcast_to(bout, (128, D)))

    wgT = np.ascontiguousarray(wg.T)
    wuT = np.ascontiguousarray(wu.T)
    wdT = np.ascontiguousarray(wd.T)

    identb = np.eye(128, dtype=bfdt)
    ident3 = np.eye(128, dtype=np.float32)
    credc = np.zeros((128, 256), dtype=bfdt)
    credc[0:64, 127] = 1.0
    credc[64:128, 143] = 1.0
    idstk = np.zeros((128, 64), dtype=bfdt)
    idstk[np.arange(128), np.arange(128) % 64] = 1.0
    selc = np.zeros((128, 64, 128), dtype=bfdt)
    for si in range(64):
        selc[2 * si, si, 0:64] = 1.0
        selc[2 * si + 1, si, 64:128] = 1.0

    shards = np.split(x, NCORES, axis=0)
    common = {
        "wq_r": wq, "wdn_r": wdn, "wup_r": wup_p, "wo_r": wo,
        "wg_r": wgT, "wu_r": wuT, "wd_r": wdT,
        "cb_identb": identb, "cb_ident3": ident3, "cb_credc": credc,
        "cb_idstk": idstk, "cb_selc": selc, "cb_bq": bq_fm, "cb_bdn": bdn_fm,
        "cb_bup": bup_fm, "cb_bout": bout_bc,
        "cb_eps": np.full((128, 1), EPS, np.float32),
    }
    in_maps = [dict(common, x=np.ascontiguousarray(s)) for s in shards]
    nc = _get_nc()
    res = run_bass_kernel_spmd(nc, in_maps, core_ids=list(range(NCORES)))
    out = np.concatenate([r["y"] for r in res.results], axis=0)
    return out.reshape(4, 8192, D)


# version 20
# speedup vs baseline: 1.1517x; 1.1517x over previous
import numpy as np
import ml_dtypes
from contextlib import ExitStack

import concourse.bass as bass
import concourse.mybir as mybir
from concourse.bass_utils import run_bass_kernel_spmd

F32 = mybir.dt.float32
BF16 = mybir.dt.bfloat16
I32 = mybir.dt.int32
AF = mybir.ActivationFunctionType
ALU = mybir.AluOpType
AX = mybir.AxisListType
bfdt = ml_dtypes.bfloat16

NCORES = 8
D = 1024
H = 16
DH = 64
DL = 256
HID = 3072
NTOK = 4 * 8192
PER_CORE = NTOK // NCORES
NT = 512
SUB = NT // 128
TILES = PER_CORE // NT
EPS = 1e-5
QSCALE = 1.0 / 127.0
RING = 10
SLOTW = 1536

ENGINES = ("SP", "GP", "PE", "ACT", "DVE")


class Builder:

    SEMS = (["X", "P", "A", "V", "F", "G", "C0", "C1", "C2", "C3",
             "O0", "O1"]
            + ["W%d" % i for i in range(RING)]
            + ["AW%d" % i for i in range(16)])

    def __init__(self):
        self.ops = {e: [] for e in ENGINES}
        self.cnt = {s: 0 for s in self.SEMS}

    def emit(self, eng, fn, waits=(), inc=None, amt=1, extra_inc=None):
        ws = [w for w in waits if w is not None]
        self.ops[eng].append((fn, ws, inc, amt, extra_inc))
        ms = None
        if inc is not None:
            self.cnt[inc] += amt
            ms = (inc, self.cnt[inc])
        if extra_inc is not None:
            s2, a2 = extra_inc
            self.cnt[s2] += a2
        return ms


def build_program(n_tiles=TILES, taps=False, stop_after=None):
    nc = bass.Bass("TRN2", target_bir_lowering=False)

    x_in = nc.dram_tensor("x", [PER_CORE, D], F32, kind="ExternalInput").ap()
    wq_r = nc.dram_tensor("wq_r", [D, D], F32, kind="ExternalInput").ap()
    wdn_r = nc.dram_tensor("wdn_r", [D, DL], F32, kind="ExternalInput").ap()
    wup_r = nc.dram_tensor("wup_r", [DL, 3072], F32, kind="ExternalInput").ap()
    wo_r = nc.dram_tensor("wo_r", [D, D], F32, kind="ExternalInput").ap()
    wg_r = nc.dram_tensor("wg_r", [D, HID], I32, kind="ExternalInput").ap()
    wu_r = nc.dram_tensor("wu_r", [D, HID], I32, kind="ExternalInput").ap()
    wd_r = nc.dram_tensor("wd_r", [HID, D], I32, kind="ExternalInput").ap()
    cb_identb = nc.dram_tensor("cb_identb", [128, 128], BF16, kind="ExternalInput").ap()
    cb_ident3 = nc.dram_tensor("cb_ident3", [128, 128], F32, kind="ExternalInput").ap()
    cb_credc = nc.dram_tensor("cb_credc", [128, 256], BF16, kind="ExternalInput").ap()
    cb_idstk = nc.dram_tensor("cb_idstk", [128, 64], BF16, kind="ExternalInput").ap()
    cb_selc = nc.dram_tensor("cb_selc", [128, 64, 128], BF16, kind="ExternalInput").ap()
    cb_bq = nc.dram_tensor("cb_bq", [128, 8], F32, kind="ExternalInput").ap()
    cb_bdn = nc.dram_tensor("cb_bdn", [128, 2], F32, kind="ExternalInput").ap()
    cb_bup = nc.dram_tensor("cb_bup", [128, 24], F32, kind="ExternalInput").ap()
    cb_bout = nc.dram_tensor("cb_bout", [128, D], F32, kind="ExternalInput").ap()
    cb_eps = nc.dram_tensor("cb_eps", [128, 1], F32, kind="ExternalInput").ap()
    y_out = nc.dram_tensor("y", [PER_CORE, D], F32, kind="ExternalOutput").ap()

    wq_s = nc.dram_tensor("wq_s", [8, 128, 1024], BF16, kind="Internal").ap()
    wdn_s = nc.dram_tensor("wdn_s", [8, 128, 256], BF16, kind="Internal").ap()
    wup_s = nc.dram_tensor("wup_s", [6, 128, 1024], BF16, kind="Internal").ap()
    wo_s = nc.dram_tensor("wo_s", [8, 128, 1024], BF16, kind="Internal").ap()
    wg_s = nc.dram_tensor("wg_s", [16, 128, 1536], BF16, kind="Internal").ap()
    wu_s = nc.dram_tensor("wu_s", [16, 128, 1536], BF16, kind="Internal").ap()
    wd_s = nc.dram_tensor("wd_s", [24, 128, 1024], BF16, kind="Internal").ap()

    bld = Builder()

    with ExitStack() as st:
        sb = lambda name, shape, dt: st.enter_context(nc.sbuf_tensor(name, shape, dt))
        c_identb = sb("c_identb", [128, 128], BF16)
        c_ident3 = sb("c_ident3", [128, 128], F32)
        c_credc = sb("c_credc", [128, 256], BF16)
        c_idstk = sb("c_idstk", [128, 64], BF16)
        c_selc = sb("c_selc", [128, 64, 128], BF16)
        c_bq = sb("c_bq", [128, 8], F32)
        c_bdn = sb("c_bdn", [128, 2], F32)
        c_bup = sb("c_bup", [128, 24], F32)
        c_bout = sb("c_bout", [128, D], F32)
        c_eps = sb("c_eps", [128, 1], F32)
        xres = sb("xres", [128, SUB, D], F32)
        ybuf = sb("ybuf", [128, SUB, D], F32)
        xmbuf = sb("xmbuf", [128, 4, D], F32)
        stt = sb("stt", [128, 32], F32)
        xln = sb("xln", [128, SUB, D], BF16)
        xlnT = sb("xlnT", [128, 8, NT], BF16)
        qfm = sb("qfm", [128, 8, NT], BF16)
        lat = sb("lat", [128, 2, NT], BF16)
        kdup = sb("kdup", [128, 16, NT], BF16)
        vcmp = sb("vcmp", [128, 8, NT], BF16)
        scev = sb("scev", [128, 2, NT], F32)
        smax = sb("smax", [128, SUB, 16], F32)
        etm = sb("etm", [128, SUB, 256], F32)
        atts = sb("atts", [128, 2048], BF16)
        ebc = sb("ebc", [128, 4, NT], BF16)
        pbf = sb("pbf", [128, 8, NT], BF16)
        aoT = sb("aoT", [128, 8, NT], BF16)
        hbuf = sb("hbuf", [128, 24, NT], BF16)
        wring = sb("wring", [128, RING, SLOTW], BF16)
        ps = st.enter_context(nc.psum_tensor("ps", [128, 8, 512], F32))

        sems = {}
        for s in Builder.SEMS:
            sems[s] = st.enter_context(nc.semaphore(s))
        blk = st.enter_context(nc.Block())

        psap = ps.ap()
        psbf = psap.rearrange("p b f -> p (b f)").bitcast(BF16).rearrange(
            "p (b f) -> p b f", b=8)

        hflat_b = hbuf.ap().rearrange("p a b -> p (a b)")
        cin_i32 = hflat_b.bitcast(I32)
        cin_f32 = hflat_b.bitcast(F32)
        aflat_b = aoT.ap().rearrange("p a b -> p (a b)")

        E = bld.emit

        for src, dst in [(cb_identb, c_identb), (cb_ident3, c_ident3),
                         (cb_credc, c_credc), (cb_idstk, c_idstk),
                         (cb_selc, c_selc), (cb_eps, c_eps),
                         (cb_bq, c_bq), (cb_bdn, c_bdn), (cb_bup, c_bup),
                         (cb_bout, c_bout)]:
            E("GP", lambda g, s=src, d=dst: g.dma_start(d.ap(), s),
              inc="X", amt=16)

        xin_ms = [None] * (TILES + 1)

        def emit_xin(t, extra_waits=()):
            dst = (xres if t % 2 == 0 else ybuf).ap()
            srcv = x_in[t * NT:(t + 1) * NT, :].rearrange(
                "(s p) d -> p s d", p=128)
            xin_ms[t] = E("GP", lambda g, s_=srcv, d=dst: g.dma_start(d, s_),
                          waits=extra_waits, inc="X", amt=16)

        emit_xin(0)

        conv = []
        for c in range(8):
            conv.append(("wq", c, wq_r[c * 128:(c + 1) * 128, :], wq_s[c], 1024, F32))
        for c in range(8):
            conv.append(("wdn", c, wdn_r[c * 128:(c + 1) * 128, :], wdn_s[c], 256, F32))
        for i in range(6):
            k2, mh = i // 3, i % 3
            conv.append(("wup", i,
                         wup_r[k2 * 128:(k2 + 1) * 128, mh * 1024:(mh + 1) * 1024],
                         wup_s[i], 1024, F32))
        for c in range(8):
            conv.append(("wo", c, wo_r[c * 128:(c + 1) * 128, :], wo_s[c], 1024, F32))
        for i in range(16):
            hh, c = i // 8, i % 8
            conv.append(("wg", i,
                         wg_r[c * 128:(c + 1) * 128, hh * 1536:(hh + 1) * 1536],
                         wg_s[i], 1536, I32))
        for i in range(16):
            hh, c = i // 8, i % 8
            conv.append(("wu", i,
                         wu_r[c * 128:(c + 1) * 128, hh * 1536:(hh + 1) * 1536],
                         wu_s[i], 1536, I32))
        for kk in range(24):
            conv.append(("wd", kk, wd_r[kk * 128:(kk + 1) * 128, :], wd_s[kk], 1024, I32))

        scratch_ready = {}
        cast_ms = [None] * 4
        outdma_ms = [None, None]
        for i, (tag, idx, src, dst, width, sdt) in enumerate(conv):
            lpar = i % 4
            opar = i % 2
            cin = (cin_i32 if sdt == I32 else cin_f32)
            cin_v = cin[:, lpar * 1536:lpar * 1536 + width]
            cout_v = aflat_b[:, opar * 1536:opar * 1536 + width]
            ld = E("SP", lambda g, s=src, d=cin_v: g.dma_start(d, s),
                   waits=[cast_ms[lpar]], inc="C%d" % lpar, amt=16)
            if i >= 30:
                cast = E("GP", lambda g, d=cout_v, s=cin_v: g.tensor_copy(d, s),
                         waits=[ld, outdma_ms[opar]], inc="G")
            elif i % 2 == 0:
                cast = E("DVE", lambda v, d=cout_v, s=cin_v: v.tensor_copy(d, s),
                         waits=[ld, outdma_ms[opar]], inc="V")
            else:
                cast = E("ACT", lambda a, d=cout_v, s=cin_v: a.copy(d, s),
                         waits=[ld, outdma_ms[opar]], inc="A")
            cast_ms[lpar] = cast
            od = E("GP", lambda g, s=cout_v, d=dst: g.dma_start(d, s),
                   waits=[cast], inc="O%d" % opar, amt=16)
            outdma_ms[opar] = od
            scratch_ready[(tag, idx)] = od
        conv_end2 = [("O0", bld.cnt["O0"]), ("O1", bld.cnt["O1"])]

        tile_chunks = ([("wq", c, 1024) for c in range(8)]
                       + [("wdn", c, 256) for c in range(8)]
                       + [("wup", i, 1024) for i in range(6)]
                       + [("wo", c, 1024) for c in range(8)]
                       + [("wg", i, 1536) for i in range(8)]
                       + [("wu", i, 1536) for i in range(8)]
                       + [("wd", kk, 1024) for kk in range(24)])
        NCH = len(tile_chunks)
        alt_aps = ([("kdup", kdup, k * 1536) for k in range(5)]
                   + [("vcmp", vcmp, k * 1536) for k in range(2)]
                   + [("qfm", qfm, k * 1536) for k in range(2)]
                   + [("pbf", pbf, k * 1536) for k in range(2)]
                   + [("etm", etm, 0), ("aoT", aoT, 0), ("aoT", aoT, 1536),
                      ("scev", scev, 0), ("atts", atts, 0)])
        assert len(alt_aps) == 16

        def alt_ap(k):
            nm, tens, off = alt_aps[k]
            fl = tens.ap().rearrange("p a b -> p (a b)") if tens.ap().ndim == 3                 else tens.ap()
            if fl.dtype != BF16:
                fl = fl.bitcast(BF16)
            return fl[:, off:off + 1536]

        def alt_w(k, t):
            return ("AW%d" % k, 16 * (t + 1))
        scr = {"wq": wq_s, "wdn": wdn_s, "wup": wup_s, "wo": wo_s,
               "wg": wg_s, "wu": wu_s, "wd": wd_s}

        chunk_free_val = []

        def w_ms(gi):
            return ("W%d" % (gi % RING), 16 * (gi // RING + 1))

        def chunk_ap(gi, width):
            return wring.ap()[:, gi % RING, 0:width]

        STOP_CHUNKS = {"ln1": 0, "q": 8, "lat": 16, "kv": 22, "scores": 22,
                       "soft": 22, "attv": 22, "wout": 30, "ln2": 30,
                       "scores_noevac": 22, "gate": 38, "up": 46, "dn": 70}

        def _ring_load(gi):
            tag, idx, width = tile_chunks[gi % NCH]
            slot = gi % RING
            waits = [scratch_ready[(tag, idx)]]
            if gi >= RING:
                waits.append(("F", chunk_free_val[gi - RING]))
            E("SP", lambda e, d=wring.ap()[:, slot, 0:width],
              s=scr[tag][idx]: e.dma_start(d, s),
              waits=waits, inc="W%d" % slot, amt=16)

        def emit_tile_loads(t, alt_free_ms=None):
            base = t * NCH
            upto = STOP_CHUNKS.get(stop_after, NCH) if stop_after else NCH
            for li in range(upto):
                _ring_load(base + li)

        def emit_alt_loads(ks, free_ms):
            for k in ks:
                tag = "wg" if k < 8 else "wu"
                idx = 8 + (k % 8)
                E("GP", lambda e, d=alt_ap(k), s=scr[tag][idx]:
                  e.dma_start(d, s),
                  waits=[scratch_ready[(tag, idx)], free_ms],
                  inc="AW%d" % k, amt=16)

        bank_w = [None] * 8
        bank_r = [None] * 8

        def ln_phase(specs, fin_waits=()):
            n = len(specs)
            rsm = [None] * n; mn = [None] * n; sub = [None] * n
            sq = [None] * n; var = [None] * n; sd = [None] * n; rs = [None] * n
            fins = [None] * n
            for i, (s, src_ap, ok_, sm, sp, res_ap) in enumerate(specs):
                so = s * 8
                if sp is None:
                    rsm[i] = E("DVE", lambda v, o=stt.ap()[:, so:so + 1],
                               i_=src_ap:
                               v.reduce_sum(o, i_, AX.X), waits=list(sm),
                               inc="V")
                else:
                    rsm[i] = sp
            for i, (s, src_ap, ok_, sm, sp, res_ap) in enumerate(specs):
                so = s * 8
                sub[i] = E("DVE", lambda v, o=xmbuf.ap()[:, s, :], i_=src_ap,
                           sb2=stt.ap()[:, so:so + 1].broadcast_to([128, D]):
                           v.scalar_tensor_tensor(o, sb2, -1.0 / D, i_,
                                                  ALU.mult, ALU.add),
                           waits=[rsm[i]] + list(sm), inc="V")
            for i, (s, src_ap, ok_, sm, sp, res_ap) in enumerate(specs):
                so = s * 8
                sq[i] = E("ACT", lambda a,
                          o=ebc.ap().rearrange("p a b -> p (a b)")[:, 0:D],
                          i_=xmbuf.ap()[:, s, :],
                          ac=stt.ap()[:, so + 2:so + 3]:
                          a.activation(o, i_, AF.Square, accum_out=ac),
                          waits=[sub[i]], inc="A")
            for i, (s, src_ap, ok_, sm, sp, res_ap) in enumerate(specs):
                so = s * 8
                sd[i] = E("ACT", lambda a, o=stt.ap()[:, so + 4:so + 5],
                          i_=stt.ap()[:, so + 2:so + 3]:
                          a.activation(o, i_, AF.Ln, scale=1.0 / D,
                                       bias=c_eps.ap()),
                          waits=[sq[i]], inc="A")
            for i, (s, src_ap, ok_, sm, sp, res_ap) in enumerate(specs):
                so = s * 8
                rs[i] = E("ACT", lambda a, o=stt.ap()[:, so + 5:so + 6],
                          i_=stt.ap()[:, so + 4:so + 5]:
                          a.activation(o, i_, AF.Exp, scale=-0.5),
                          waits=[sd[i]], inc="A")
            for i, (s, src_ap, ok_, sm, sp, res_ap) in enumerate(specs):
                so = s * 8
                if ok_ == "xln":
                    fins[i] = E("DVE", lambda v, o=xln.ap()[:, s, :],
                                i_=xmbuf.ap()[:, s, :],
                                sc=stt.ap()[:, so + 5:so + 6]:
                                v.tensor_scalar_mul(o, i_, sc),
                                waits=[rs[i]] + list(fin_waits), inc="V")
                elif ok_ == "resid":
                    fins[i] = E("DVE", lambda v, o=res_ap, i_=xmbuf.ap()[:, s, :],
                                i1=res_ap, sc=stt.ap()[:, so + 5:so + 6]:
                                v.scalar_tensor_tensor(o, i_, sc, i1,
                                                       ALU.mult, ALU.add),
                                waits=[rs[i]], inc="V")
                else:
                    fins[i] = E("DVE", lambda v, o=res_ap, i_=xmbuf.ap()[:, s, :],
                                i1=res_ap, sc=stt.ap()[:, so + 5:so + 6],
                                ac=stt.ap()[:, so:so + 1]:
                                v.scalar_tensor_tensor(o, i_, sc, i1,
                                                       ALU.mult, ALU.add,
                                                       accum_out=ac),
                                waits=[rs[i]], inc="V")
            return fins, sub

        def emit_transposes(ln_ms):
            evs = []
            tms = []
            for s in range(SUB):
                w = [ln_ms[s], bank_r[s]]
                tm = None
                for c in range(8):
                    tm = E("PE", lambda e, o=psbf[:, s, c * 128:(c + 1) * 128],
                           i=xln.ap()[:, s, c * 128:(c + 1) * 128]:
                           e.transpose(o, i, c_identb.ap()),
                           waits=(w if c == 0 else []),
                           inc=("P" if c == 7 else None))
                bank_w[s] = tm
                ev = E("DVE", lambda v,
                       o=xlnT.ap()[:, :, s * 128:(s + 1) * 128],
                       i=psbf[:, s, :].rearrange("p (c q) -> p c q", c=8):
                       v.tensor_copy(o, i), waits=[tm], inc="V")
                bank_r[s] = ev
                evs.append(ev)
                tms.append(tm)
            return evs[-1], tms[-1]

        prev_out_ms = [None]
        for t in range(n_tiles):
            first_tile = (t == 0)
            base = t * NCH
            res_b, scr_b = (xres, ybuf) if t % 2 == 0 else (ybuf, xres)
            if t + 1 < n_tiles:
                emit_xin(t + 1, extra_waits=[prev_out_ms[0]])

            def gq(c): return base + c
            def gdn(c): return base + 8 + c
            def gup(i): return base + 16 + i
            def gwo(c): return base + 22 + c
            def gwg(i): return base + 30 + i
            def gwu(i): return base + 38 + i
            def gwd(kk): return base + 46 + kk

            if t == 0:
                ln1, _ = ln_phase([(s, res_b.ap()[:, s, :], "xln",
                                    [xin_ms[0]], None, None)
                                   for s in range(SUB)])
                xlnT_ready, _ = emit_transposes(ln1)
            else:
                xlnT_ready = pending_xlnT
            if taps and t == 0:
                d1 = nc.dram_tensor("d_xlnT1", [128, 8, NT], BF16,
                                    kind="ExternalOutput").ap()
                E("GP", lambda g, d=d1: g.dma_start(d, xlnT.ap()),
                  waits=[xlnT_ready], inc="X", amt=16)
                d2 = nc.dram_tensor("d_xln1", [128, SUB, D], BF16,
                                    kind="ExternalOutput").ap()
                E("GP", lambda g, d=d2: g.dma_start(d, xln.ap()),
                  waits=[ln1[3]], inc="X", amt=16)
            if stop_after == "ln1":
                for nm2, buf2 in [("d_xres", xres), ("d_stt", stt),
                                  ("d_xm", xmbuf)]:
                    shp2 = list(buf2.ap().shape)
                    dr2 = nc.dram_tensor(nm2, shp2, buf2.ap().dtype,
                                         kind="ExternalOutput").ap()
                    E("GP", lambda g, d=dr2, s=buf2: g.dma_start(d, s.ap()),
                      waits=[("P", bld.cnt["P"]), ("A", bld.cnt["A"]),
                             ("V", bld.cnt["V"])], inc="X", amt=16)
                emit_tile_loads(t, None); continue

            qfm_ev = []
            for m in range(8):
                b = m % 4
                w = [xlnT_ready, bank_r[b]]
                mm = None
                for k in range(8):
                    mm = E("PE", lambda e, o=psap[:, b, :],
                           l=chunk_ap(gq(k), 1024), m_=m, k_=k:
                           e.matmul(o, l[:, m_ * 128:(m_ + 1) * 128],
                                    xlnT.ap()[:, k_, :],
                                    start=(k_ == 0), stop=(k_ == 7)),
                           waits=(w + [w_ms(gq(k))] if k == 0
                                  else [w_ms(gq(k))]),
                           inc=(("F" if m == 7 else "P") if k == 7 else None))
                bank_w[b] = mm
                ev = E("ACT", lambda a, o=qfm.ap()[:, m, :], i=psap[:, b, :],
                       bi=c_bq.ap()[:, m:m + 1]:
                       a.activation(o, i, AF.Identity, bias=bi), waits=[mm], inc="A")
                bank_r[b] = ev
                qfm_ev.append(ev)
            chunk_free_val.extend([bld.cnt["F"]] * 8)
            if stop_after == "q":
                emit_tile_loads(t, None); continue

            lat_ev = []
            for m in range(2):
                b = 4 + m
                w = [xlnT_ready, bank_r[b]]
                mm = None
                for k in range(8):
                    mm = E("PE", lambda e, o=psap[:, b, :],
                           l=chunk_ap(gdn(k), 256), m_=m, k_=k:
                           e.matmul(o, l[:, m_ * 128:(m_ + 1) * 128],
                                    xlnT.ap()[:, k_, :],
                                    start=(k_ == 0), stop=(k_ == 7)),
                           waits=(w + [w_ms(gdn(k))] if k == 0
                                  else [w_ms(gdn(k))]),
                           inc=(("F" if m == 1 else "P") if k == 7 else None))
                bank_w[b] = mm
                ev = E("ACT", lambda a, o=lat.ap()[:, m, :], i=psap[:, b, :],
                       bi=c_bdn.ap()[:, m:m + 1]:
                       a.activation(o, i, AF.Identity, bias=bi), waits=[mm], inc="A")
                bank_r[b] = ev
                lat_ev.append(ev)
            chunk_free_val.extend([bld.cnt["F"]] * 8)
            if stop_after == "lat":
                emit_tile_loads(t, None); continue

            kv_ev = [None] * 24
            for g in range(24):
                b = g % 4
                w = [lat_ev[-1], bank_r[b]]
                mm = None
                for k2 in range(2):
                    mh = g // 8
                    mloc = g % 8
                    ci = gup(k2 * 3 + mh)
                    mm = E("PE", lambda e, o=psap[:, b, :],
                           l=chunk_ap(ci, 1024), ml=mloc, k_=k2:
                           e.matmul(o, l[:, ml * 128:(ml + 1) * 128],
                                    lat.ap()[:, k_, :],
                                    start=(k_ == 0), stop=(k_ == 1)),
                           waits=(w + [w_ms(ci)] if k2 == 0 else [w_ms(ci)]),
                           inc=(("F" if g == 23 else "P") if k2 == 1 else None))
                bank_w[b] = mm
                dst = kdup.ap()[:, g, :] if g < 16 else vcmp.ap()[:, g - 16, :]
                ev = E("ACT", lambda a, o=dst, i=psap[:, b, :],
                       bi=c_bup.ap()[:, g:g + 1]:
                       a.activation(o, i, AF.Identity, bias=bi), waits=[mm], inc="A")
                bank_r[b] = ev
                kv_ev[g] = ev
            chunk_free_val.extend([bld.cnt["F"]] * 6)
            if stop_after == "kv":
                emit_tile_loads(t, None); continue

            sred_ms = [None] * 128
            idx = 0
            pair_i = 0
            for c in range(8):
                for gp in range(8):
                    g = 2 * gp
                    w = [qfm_ev[c], kv_ev[g + 1]]
                    if idx >= 8:
                        w.append(sred_ms[idx - 7])
                    peng = "DVE"
                    sl = idx % 8
                    pm = E(peng, lambda v, o=pbf.ap()[:, sl:sl + 2, :],
                           a=qfm.ap()[:, c:c + 1, :].broadcast_to([128, 2, NT]),
                           b2=kdup.ap()[:, g:g + 2, :]:
                           v.tensor_tensor(o, a, b2, ALU.mult),
                           waits=w, inc=("V" if peng == "DVE" else "G"))
                    pair_i += 1
                    for q2 in range(2):
                        bank = 6 + c // 4
                        r1 = ((2 * c) % 8) * 16 + g + q2
                        first = (c % 4 == 0 and g + q2 == 0)
                        last = (c % 4 == 3 and g + q2 == 15)
                        w2 = [pm]
                        if first:
                            w2.append(bank_r[bank])
                        mm = E("PE", lambda e, o=psap[:, bank, :],
                               l=c_credc.ap()[:, 127 - r1:255 - r1],
                               pi=idx % 8, f=first, la=last:
                               e.matmul(o, l, pbf.ap()[:, pi, :],
                                        start=f, stop=la),
                               waits=(w2 if q2 == 0 else []), inc="P")
                        sred_ms[idx] = mm
                        if last:
                            bank_w[bank] = mm
                        idx += 1

            if stop_after == "scores_noevac":
                emit_tile_loads(t, None); continue
            sev = []
            for c2 in range(2):
                ev = E("ACT", lambda a, o=scev.ap()[:, c2, :],
                       i=psap[:, 6 + c2, :]: a.copy(o, i),
                       waits=[bank_w[6 + c2]], inc="A")
                bank_r[6 + c2] = ev
                sev.append(ev)
            strn_last = None
            for s in range(SUB):
                b = 4 + s // 2
                w = [sev[-1], bank_r[4], bank_r[5]]
                for c2 in range(2):
                    strn_last = E("PE", lambda e,
                                  o=psap[:, b, (s % 2) * 256 + c2 * 128:
                                         (s % 2) * 256 + (c2 + 1) * 128],
                                  i=scev.ap()[:, c2, s * 128:(s + 1) * 128]:
                                  e.transpose(o, i, c_ident3.ap()),
                                  waits=(w if (s == 0 and c2 == 0) else []),
                                  inc=("P" if (s == 3 and c2 == 1) else None))
            bank_w[4] = strn_last
            bank_w[5] = strn_last

            tmv_all = psap[:, 4:6, :].rearrange("p b (x h g) -> p (b x h) g",
                                                x=2, h=16)
            smax_f = smax.ap().rearrange("p s h -> p (s h)")
            etm_f = etm.ap().rearrange("p s q -> p (s q)")
            mx = E("DVE", lambda v, o=smax_f, i=tmv_all:
                   v.reduce_max(o, i, AX.X),
                   waits=[strn_last], inc="V")
            sb_ = E("DVE", lambda v,
                    o=etm_f.rearrange("p (a g) -> p a g", g=16),
                    i=tmv_all,
                    m=smax_f.unsqueeze(2).broadcast_to([128, 64, 16]):
                    v.tensor_tensor(o, i, m, ALU.subtract),
                    waits=[mx], inc="V")
            ex = E("ACT", lambda a, o=etm_f, i=etm_f:
                   a.activation(o, i, AF.Exp, scale=0.125),
                   waits=[sb_], inc="A")
            zs = E("DVE", lambda v, o=smax_f,
                   i=etm_f.rearrange("p (a g) -> p a g", g=16):
                   v.reduce_sum(o, i, AX.X), waits=[ex], inc="V")
            rc = E("DVE", lambda v, o=smax_f, i=smax_f: v.reciprocal(o, i),
                   waits=[zs], inc="V")
            nm = E("DVE", lambda v,
                   o=atts.ap()[:, 1024:2048].rearrange("p (a g) -> p a g", g=16),
                   i=etm_f.rearrange("p (a g) -> p a g", g=16),
                   m=smax_f.unsqueeze(2).broadcast_to([128, 64, 16]):
                   v.tensor_tensor(o, i, m, ALU.mult),
                   waits=[rc], inc="V")
            soft = [nm] * SUB
            bank_r[4] = nm
            bank_r[5] = nm

            atrn = None
            w0 = soft + [bank_r[6]]
            for c2 in range(2):
                for s in range(SUB):
                    atrn = E("PE", lambda e,
                             o=psbf[:, 6, c2 * 512 + s * 128:c2 * 512 + (s + 1) * 128],
                             i=atts.ap()[:, 1024 + s * 256 + c2 * 128:1024 + s * 256 + (c2 + 1) * 128]:
                             e.transpose(o, i, c_identb.ap()),
                             waits=(w0 if (c2 == 0 and s == 0) else []),
                             inc=("P" if (c2 == 1 and s == 3) else None))
            bank_w[6] = atrn
            attT_ev = []
            for c2 in range(2):
                ev = E("DVE", lambda v, o=atts.ap()[:, c2 * 512:(c2 + 1) * 512],
                       i=psbf[:, 6, c2 * 512:(c2 + 1) * 512]:
                       v.tensor_copy(o, i), waits=[atrn], inc="V")
                attT_ev.append(ev)
            bank_r[6] = attT_ev[-1]

            idred_ms = [None] * 128
            bc_ms = [None] * 8
            pp_ms = [None] * 4
            ao_ev = [None] * 8
            pidx0 = 0

            def emit_bcast(h, j):
                eb = j % 4
                si = (h % 8) * 8 + j
                w = [attT_ev[h // 8], bank_r[eb]]
                bc_ms[j] = E("PE", lambda e, o=psap[:, eb, :],
                             l=c_selc.ap()[:, si, :], ti=h // 8:
                             e.matmul(o, l, atts.ap()[:, ti * 512:(ti + 1) * 512],
                                      start=True, stop=True),
                             waits=w, inc="P")
                bank_w[eb] = bc_ms[j]

            def emit_prodpair(h, jp, pidx):
                b0 = (2 * jp) % 4
                w2 = [bc_ms[2 * jp + 1], kv_ev[16 + 2 * jp + 1]]
                if pidx >= 8:
                    w2.append(idred_ms[pidx - 7])
                else:
                    w2.append(sred_ms[120 + pidx + 1])
                sl = pidx % 8
                pp_ms[jp] = E("DVE", lambda v, o=pbf.ap()[:, sl:sl + 2, :],
                              a=psap[:, b0:b0 + 2, :],
                              b2=vcmp.ap()[:, 2 * jp:2 * jp + 2, :]:
                              v.tensor_tensor(o, a, b2, ALU.mult),
                              waits=w2, inc="V")
                bank_r[b0] = pp_ms[jp]
                bank_r[b0 + 1] = pp_ms[jp]

            def emit_idred(h, j, pidx, ob):
                w3 = [pp_ms[j // 2]]
                if j == 0:
                    w3.append(bank_r[ob])
                mm = E("PE", lambda e,
                       o=psap[(h % 2) * 64:(h % 2) * 64 + 64, ob, :],
                       pi=pidx % 8, f=(j == 0), la=(j == 7):
                       e.matmul(o, c_idstk.ap(), pbf.ap()[:, pi, :],
                                start=f, stop=la),
                       waits=w3, inc="P")
                idred_ms[pidx] = mm
                return mm

            st_bc = [[None] * 8, [None] * 8]
            st_pp = [[None] * 4, [None] * 4]

            def unitA(h, jp, pidx0):
                par = h % 2
                bc_ms[:] = st_bc[par]
                pp_ms[:] = st_pp[par]
                emit_bcast(h, 2 * jp)
                emit_bcast(h, 2 * jp + 1)
                emit_prodpair(h, jp, pidx0 + 2 * jp)
                st_bc[par][:] = bc_ms
                st_pp[par][:] = pp_ms

            def unitB(h, jp, pidx0, ob):
                par = h % 2
                pp_ms[:] = st_pp[par]
                last = emit_idred(h, 2 * jp, pidx0 + 2 * jp, ob)
                last = emit_idred(h, 2 * jp + 1, pidx0 + 2 * jp + 1, ob)
                return last

            for jp in range(4):
                unitA(0, jp, 0)
            for h in range(H):
                pair = h // 2
                ob = 7 - (pair % 2)
                last = None
                for jp in range(4):
                    last = unitB(h, jp, 8 * h, ob)
                    if h + 1 < H:
                        unitA(h + 1, jp, 8 * (h + 1))
                if h % 2 == 1:
                    bank_w[ob] = last
                    w4 = [bank_w[ob]]
                    if first_tile and pair == 0:
                        w4.extend(conv_end2)
                    ev = E("ACT", lambda a, o=aoT.ap()[:, pair, :],
                           i=psap[:, ob, :]: a.copy(o, i), waits=w4, inc="A")
                    bank_r[ob] = ev
                    ao_ev[pair] = ev

            attv_last = idred_ms[127]
            if stop_after is None:
                emit_alt_loads([k for k in range(16) if k not in (12, 13)],
                               attv_last)
            if stop_after == "attv":
                emit_tile_loads(t, None); continue
            wout_ev = [None] * SUB
            wout_mm = [None] * SUB
            attn_fin = [None] * SUB
            ln2 = [None] * SUB
            for pr in (0, 1):
                ss = (2 * pr, 2 * pr + 1)
                for s in ss:
                    sb0 = 4 + (s % 2) * 2
                    w = [bank_r[sb0], bank_r[sb0 + 1]] + ao_ev
                    mm = None
                    for c in range(8):
                        for half in range(2):
                            wk = [w_ms(gwo(c))] if half == 0 else []
                            mm = E("PE", lambda e, o=psap[:, sb0 + half, :],
                                   l=aoT.ap()[:, c, s * 128:(s + 1) * 128],
                                   wch=chunk_ap(gwo(c), 1024), hf=half, c_=c:
                                   e.matmul(o, l,
                                            wch[:, hf * 512:(hf + 1) * 512],
                                            start=(c_ == 0), stop=(c_ == 7)),
                                   waits=(w + wk if (c == 0 and half == 0)
                                          else wk),
                                   inc=(("F" if s == 3 else "P")
                                        if (c == 7 and half == 1) else None))
                    bank_w[sb0] = mm
                    bank_w[sb0 + 1] = mm
                    wout_mm[s] = mm
                for s in ss:
                    sb0 = 4 + (s % 2) * 2
                    ev = E("DVE", lambda v, o=xmbuf.ap()[:, s, :],
                           i=psap[:, sb0:sb0 + 2, :].rearrange(
                               "p b f -> p (b f)"),
                           ac=stt.ap()[:, s * 8:s * 8 + 1]:
                           v.scalar_tensor_tensor(o, i, 1.0, c_bout.ap(),
                                                  ALU.mult, ALU.add,
                                                  accum_out=ac),
                           waits=[wout_mm[s]], inc="V")
                    bank_r[sb0] = ev
                    bank_r[sb0 + 1] = ev
                    wout_ev[s] = ev
                af, _ = ln_phase(
                    [(s, xmbuf.ap()[:, s, :], "resid_accum", [wout_ev[s]],
                      wout_ev[s], res_b.ap()[:, s, :]) for s in ss])
                attn_fin[2 * pr:2 * pr + 2] = af
                l2, _ = ln_phase([(s, res_b.ap()[:, s, :], "xln",
                                   [attn_fin[s]], attn_fin[s], None)
                                  for s in ss])
                ln2[2 * pr:2 * pr + 2] = l2
            chunk_free_val.extend([bld.cnt["F"]] * 8)
            wout_last = wout_mm[3]
            if stop_after is None:
                emit_alt_loads([12, 13], wout_last)
            if stop_after == "wout":
                emit_tile_loads(t, None); continue

            xlnT_ready, t2_tm = emit_transposes(ln2)
            ln1_next = [None] * SUB
            if t + 1 < n_tiles:
                ln1_next, _ = ln_phase(
                    [(s, scr_b.ap()[:, s, :], "xln", [xin_ms[t + 1]],
                      None, None) for s in range(SUB)],
                    fin_waits=[("P", t2_tm[1])])
                pending_ln1 = ln1_next

            silu_ms = [None] * 24
            for hh in range(2):
                for mmi in range(12):
                    m = hh * 12 + mmi
                    b = 2 * (m % 2)
                    w = [xlnT_ready, bank_r[b]]
                    mm = None
                    for k in range(8):
                        if hh == 0:
                            lap = chunk_ap(gwg(k), 1536)
                            wk = [w_ms(gwg(k))]
                        else:
                            lap = alt_ap(k)
                            wk = [alt_w(k, t)]
                        mm = E("PE", lambda e, o=psap[:, b, :],
                               l=lap, ml=mmi, k_=k:
                               e.matmul(o, l[:, ml * 128:(ml + 1) * 128],
                                        xlnT.ap()[:, k_, :],
                                        start=(k_ == 0), stop=(k_ == 7)),
                               waits=(w + wk if k == 0 else wk),
                               inc=(("F" if (mmi == 11 and hh == 0) else "P")
                                    if k == 7 else None))
                    bank_w[b] = mm
                    w2 = [mm]
                    if first_tile and m == 0:
                        w2.extend(conv_end2)
                    ev = E("ACT", lambda a, o=hbuf.ap()[:, m, :], i=psap[:, b, :]:
                           a.activation(o, i, AF.Silu, scale=QSCALE),
                           waits=w2, inc="A")
                    bank_r[b] = ev
                    silu_ms[m] = ev
                if hh == 0:
                    chunk_free_val.extend([bld.cnt["F"]] * 8)

            stt_ms = [None] * 24
            for hh in range(2):
                for mmi in range(12):
                    m = hh * 12 + mmi
                    b = 2 * (m % 2) + 1
                    w = [xlnT_ready, bank_r[b]]
                    mm = None
                    for k in range(8):
                        if hh == 0:
                            lap = chunk_ap(gwu(k), 1536)
                            wk = [w_ms(gwu(k))]
                        else:
                            lap = alt_ap(8 + k)
                            wk = [alt_w(8 + k, t)]
                        mm = E("PE", lambda e, o=psap[:, b, :],
                               l=lap, ml=mmi, k_=k:
                               e.matmul(o, l[:, ml * 128:(ml + 1) * 128],
                                        xlnT.ap()[:, k_, :],
                                        start=(k_ == 0), stop=(k_ == 7)),
                               waits=(w + wk if k == 0 else wk),
                               inc=(("F" if (mmi == 11 and hh == 0) else "P")
                                    if k == 7 else None))
                    bank_w[b] = mm
                    ev = E("DVE", lambda v, o=hbuf.ap()[:, m, :], i=psap[:, b, :],
                           i1=hbuf.ap()[:, m, :]:
                           v.scalar_tensor_tensor(o, i, QSCALE, i1,
                                                  ALU.mult, ALU.mult),
                           waits=[mm, silu_ms[m]], inc="V")
                    bank_r[b] = ev
                    stt_ms[m] = ev
                if hh == 0:
                    chunk_free_val.extend([bld.cnt["F"]] * 8)

            for kk in range(24):
                for s in range(SUB):
                    for half in range(2):
                        b = s * 2 + half
                        w = [w_ms(gwd(kk)), stt_ms[kk]]
                        if kk == 0:
                            w.append(bank_r[b])
                        mm = E("PE", lambda e, o=psap[:, b, :],
                               l=hbuf.ap()[:, kk, s * 128:(s + 1) * 128],
                               wch=chunk_ap(gwd(kk), 1024), hf=half, k_=kk:
                               e.matmul(o, l, wch[:, hf * 512:(hf + 1) * 512],
                                        start=(k_ == 0), stop=(k_ == 23)),
                               waits=w,
                               inc=(("F" if (s == 3 and half == 1) else
                                     ("P" if kk == 23 else None))))
                        if kk == 23:
                            bank_w[b] = mm
                chunk_free_val.append(bld.cnt["F"])

            dn_ev = [None] * SUB
            for s in range(SUB):
                ev = E("DVE", lambda v, o=xmbuf.ap()[:, s, :],
                       i=psap[:, s * 2:s * 2 + 2, :].rearrange("p b f -> p (b f)"),
                       ac=stt.ap()[:, s * 8:s * 8 + 1]:
                       v.tensor_scalar(o, i, 1.0, 0.0, ALU.mult, ALU.add,
                                       accum_out=ac),
                       waits=[bank_w[s * 2], bank_w[s * 2 + 1],
                              ln1_next[s]], inc="V")
                bank_r[s * 2] = ev
                bank_r[s * 2 + 1] = ev
                dn_ev[s] = ev
            if t + 1 < n_tiles:
                pending_xlnT, _ = emit_transposes(ln1_next)
            dn_fin, _ = ln_phase(
                [(s, xmbuf.ap()[:, s, :], "resid", [dn_ev[s]],
                  dn_ev[s], res_b.ap()[:, s, :]) for s in range(SUB)])
            dst = y_out[t * NT:(t + 1) * NT, :].rearrange("(s p) d -> p s d", p=128)
            out_ms = E("GP", lambda g, d=dst, sb_=res_b: g.dma_start(d, sb_.ap()),
                       waits=dn_fin, inc="X", amt=16)
            prev_out_ms[0] = out_ms

            emit_tile_loads(t)

        if taps:
            tap_list = [
                ("d_xlnT", xlnT), ("d_qfm", qfm), ("d_lat", lat),
                ("d_kdup", kdup), ("d_vcmp", vcmp), ("d_scev", scev),
                ("d_atts", atts), ("d_aoT", aoT),
                ("d_hbuf", hbuf), ("d_ybuf", ybuf), ("d_etm", etm),
            ]
            for nm, dr_s, shp in [("d_wqs", wq_s, [8, 128, 1024]),
                                  ("d_wds", wd_s, [24, 128, 1024])]:
                drt = nc.dram_tensor(nm, shp, BF16, kind="ExternalOutput").ap()
                E("GP", lambda g, d=drt, s=dr_s: g.dma_start(d, s),
                  waits=[("X", bld.cnt["X"])], inc="X", amt=16)
            allw = [("P", bld.cnt["P"]), ("A", bld.cnt["A"]),
                    ("V", bld.cnt["V"]), ("X", bld.cnt["X"])]
            for nm, buf in tap_list:
                shp = list(buf.ap().shape)
                dt = buf.ap().dtype
                dr = nc.dram_tensor(nm, shp, dt, kind="ExternalOutput").ap()
                E("GP", lambda g, d=dr, s=buf: g.dma_start(d, s.ap()),
                  waits=allw, inc="X", amt=16)

        E("GP", lambda g: g.engine_nop(), waits=[("X", bld.cnt["X"])])

        def render(eng_name, deco):
            @deco
            def _(e):
                waited = {}
                for fn, waits, inc, amt, extra in bld.ops[eng_name]:
                    for (snm, val) in waits:
                        if waited.get(snm, -1) < val:
                            e.wait_ge(sems[snm], val)
                            waited[snm] = val
                    inst = fn(e)
                    if inc is not None:
                        inst = inst.then_inc(sems[inc], amt)
                    if extra is not None:
                        inst.then_inc(sems[extra[0]], extra[1])

        render("SP", blk.sync)
        render("GP", blk.gpsimd)
        render("PE", blk.tensor)
        render("ACT", blk.scalar)
        render("DVE", blk.vector)

    nc.finalize()
    return nc


_NC_CACHE = None


def _get_nc():
    global _NC_CACHE
    if _NC_CACHE is None:
        _NC_CACHE = build_program()
    return _NC_CACHE


def kernel(L0_attn_x, L0_attn_Wq, L0_attn_bq, L0_attn_Wdown, L0_attn_bdown,
           L0_attn_Wup, L0_attn_bup, L0_attn_Wout, L0_attn_bout,
           L0_ffn_w_gate, L0_ffn_w_up, L0_ffn_w_down):
    x = np.ascontiguousarray(np.asarray(L0_attn_x, dtype=np.float32).reshape(NTOK, D))
    wq = np.ascontiguousarray(np.asarray(L0_attn_Wq, dtype=np.float32))
    bq = np.asarray(L0_attn_bq, dtype=np.float32)
    wdn = np.ascontiguousarray(np.asarray(L0_attn_Wdown, dtype=np.float32))
    bdn = np.asarray(L0_attn_bdown, dtype=np.float32)
    wup = np.asarray(L0_attn_Wup, dtype=np.float32)
    bup = np.asarray(L0_attn_bup, dtype=np.float32)
    wo = np.ascontiguousarray(np.asarray(L0_attn_Wout, dtype=np.float32))
    bout = np.asarray(L0_attn_bout, dtype=np.float32)
    wg = np.asarray(L0_ffn_w_gate, dtype=np.int32)
    wu = np.asarray(L0_ffn_w_up, dtype=np.int32)
    wd = np.asarray(L0_ffn_w_down, dtype=np.int32)

    wup4 = wup.reshape(DL, H, 2, DH)
    kcols = wup4[:, :, 0, :]
    vcols = wup4[:, :, 1, :]
    wup_p = np.ascontiguousarray(np.concatenate(
        [np.repeat(kcols, 2, axis=1).reshape(DL, 2048),
         vcols.reshape(DL, 1024)], axis=1))
    bup4 = bup.reshape(H, 2, DH)
    bup_p = np.concatenate([np.repeat(bup4[:, 0, :], 2, axis=0).reshape(2048),
                            bup4[:, 1, :].reshape(1024)])
    bq_fm = np.ascontiguousarray(bq.reshape(8, 128).T)
    bdn_fm = np.ascontiguousarray(bdn.reshape(2, 128).T)
    bup_fm = np.ascontiguousarray(bup_p.reshape(24, 128).T)
    bout_bc = np.ascontiguousarray(np.broadcast_to(bout, (128, D)))

    wgT = np.ascontiguousarray(wg.T)
    wuT = np.ascontiguousarray(wu.T)
    wdT = np.ascontiguousarray(wd.T)

    identb = np.eye(128, dtype=bfdt)
    ident3 = np.eye(128, dtype=np.float32)
    credc = np.zeros((128, 256), dtype=bfdt)
    credc[0:64, 127] = 1.0
    credc[64:128, 143] = 1.0
    idstk = np.zeros((128, 64), dtype=bfdt)
    idstk[np.arange(128), np.arange(128) % 64] = 1.0
    selc = np.zeros((128, 64, 128), dtype=bfdt)
    for si in range(64):
        selc[2 * si, si, 0:64] = 1.0
        selc[2 * si + 1, si, 64:128] = 1.0

    shards = np.split(x, NCORES, axis=0)
    common = {
        "wq_r": wq, "wdn_r": wdn, "wup_r": wup_p, "wo_r": wo,
        "wg_r": wgT, "wu_r": wuT, "wd_r": wdT,
        "cb_identb": identb, "cb_ident3": ident3, "cb_credc": credc,
        "cb_idstk": idstk, "cb_selc": selc, "cb_bq": bq_fm, "cb_bdn": bdn_fm,
        "cb_bup": bup_fm, "cb_bout": bout_bc,
        "cb_eps": np.full((128, 1), EPS, np.float32),
    }
    in_maps = [dict(common, x=np.ascontiguousarray(s)) for s in shards]
    nc = _get_nc()
    res = run_bass_kernel_spmd(nc, in_maps, core_ids=list(range(NCORES)))
    out = np.concatenate([r["y"] for r in res.results], axis=0)
    return out.reshape(4, 8192, D)


# version 22
# speedup vs baseline: 1.1546x; 1.0025x over previous
import numpy as np
import ml_dtypes
from contextlib import ExitStack

import concourse.bass as bass
import concourse.mybir as mybir
from concourse.bass_utils import run_bass_kernel_spmd

F32 = mybir.dt.float32
BF16 = mybir.dt.bfloat16
I32 = mybir.dt.int32
AF = mybir.ActivationFunctionType
ALU = mybir.AluOpType
AX = mybir.AxisListType
bfdt = ml_dtypes.bfloat16

NCORES = 8
D = 1024
H = 16
DH = 64
DL = 256
HID = 3072
NTOK = 4 * 8192
PER_CORE = NTOK // NCORES
NT = 512
SUB = NT // 128
TILES = PER_CORE // NT
EPS = 1e-5
QSCALE = 1.0 / 127.0
RING = 10
SLOTW = 1536

ENGINES = ("SP", "GP", "PE", "ACT", "DVE")


class Builder:

    SEMS = (["X", "P", "A", "V", "F", "G", "C0", "C1", "C2", "C3",
             "O0", "O1"]
            + ["W%d" % i for i in range(RING)]
            + ["AW%d" % i for i in range(16)])

    def __init__(self):
        self.ops = {e: [] for e in ENGINES}
        self.cnt = {s: 0 for s in self.SEMS}

    def emit(self, eng, fn, waits=(), inc=None, amt=1, extra_inc=None):
        ws = [w for w in waits if w is not None]
        self.ops[eng].append((fn, ws, inc, amt, extra_inc))
        ms = None
        if inc is not None:
            self.cnt[inc] += amt
            ms = (inc, self.cnt[inc])
        if extra_inc is not None:
            s2, a2 = extra_inc
            self.cnt[s2] += a2
        return ms


def build_program(n_tiles=TILES, taps=False, stop_after=None,
                  INTERLEAVE_T0=True):
    nc = bass.Bass("TRN2", target_bir_lowering=False)

    x_in = nc.dram_tensor("x", [PER_CORE, D], F32, kind="ExternalInput").ap()
    wq_r = nc.dram_tensor("wq_r", [D, D], F32, kind="ExternalInput").ap()
    wdn_r = nc.dram_tensor("wdn_r", [D, DL], F32, kind="ExternalInput").ap()
    wup_r = nc.dram_tensor("wup_r", [DL, 3072], F32, kind="ExternalInput").ap()
    wo_r = nc.dram_tensor("wo_r", [D, D], F32, kind="ExternalInput").ap()
    wg_r = nc.dram_tensor("wg_r", [D, HID], I32, kind="ExternalInput").ap()
    wu_r = nc.dram_tensor("wu_r", [D, HID], I32, kind="ExternalInput").ap()
    wd_r = nc.dram_tensor("wd_r", [HID, D], I32, kind="ExternalInput").ap()
    cb_identb = nc.dram_tensor("cb_identb", [128, 128], BF16, kind="ExternalInput").ap()
    cb_ident3 = nc.dram_tensor("cb_ident3", [128, 128], F32, kind="ExternalInput").ap()
    cb_credc = nc.dram_tensor("cb_credc", [128, 256], BF16, kind="ExternalInput").ap()
    cb_idstk = nc.dram_tensor("cb_idstk", [128, 64], BF16, kind="ExternalInput").ap()
    cb_selc = nc.dram_tensor("cb_selc", [128, 64, 128], BF16, kind="ExternalInput").ap()
    cb_bq = nc.dram_tensor("cb_bq", [128, 8], F32, kind="ExternalInput").ap()
    cb_bdn = nc.dram_tensor("cb_bdn", [128, 2], F32, kind="ExternalInput").ap()
    cb_bup = nc.dram_tensor("cb_bup", [128, 24], F32, kind="ExternalInput").ap()
    cb_bout = nc.dram_tensor("cb_bout", [128, D], F32, kind="ExternalInput").ap()
    cb_eps = nc.dram_tensor("cb_eps", [128, 1], F32, kind="ExternalInput").ap()
    y_out = nc.dram_tensor("y", [PER_CORE, D], F32, kind="ExternalOutput").ap()

    wq_s = nc.dram_tensor("wq_s", [8, 128, 1024], BF16, kind="Internal").ap()
    wdn_s = nc.dram_tensor("wdn_s", [8, 128, 256], BF16, kind="Internal").ap()
    wup_s = nc.dram_tensor("wup_s", [6, 128, 1024], BF16, kind="Internal").ap()
    wo_s = nc.dram_tensor("wo_s", [8, 128, 1024], BF16, kind="Internal").ap()
    wg_s = nc.dram_tensor("wg_s", [16, 128, 1536], BF16, kind="Internal").ap()
    wu_s = nc.dram_tensor("wu_s", [16, 128, 1536], BF16, kind="Internal").ap()
    wd_s = nc.dram_tensor("wd_s", [24, 128, 1024], BF16, kind="Internal").ap()

    bld = Builder()

    with ExitStack() as st:
        sb = lambda name, shape, dt: st.enter_context(nc.sbuf_tensor(name, shape, dt))
        c_identb = sb("c_identb", [128, 128], BF16)
        c_ident3 = sb("c_ident3", [128, 128], F32)
        c_credc = sb("c_credc", [128, 256], BF16)
        c_idstk = sb("c_idstk", [128, 64], BF16)
        c_selc = sb("c_selc", [128, 64, 128], BF16)
        c_bq = sb("c_bq", [128, 8], F32)
        c_bdn = sb("c_bdn", [128, 2], F32)
        c_bup = sb("c_bup", [128, 24], F32)
        c_bout = sb("c_bout", [128, D], F32)
        c_eps = sb("c_eps", [128, 1], F32)
        xres = sb("xres", [128, SUB, D], F32)
        ybuf = sb("ybuf", [128, SUB, D], F32)
        xmbuf = sb("xmbuf", [128, 4, D], F32)
        stt = sb("stt", [128, 32], F32)
        xln = sb("xln", [128, SUB, D], BF16)
        xlnT = sb("xlnT", [128, 8, NT], BF16)
        qfm = sb("qfm", [128, 8, NT], BF16)
        lat = sb("lat", [128, 2, NT], BF16)
        kdup = sb("kdup", [128, 16, NT], BF16)
        vcmp = sb("vcmp", [128, 8, NT], BF16)
        scev = sb("scev", [128, 2, NT], F32)
        smax = sb("smax", [128, SUB, 16], F32)
        etm = sb("etm", [128, SUB, 256], F32)
        atts = sb("atts", [128, 2048], BF16)
        ebc = sb("ebc", [128, 4, NT], BF16)
        pbf = sb("pbf", [128, 8, NT], BF16)
        aoT = sb("aoT", [128, 8, NT], BF16)
        hbuf = sb("hbuf", [128, 24, NT], BF16)
        wring = sb("wring", [128, RING, SLOTW], BF16)
        ps = st.enter_context(nc.psum_tensor("ps", [128, 8, 512], F32))

        sems = {}
        for s in Builder.SEMS:
            sems[s] = st.enter_context(nc.semaphore(s))
        blk = st.enter_context(nc.Block())

        psap = ps.ap()
        psbf = psap.rearrange("p b f -> p (b f)").bitcast(BF16).rearrange(
            "p (b f) -> p b f", b=8)

        hflat_b = hbuf.ap().rearrange("p a b -> p (a b)")
        cin_i32 = hflat_b.bitcast(I32)
        cin_f32 = hflat_b.bitcast(F32)
        aflat_b = aoT.ap().rearrange("p a b -> p (a b)")

        E = bld.emit

        for src, dst in [(cb_identb, c_identb), (cb_ident3, c_ident3),
                         (cb_credc, c_credc), (cb_idstk, c_idstk),
                         (cb_selc, c_selc), (cb_eps, c_eps),
                         (cb_bq, c_bq), (cb_bdn, c_bdn), (cb_bup, c_bup),
                         (cb_bout, c_bout)]:
            E("GP", lambda g, s=src, d=dst: g.dma_start(d.ap(), s),
              inc="X", amt=16)

        xin_ms = [None] * (TILES + 1)

        def emit_xin(t, extra_waits=()):
            dst = (xres if t % 2 == 0 else ybuf).ap()
            srcv = x_in[t * NT:(t + 1) * NT, :].rearrange(
                "(s p) d -> p s d", p=128)
            xin_ms[t] = E("GP", lambda g, s_=srcv, d=dst: g.dma_start(d, s_),
                          waits=extra_waits, inc="X", amt=16)

        emit_xin(0)

        tile_chunks = ([("wq", c, 1024) for c in range(8)]
                       + [("wdn", c, 256) for c in range(8)]
                       + [("wup", i, 1024) for i in range(6)]
                       + [("wo", c, 1024) for c in range(8)]
                       + [("wg", i, 1536) for i in range(8)]
                       + [("wu", i, 1536) for i in range(8)]
                       + [("wd", kk, 1024) for kk in range(24)])
        NCH = len(tile_chunks)
        alt_aps = ([("kdup", kdup, k * 1536) for k in range(5)]
                   + [("vcmp", vcmp, k * 1536) for k in range(2)]
                   + [("qfm", qfm, k * 1536) for k in range(2)]
                   + [("pbf", pbf, k * 1536) for k in range(2)]
                   + [("etm", etm, 0), ("aoT", aoT, 0), ("aoT", aoT, 1536),
                      ("scev", scev, 0), ("atts", atts, 0)])
        assert len(alt_aps) == 16

        def alt_ap(k):
            nm, tens, off = alt_aps[k]
            fl = tens.ap().rearrange("p a b -> p (a b)") if tens.ap().ndim == 3                 else tens.ap()
            if fl.dtype != BF16:
                fl = fl.bitcast(BF16)
            return fl[:, off:off + 1536]

        def alt_w(k, t):
            return ("AW%d" % k, 16 * (t + 1))
        scr = {"wq": wq_s, "wdn": wdn_s, "wup": wup_s, "wo": wo_s,
               "wg": wg_s, "wu": wu_s, "wd": wd_s}

        chunk_free_val = []

        def free_val(gi):
            t_, li = divmod(gi, NCH)
            base_f = 30 * t_
            if li < 8:       return base_f + 1
            if li < 16:      return base_f + 2
            if li < 22:      return base_f + 3
            if li < 30:      return base_f + 4
            if li < 38:      return base_f + 5
            if li < 46:      return base_f + 6
            return base_f + 7 + (li - 46)

        def w_ms(gi):
            return ("W%d" % (gi % RING), 16 * (gi // RING + 1))

        def chunk_ap(gi, width):
            return wring.ap()[:, gi % RING, 0:width]

        STOP_CHUNKS = {"ln1": 0, "q": 8, "lat": 16, "kv": 22, "scores": 22,
                       "soft": 22, "attv": 22, "wout": 30, "ln2": 30,
                       "scores_noevac": 22, "gate": 38, "up": 46, "dn": 70}

        def _ring_load(gi):
            tag, idx, width = tile_chunks[gi % NCH]
            slot = gi % RING
            waits = [scratch_ready[(tag, idx)]]
            if gi >= RING:
                waits.append(("F", free_val(gi - RING)))
            E("SP", lambda e, d=wring.ap()[:, slot, 0:width],
              s=scr[tag][idx]: e.dma_start(d, s),
              waits=waits, inc="W%d" % slot, amt=16)

        def emit_tile_loads(t, alt_free_ms=None):
            base = t * NCH
            upto = STOP_CHUNKS.get(stop_after, NCH) if stop_after else NCH
            start_li = len(_t0_early) if t == 0 else 0
            for li in range(start_li, upto):
                _ring_load(base + li)

        def emit_alt_loads(ks, free_ms):
            for k in ks:
                tag = "wg" if k < 8 else "wu"
                idx = 8 + (k % 8)
                E("GP", lambda e, d=alt_ap(k), s=scr[tag][idx]:
                  e.dma_start(d, s),
                  waits=[scratch_ready[(tag, idx)], free_ms],
                  inc="AW%d" % k, amt=16)


        conv = []
        for c in range(8):
            conv.append(("wq", c, wq_r[c * 128:(c + 1) * 128, :], wq_s[c], 1024, F32))
        for c in range(8):
            conv.append(("wdn", c, wdn_r[c * 128:(c + 1) * 128, :], wdn_s[c], 256, F32))
        for i in range(6):
            k2, mh = i // 3, i % 3
            conv.append(("wup", i,
                         wup_r[k2 * 128:(k2 + 1) * 128, mh * 1024:(mh + 1) * 1024],
                         wup_s[i], 1024, F32))
        for c in range(8):
            conv.append(("wo", c, wo_r[c * 128:(c + 1) * 128, :], wo_s[c], 1024, F32))
        for i in range(16):
            hh, c = i // 8, i % 8
            conv.append(("wg", i,
                         wg_r[c * 128:(c + 1) * 128, hh * 1536:(hh + 1) * 1536],
                         wg_s[i], 1536, I32))
        for i in range(16):
            hh, c = i // 8, i % 8
            conv.append(("wu", i,
                         wu_r[c * 128:(c + 1) * 128, hh * 1536:(hh + 1) * 1536],
                         wu_s[i], 1536, I32))
        for kk in range(24):
            conv.append(("wd", kk, wd_r[kk * 128:(kk + 1) * 128, :], wd_s[kk], 1024, I32))

        scratch_ready = {}
        cast_ms = [None] * 4
        outdma_ms = [None, None]
        _t0_early = []
        for i, (tag, idx, src, dst, width, sdt) in enumerate(conv):
            lpar = i % 4
            opar = i % 2
            cin = (cin_i32 if sdt == I32 else cin_f32)
            cin_v = cin[:, lpar * 1536:lpar * 1536 + width]
            cout_v = aflat_b[:, opar * 1536:opar * 1536 + width]
            ld = E("SP", lambda g, s=src, d=cin_v: g.dma_start(d, s),
                   waits=[cast_ms[lpar]], inc="C%d" % lpar, amt=16)
            if i >= 30:
                cast = E("GP", lambda g, d=cout_v, s=cin_v: g.tensor_copy(d, s),
                         waits=[ld, outdma_ms[opar]], inc="G")
            elif i % 2 == 0:
                cast = E("DVE", lambda v, d=cout_v, s=cin_v: v.tensor_copy(d, s),
                         waits=[ld, outdma_ms[opar]], inc="V")
            else:
                cast = E("ACT", lambda a, d=cout_v, s=cin_v: a.copy(d, s),
                         waits=[ld, outdma_ms[opar]], inc="A")
            cast_ms[lpar] = cast
            od = E("GP", lambda g, s=cout_v, d=dst: g.dma_start(d, s),
                   waits=[cast], inc="O%d" % opar, amt=16)
            outdma_ms[opar] = od
            scratch_ready[(tag, idx)] = od
            if i == 29 and not stop_after and INTERLEAVE_T0:
                for li in range(30):
                    _ring_load(li)
                    _t0_early.append(li)
        conv_end2 = [("O0", bld.cnt["O0"]), ("O1", bld.cnt["O1"])]

        bank_w = [None] * 8
        bank_r = [None] * 8

        def ln_phase(specs, fin_waits=()):
            n = len(specs)
            rsm = [None] * n; mn = [None] * n; sub = [None] * n
            sq = [None] * n; var = [None] * n; sd = [None] * n; rs = [None] * n
            fins = [None] * n
            for i, (s, src_ap, ok_, sm, sp, res_ap) in enumerate(specs):
                so = s * 8
                if sp is None:
                    rsm[i] = E("DVE", lambda v, o=stt.ap()[:, so:so + 1],
                               i_=src_ap:
                               v.reduce_sum(o, i_, AX.X), waits=list(sm),
                               inc="V")
                else:
                    rsm[i] = sp
            for i, (s, src_ap, ok_, sm, sp, res_ap) in enumerate(specs):
                so = s * 8
                sub[i] = E("DVE", lambda v, o=xmbuf.ap()[:, s, :], i_=src_ap,
                           sb2=stt.ap()[:, so:so + 1].broadcast_to([128, D]):
                           v.scalar_tensor_tensor(o, sb2, -1.0 / D, i_,
                                                  ALU.mult, ALU.add),
                           waits=[rsm[i]] + list(sm), inc="V")
            for i, (s, src_ap, ok_, sm, sp, res_ap) in enumerate(specs):
                so = s * 8
                sq[i] = E("ACT", lambda a,
                          o=ebc.ap().rearrange("p a b -> p (a b)")[:, 0:D],
                          i_=xmbuf.ap()[:, s, :],
                          ac=stt.ap()[:, so + 2:so + 3]:
                          a.activation(o, i_, AF.Square, accum_out=ac),
                          waits=[sub[i]], inc="A")
            for i, (s, src_ap, ok_, sm, sp, res_ap) in enumerate(specs):
                so = s * 8
                sd[i] = E("ACT", lambda a, o=stt.ap()[:, so + 4:so + 5],
                          i_=stt.ap()[:, so + 2:so + 3]:
                          a.activation(o, i_, AF.Ln, scale=1.0 / D,
                                       bias=c_eps.ap()),
                          waits=[sq[i]], inc="A")
            for i, (s, src_ap, ok_, sm, sp, res_ap) in enumerate(specs):
                so = s * 8
                rs[i] = E("ACT", lambda a, o=stt.ap()[:, so + 5:so + 6],
                          i_=stt.ap()[:, so + 4:so + 5]:
                          a.activation(o, i_, AF.Exp, scale=-0.5),
                          waits=[sd[i]], inc="A")
            for i, (s, src_ap, ok_, sm, sp, res_ap) in enumerate(specs):
                so = s * 8
                if ok_ == "xln":
                    fins[i] = E("DVE", lambda v, o=xln.ap()[:, s, :],
                                i_=xmbuf.ap()[:, s, :],
                                sc=stt.ap()[:, so + 5:so + 6]:
                                v.tensor_scalar_mul(o, i_, sc),
                                waits=[rs[i]] + list(fin_waits), inc="V")
                elif ok_ == "resid":
                    fins[i] = E("DVE", lambda v, o=res_ap, i_=xmbuf.ap()[:, s, :],
                                i1=res_ap, sc=stt.ap()[:, so + 5:so + 6]:
                                v.scalar_tensor_tensor(o, i_, sc, i1,
                                                       ALU.mult, ALU.add),
                                waits=[rs[i]], inc="V")
                else:
                    fins[i] = E("DVE", lambda v, o=res_ap, i_=xmbuf.ap()[:, s, :],
                                i1=res_ap, sc=stt.ap()[:, so + 5:so + 6],
                                ac=stt.ap()[:, so:so + 1]:
                                v.scalar_tensor_tensor(o, i_, sc, i1,
                                                       ALU.mult, ALU.add,
                                                       accum_out=ac),
                                waits=[rs[i]], inc="V")
            return fins, sub

        def emit_transposes(ln_ms):
            evs = []
            tms = []
            for s in range(SUB):
                w = [ln_ms[s], bank_r[s]]
                tm = None
                for c in range(8):
                    tm = E("PE", lambda e, o=psbf[:, s, c * 128:(c + 1) * 128],
                           i=xln.ap()[:, s, c * 128:(c + 1) * 128]:
                           e.transpose(o, i, c_identb.ap()),
                           waits=(w if c == 0 else []),
                           inc=("P" if c == 7 else None))
                bank_w[s] = tm
                ev = E("DVE", lambda v,
                       o=xlnT.ap()[:, :, s * 128:(s + 1) * 128],
                       i=psbf[:, s, :].rearrange("p (c q) -> p c q", c=8):
                       v.tensor_copy(o, i), waits=[tm], inc="V")
                bank_r[s] = ev
                evs.append(ev)
                tms.append(tm)
            return evs[-1], tms[-1]

        prev_out_ms = [None]
        for t in range(n_tiles):
            first_tile = (t == 0)
            base = t * NCH
            res_b, scr_b = (xres, ybuf) if t % 2 == 0 else (ybuf, xres)
            if t + 1 < n_tiles:
                emit_xin(t + 1, extra_waits=[prev_out_ms[0]])

            def gq(c): return base + c
            def gdn(c): return base + 8 + c
            def gup(i): return base + 16 + i
            def gwo(c): return base + 22 + c
            def gwg(i): return base + 30 + i
            def gwu(i): return base + 38 + i
            def gwd(kk): return base + 46 + kk

            if t == 0:
                ln1, _ = ln_phase([(s, res_b.ap()[:, s, :], "xln",
                                    [xin_ms[0]], None, None)
                                   for s in range(SUB)])
                xlnT_ready, _ = emit_transposes(ln1)
            else:
                xlnT_ready = pending_xlnT
            if taps and t == 0:
                d1 = nc.dram_tensor("d_xlnT1", [128, 8, NT], BF16,
                                    kind="ExternalOutput").ap()
                E("GP", lambda g, d=d1: g.dma_start(d, xlnT.ap()),
                  waits=[xlnT_ready], inc="X", amt=16)
                d2 = nc.dram_tensor("d_xln1", [128, SUB, D], BF16,
                                    kind="ExternalOutput").ap()
                E("GP", lambda g, d=d2: g.dma_start(d, xln.ap()),
                  waits=[ln1[3]], inc="X", amt=16)
            if stop_after == "ln1":
                for nm2, buf2 in [("d_xres", xres), ("d_stt", stt),
                                  ("d_xm", xmbuf)]:
                    shp2 = list(buf2.ap().shape)
                    dr2 = nc.dram_tensor(nm2, shp2, buf2.ap().dtype,
                                         kind="ExternalOutput").ap()
                    E("GP", lambda g, d=dr2, s=buf2: g.dma_start(d, s.ap()),
                      waits=[("P", bld.cnt["P"]), ("A", bld.cnt["A"]),
                             ("V", bld.cnt["V"])], inc="X", amt=16)
                emit_tile_loads(t, None); continue

            qfm_ev = []
            for m in range(8):
                b = m % 4
                w = [xlnT_ready, bank_r[b]]
                mm = None
                for k in range(8):
                    mm = E("PE", lambda e, o=psap[:, b, :],
                           l=chunk_ap(gq(k), 1024), m_=m, k_=k:
                           e.matmul(o, l[:, m_ * 128:(m_ + 1) * 128],
                                    xlnT.ap()[:, k_, :],
                                    start=(k_ == 0), stop=(k_ == 7)),
                           waits=(w + [w_ms(gq(k))] if k == 0
                                  else [w_ms(gq(k))]),
                           inc=(("F" if m == 7 else "P") if k == 7 else None))
                bank_w[b] = mm
                ev = E("ACT", lambda a, o=qfm.ap()[:, m, :], i=psap[:, b, :],
                       bi=c_bq.ap()[:, m:m + 1]:
                       a.activation(o, i, AF.Identity, bias=bi), waits=[mm], inc="A")
                bank_r[b] = ev
                qfm_ev.append(ev)
            chunk_free_val.extend([bld.cnt["F"]] * 8)
            if stop_after == "q":
                emit_tile_loads(t, None); continue

            lat_ev = []
            for m in range(2):
                b = 4 + m
                w = [xlnT_ready, bank_r[b]]
                mm = None
                for k in range(8):
                    mm = E("PE", lambda e, o=psap[:, b, :],
                           l=chunk_ap(gdn(k), 256), m_=m, k_=k:
                           e.matmul(o, l[:, m_ * 128:(m_ + 1) * 128],
                                    xlnT.ap()[:, k_, :],
                                    start=(k_ == 0), stop=(k_ == 7)),
                           waits=(w + [w_ms(gdn(k))] if k == 0
                                  else [w_ms(gdn(k))]),
                           inc=(("F" if m == 1 else "P") if k == 7 else None))
                bank_w[b] = mm
                ev = E("ACT", lambda a, o=lat.ap()[:, m, :], i=psap[:, b, :],
                       bi=c_bdn.ap()[:, m:m + 1]:
                       a.activation(o, i, AF.Identity, bias=bi), waits=[mm], inc="A")
                bank_r[b] = ev
                lat_ev.append(ev)
            chunk_free_val.extend([bld.cnt["F"]] * 8)
            if stop_after == "lat":
                emit_tile_loads(t, None); continue

            kv_ev = [None] * 24
            for g in range(24):
                b = g % 4
                w = [lat_ev[-1], bank_r[b]]
                mm = None
                for k2 in range(2):
                    mh = g // 8
                    mloc = g % 8
                    ci = gup(k2 * 3 + mh)
                    mm = E("PE", lambda e, o=psap[:, b, :],
                           l=chunk_ap(ci, 1024), ml=mloc, k_=k2:
                           e.matmul(o, l[:, ml * 128:(ml + 1) * 128],
                                    lat.ap()[:, k_, :],
                                    start=(k_ == 0), stop=(k_ == 1)),
                           waits=(w + [w_ms(ci)] if k2 == 0 else [w_ms(ci)]),
                           inc=(("F" if g == 23 else "P") if k2 == 1 else None))
                bank_w[b] = mm
                dst = kdup.ap()[:, g, :] if g < 16 else vcmp.ap()[:, g - 16, :]
                ev = E("ACT", lambda a, o=dst, i=psap[:, b, :],
                       bi=c_bup.ap()[:, g:g + 1]:
                       a.activation(o, i, AF.Identity, bias=bi), waits=[mm], inc="A")
                bank_r[b] = ev
                kv_ev[g] = ev
            chunk_free_val.extend([bld.cnt["F"]] * 6)
            if stop_after == "kv":
                emit_tile_loads(t, None); continue

            sred_ms = [None] * 128
            idx = 0
            pair_i = 0
            for c in range(8):
                for gp in range(8):
                    g = 2 * gp
                    w = [qfm_ev[c], kv_ev[g + 1]]
                    if idx >= 8:
                        w.append(sred_ms[idx - 7])
                    peng = "DVE"
                    sl = idx % 8
                    pm = E(peng, lambda v, o=pbf.ap()[:, sl:sl + 2, :],
                           a=qfm.ap()[:, c:c + 1, :].broadcast_to([128, 2, NT]),
                           b2=kdup.ap()[:, g:g + 2, :]:
                           v.tensor_tensor(o, a, b2, ALU.mult),
                           waits=w, inc=("V" if peng == "DVE" else "G"))
                    pair_i += 1
                    for q2 in range(2):
                        bank = 6 + c // 4
                        r1 = ((2 * c) % 8) * 16 + g + q2
                        first = (c % 4 == 0 and g + q2 == 0)
                        last = (c % 4 == 3 and g + q2 == 15)
                        w2 = [pm]
                        if first:
                            w2.append(bank_r[bank])
                        mm = E("PE", lambda e, o=psap[:, bank, :],
                               l=c_credc.ap()[:, 127 - r1:255 - r1],
                               pi=idx % 8, f=first, la=last:
                               e.matmul(o, l, pbf.ap()[:, pi, :],
                                        start=f, stop=la),
                               waits=(w2 if q2 == 0 else []), inc="P")
                        sred_ms[idx] = mm
                        if last:
                            bank_w[bank] = mm
                        idx += 1

            if stop_after == "scores_noevac":
                emit_tile_loads(t, None); continue
            sev = []
            for c2 in range(2):
                ev = E("ACT", lambda a, o=scev.ap()[:, c2, :],
                       i=psap[:, 6 + c2, :]: a.copy(o, i),
                       waits=[bank_w[6 + c2]], inc="A")
                bank_r[6 + c2] = ev
                sev.append(ev)
            strn_last = None
            for s in range(SUB):
                b = 4 + s // 2
                w = [sev[-1], bank_r[4], bank_r[5]]
                for c2 in range(2):
                    strn_last = E("PE", lambda e,
                                  o=psap[:, b, (s % 2) * 256 + c2 * 128:
                                         (s % 2) * 256 + (c2 + 1) * 128],
                                  i=scev.ap()[:, c2, s * 128:(s + 1) * 128]:
                                  e.transpose(o, i, c_ident3.ap()),
                                  waits=(w if (s == 0 and c2 == 0) else []),
                                  inc=("P" if (s == 3 and c2 == 1) else None))
            bank_w[4] = strn_last
            bank_w[5] = strn_last

            tmv_all = psap[:, 4:6, :].rearrange("p b (x h g) -> p (b x h) g",
                                                x=2, h=16)
            smax_f = smax.ap().rearrange("p s h -> p (s h)")
            etm_f = etm.ap().rearrange("p s q -> p (s q)")
            mx = E("DVE", lambda v, o=smax_f, i=tmv_all:
                   v.reduce_max(o, i, AX.X),
                   waits=[strn_last], inc="V")
            sb_ = E("DVE", lambda v,
                    o=etm_f.rearrange("p (a g) -> p a g", g=16),
                    i=tmv_all,
                    m=smax_f.unsqueeze(2).broadcast_to([128, 64, 16]):
                    v.tensor_tensor(o, i, m, ALU.subtract),
                    waits=[mx], inc="V")
            ex = E("ACT", lambda a, o=etm_f, i=etm_f:
                   a.activation(o, i, AF.Exp, scale=0.125),
                   waits=[sb_], inc="A")
            zs = E("DVE", lambda v, o=smax_f,
                   i=etm_f.rearrange("p (a g) -> p a g", g=16):
                   v.reduce_sum(o, i, AX.X), waits=[ex], inc="V")
            rc = E("DVE", lambda v, o=smax_f, i=smax_f: v.reciprocal(o, i),
                   waits=[zs], inc="V")
            nm = E("DVE", lambda v,
                   o=atts.ap()[:, 1024:2048].rearrange("p (a g) -> p a g", g=16),
                   i=etm_f.rearrange("p (a g) -> p a g", g=16),
                   m=smax_f.unsqueeze(2).broadcast_to([128, 64, 16]):
                   v.tensor_tensor(o, i, m, ALU.mult),
                   waits=[rc], inc="V")
            soft = [nm] * SUB
            bank_r[4] = nm
            bank_r[5] = nm

            atrn = None
            w0 = soft + [bank_r[6]]
            for c2 in range(2):
                for s in range(SUB):
                    atrn = E("PE", lambda e,
                             o=psbf[:, 6, c2 * 512 + s * 128:c2 * 512 + (s + 1) * 128],
                             i=atts.ap()[:, 1024 + s * 256 + c2 * 128:1024 + s * 256 + (c2 + 1) * 128]:
                             e.transpose(o, i, c_identb.ap()),
                             waits=(w0 if (c2 == 0 and s == 0) else []),
                             inc=("P" if (c2 == 1 and s == 3) else None))
            bank_w[6] = atrn
            attT_ev = []
            for c2 in range(2):
                ev = E("DVE", lambda v, o=atts.ap()[:, c2 * 512:(c2 + 1) * 512],
                       i=psbf[:, 6, c2 * 512:(c2 + 1) * 512]:
                       v.tensor_copy(o, i), waits=[atrn], inc="V")
                attT_ev.append(ev)
            bank_r[6] = attT_ev[-1]

            idred_ms = [None] * 128
            bc_ms = [None] * 8
            pp_ms = [None] * 4
            ao_ev = [None] * 8
            pidx0 = 0

            def emit_bcast(h, j):
                eb = j % 4
                si = (h % 8) * 8 + j
                w = [attT_ev[h // 8], bank_r[eb]]
                bc_ms[j] = E("PE", lambda e, o=psap[:, eb, :],
                             l=c_selc.ap()[:, si, :], ti=h // 8:
                             e.matmul(o, l, atts.ap()[:, ti * 512:(ti + 1) * 512],
                                      start=True, stop=True),
                             waits=w, inc="P")
                bank_w[eb] = bc_ms[j]

            def emit_prodpair(h, jp, pidx):
                b0 = (2 * jp) % 4
                w2 = [bc_ms[2 * jp + 1], kv_ev[16 + 2 * jp + 1]]
                if pidx >= 8:
                    w2.append(idred_ms[pidx - 7])
                else:
                    w2.append(sred_ms[120 + pidx + 1])
                sl = pidx % 8
                pp_ms[jp] = E("DVE", lambda v, o=pbf.ap()[:, sl:sl + 2, :],
                              a=psap[:, b0:b0 + 2, :],
                              b2=vcmp.ap()[:, 2 * jp:2 * jp + 2, :]:
                              v.tensor_tensor(o, a, b2, ALU.mult),
                              waits=w2, inc="V")
                bank_r[b0] = pp_ms[jp]
                bank_r[b0 + 1] = pp_ms[jp]

            def emit_idred(h, j, pidx, ob):
                w3 = [pp_ms[j // 2]]
                if j == 0:
                    w3.append(bank_r[ob])
                mm = E("PE", lambda e,
                       o=psap[(h % 2) * 64:(h % 2) * 64 + 64, ob, :],
                       pi=pidx % 8, f=(j == 0), la=(j == 7):
                       e.matmul(o, c_idstk.ap(), pbf.ap()[:, pi, :],
                                start=f, stop=la),
                       waits=w3, inc="P")
                idred_ms[pidx] = mm
                return mm

            st_bc = [[None] * 8, [None] * 8]
            st_pp = [[None] * 4, [None] * 4]

            def unitA(h, jp, pidx0):
                par = h % 2
                bc_ms[:] = st_bc[par]
                pp_ms[:] = st_pp[par]
                emit_bcast(h, 2 * jp)
                emit_bcast(h, 2 * jp + 1)
                emit_prodpair(h, jp, pidx0 + 2 * jp)
                st_bc[par][:] = bc_ms
                st_pp[par][:] = pp_ms

            def unitB(h, jp, pidx0, ob):
                par = h % 2
                pp_ms[:] = st_pp[par]
                last = emit_idred(h, 2 * jp, pidx0 + 2 * jp, ob)
                last = emit_idred(h, 2 * jp + 1, pidx0 + 2 * jp + 1, ob)
                return last

            for jp in range(4):
                unitA(0, jp, 0)
            for h in range(H):
                pair = h // 2
                ob = 7 - (pair % 2)
                last = None
                for jp in range(4):
                    last = unitB(h, jp, 8 * h, ob)
                    if h + 1 < H:
                        unitA(h + 1, jp, 8 * (h + 1))
                if h % 2 == 1:
                    bank_w[ob] = last
                    w4 = [bank_w[ob]]
                    if first_tile and pair == 0:
                        w4.extend(conv_end2)
                    ev = E("ACT", lambda a, o=aoT.ap()[:, pair, :],
                           i=psap[:, ob, :]: a.copy(o, i), waits=w4, inc="A")
                    bank_r[ob] = ev
                    ao_ev[pair] = ev

            attv_last = idred_ms[127]
            if stop_after is None:
                emit_alt_loads([k for k in range(16) if k not in (12, 13)],
                               attv_last)
            if stop_after == "attv":
                emit_tile_loads(t, None); continue
            wout_ev = [None] * SUB
            wout_mm = [None] * SUB
            attn_fin = [None] * SUB
            ln2 = [None] * SUB
            for pr in (0, 1):
                ss = (2 * pr, 2 * pr + 1)
                for s in ss:
                    sb0 = 4 + (s % 2) * 2
                    w = [bank_r[sb0], bank_r[sb0 + 1]] + ao_ev
                    mm = None
                    for c in range(8):
                        for half in range(2):
                            wk = [w_ms(gwo(c))] if half == 0 else []
                            mm = E("PE", lambda e, o=psap[:, sb0 + half, :],
                                   l=aoT.ap()[:, c, s * 128:(s + 1) * 128],
                                   wch=chunk_ap(gwo(c), 1024), hf=half, c_=c:
                                   e.matmul(o, l,
                                            wch[:, hf * 512:(hf + 1) * 512],
                                            start=(c_ == 0), stop=(c_ == 7)),
                                   waits=(w + wk if (c == 0 and half == 0)
                                          else wk),
                                   inc=(("F" if s == 3 else "P")
                                        if (c == 7 and half == 1) else None))
                    bank_w[sb0] = mm
                    bank_w[sb0 + 1] = mm
                    wout_mm[s] = mm
                for s in ss:
                    sb0 = 4 + (s % 2) * 2
                    ev = E("DVE", lambda v, o=xmbuf.ap()[:, s, :],
                           i=psap[:, sb0:sb0 + 2, :].rearrange(
                               "p b f -> p (b f)"),
                           ac=stt.ap()[:, s * 8:s * 8 + 1]:
                           v.scalar_tensor_tensor(o, i, 1.0, c_bout.ap(),
                                                  ALU.mult, ALU.add,
                                                  accum_out=ac),
                           waits=[wout_mm[s]], inc="V")
                    bank_r[sb0] = ev
                    bank_r[sb0 + 1] = ev
                    wout_ev[s] = ev
                af, _ = ln_phase(
                    [(s, xmbuf.ap()[:, s, :], "resid_accum", [wout_ev[s]],
                      wout_ev[s], res_b.ap()[:, s, :]) for s in ss])
                attn_fin[2 * pr:2 * pr + 2] = af
                l2, _ = ln_phase([(s, res_b.ap()[:, s, :], "xln",
                                   [attn_fin[s]], attn_fin[s], None)
                                  for s in ss])
                ln2[2 * pr:2 * pr + 2] = l2
            chunk_free_val.extend([bld.cnt["F"]] * 8)
            wout_last = wout_mm[3]
            if stop_after is None:
                emit_alt_loads([12, 13], wout_last)
            if stop_after == "wout":
                emit_tile_loads(t, None); continue

            xlnT_ready, t2_tm = emit_transposes(ln2)
            ln1_next = [None] * SUB
            if t + 1 < n_tiles:
                ln1_next, _ = ln_phase(
                    [(s, scr_b.ap()[:, s, :], "xln", [xin_ms[t + 1]],
                      None, None) for s in range(SUB)],
                    fin_waits=[("P", t2_tm[1])])
                pending_ln1 = ln1_next

            silu_ms = [None] * 24
            for hh in range(2):
                for mmi in range(12):
                    m = hh * 12 + mmi
                    b = 2 * (m % 2)
                    w = [xlnT_ready, bank_r[b]]
                    mm = None
                    for k in range(8):
                        if hh == 0:
                            lap = chunk_ap(gwg(k), 1536)
                            wk = [w_ms(gwg(k))]
                        else:
                            lap = alt_ap(k)
                            wk = [alt_w(k, t)]
                        mm = E("PE", lambda e, o=psap[:, b, :],
                               l=lap, ml=mmi, k_=k:
                               e.matmul(o, l[:, ml * 128:(ml + 1) * 128],
                                        xlnT.ap()[:, k_, :],
                                        start=(k_ == 0), stop=(k_ == 7)),
                               waits=(w + wk if k == 0 else wk),
                               inc=(("F" if (mmi == 11 and hh == 0) else "P")
                                    if k == 7 else None))
                    bank_w[b] = mm
                    w2 = [mm]
                    if first_tile and m == 0:
                        w2.extend(conv_end2)
                    ev = E("ACT", lambda a, o=hbuf.ap()[:, m, :], i=psap[:, b, :]:
                           a.activation(o, i, AF.Silu, scale=QSCALE),
                           waits=w2, inc="A")
                    bank_r[b] = ev
                    silu_ms[m] = ev
                if hh == 0:
                    chunk_free_val.extend([bld.cnt["F"]] * 8)

            stt_ms = [None] * 24
            for hh in range(2):
                for mmi in range(12):
                    m = hh * 12 + mmi
                    b = 2 * (m % 2) + 1
                    w = [xlnT_ready, bank_r[b]]
                    mm = None
                    for k in range(8):
                        if hh == 0:
                            lap = chunk_ap(gwu(k), 1536)
                            wk = [w_ms(gwu(k))]
                        else:
                            lap = alt_ap(8 + k)
                            wk = [alt_w(8 + k, t)]
                        mm = E("PE", lambda e, o=psap[:, b, :],
                               l=lap, ml=mmi, k_=k:
                               e.matmul(o, l[:, ml * 128:(ml + 1) * 128],
                                        xlnT.ap()[:, k_, :],
                                        start=(k_ == 0), stop=(k_ == 7)),
                               waits=(w + wk if k == 0 else wk),
                               inc=(("F" if (mmi == 11 and hh == 0) else "P")
                                    if k == 7 else None))
                    bank_w[b] = mm
                    ev = E("DVE", lambda v, o=hbuf.ap()[:, m, :], i=psap[:, b, :],
                           i1=hbuf.ap()[:, m, :]:
                           v.scalar_tensor_tensor(o, i, QSCALE, i1,
                                                  ALU.mult, ALU.mult),
                           waits=[mm, silu_ms[m]], inc="V")
                    bank_r[b] = ev
                    stt_ms[m] = ev
                if hh == 0:
                    chunk_free_val.extend([bld.cnt["F"]] * 8)

            for kk in range(24):
                for s in range(SUB):
                    for half in range(2):
                        b = s * 2 + half
                        w = [w_ms(gwd(kk)), stt_ms[kk]]
                        if kk == 0:
                            w.append(bank_r[b])
                        mm = E("PE", lambda e, o=psap[:, b, :],
                               l=hbuf.ap()[:, kk, s * 128:(s + 1) * 128],
                               wch=chunk_ap(gwd(kk), 1024), hf=half, k_=kk:
                               e.matmul(o, l, wch[:, hf * 512:(hf + 1) * 512],
                                        start=(k_ == 0), stop=(k_ == 23)),
                               waits=w,
                               inc=(("F" if (s == 3 and half == 1) else
                                     ("P" if kk == 23 else None))))
                        if kk == 23:
                            bank_w[b] = mm
                chunk_free_val.append(bld.cnt["F"])

            dn_ev = [None] * SUB
            for s in range(SUB):
                ev = E("DVE", lambda v, o=xmbuf.ap()[:, s, :],
                       i=psap[:, s * 2:s * 2 + 2, :].rearrange("p b f -> p (b f)"),
                       ac=stt.ap()[:, s * 8:s * 8 + 1]:
                       v.tensor_scalar(o, i, 1.0, 0.0, ALU.mult, ALU.add,
                                       accum_out=ac),
                       waits=[bank_w[s * 2], bank_w[s * 2 + 1],
                              ln1_next[s]], inc="V")
                bank_r[s * 2] = ev
                bank_r[s * 2 + 1] = ev
                dn_ev[s] = ev
            if t + 1 < n_tiles:
                pending_xlnT, _ = emit_transposes(ln1_next)
            dn_fin, _ = ln_phase(
                [(s, xmbuf.ap()[:, s, :], "resid", [dn_ev[s]],
                  dn_ev[s], res_b.ap()[:, s, :]) for s in range(SUB)])
            dst = y_out[t * NT:(t + 1) * NT, :].rearrange("(s p) d -> p s d", p=128)
            out_ms = E("GP", lambda g, d=dst, sb_=res_b: g.dma_start(d, sb_.ap()),
                       waits=dn_fin, inc="X", amt=16)
            prev_out_ms[0] = out_ms

            emit_tile_loads(t)

        if taps:
            tap_list = [
                ("d_xlnT", xlnT), ("d_qfm", qfm), ("d_lat", lat),
                ("d_kdup", kdup), ("d_vcmp", vcmp), ("d_scev", scev),
                ("d_atts", atts), ("d_aoT", aoT),
                ("d_hbuf", hbuf), ("d_ybuf", ybuf), ("d_etm", etm),
            ]
            for nm, dr_s, shp in [("d_wqs", wq_s, [8, 128, 1024]),
                                  ("d_wds", wd_s, [24, 128, 1024])]:
                drt = nc.dram_tensor(nm, shp, BF16, kind="ExternalOutput").ap()
                E("GP", lambda g, d=drt, s=dr_s: g.dma_start(d, s),
                  waits=[("X", bld.cnt["X"])], inc="X", amt=16)
            allw = [("P", bld.cnt["P"]), ("A", bld.cnt["A"]),
                    ("V", bld.cnt["V"]), ("X", bld.cnt["X"])]
            for nm, buf in tap_list:
                shp = list(buf.ap().shape)
                dt = buf.ap().dtype
                dr = nc.dram_tensor(nm, shp, dt, kind="ExternalOutput").ap()
                E("GP", lambda g, d=dr, s=buf: g.dma_start(d, s.ap()),
                  waits=allw, inc="X", amt=16)

        E("GP", lambda g: g.engine_nop(), waits=[("X", bld.cnt["X"])])

        def render(eng_name, deco):
            @deco
            def _(e):
                waited = {}
                for fn, waits, inc, amt, extra in bld.ops[eng_name]:
                    for (snm, val) in waits:
                        if waited.get(snm, -1) < val:
                            e.wait_ge(sems[snm], val)
                            waited[snm] = val
                    inst = fn(e)
                    if inc is not None:
                        inst = inst.then_inc(sems[inc], amt)
                    if extra is not None:
                        inst.then_inc(sems[extra[0]], extra[1])

        render("SP", blk.sync)
        render("GP", blk.gpsimd)
        render("PE", blk.tensor)
        render("ACT", blk.scalar)
        render("DVE", blk.vector)

    nc.finalize()
    return nc


_NC_CACHE = None


def _get_nc():
    global _NC_CACHE
    if _NC_CACHE is None:
        _NC_CACHE = build_program()
    return _NC_CACHE


def kernel(L0_attn_x, L0_attn_Wq, L0_attn_bq, L0_attn_Wdown, L0_attn_bdown,
           L0_attn_Wup, L0_attn_bup, L0_attn_Wout, L0_attn_bout,
           L0_ffn_w_gate, L0_ffn_w_up, L0_ffn_w_down):
    x = np.ascontiguousarray(np.asarray(L0_attn_x, dtype=np.float32).reshape(NTOK, D))
    wq = np.ascontiguousarray(np.asarray(L0_attn_Wq, dtype=np.float32))
    bq = np.asarray(L0_attn_bq, dtype=np.float32)
    wdn = np.ascontiguousarray(np.asarray(L0_attn_Wdown, dtype=np.float32))
    bdn = np.asarray(L0_attn_bdown, dtype=np.float32)
    wup = np.asarray(L0_attn_Wup, dtype=np.float32)
    bup = np.asarray(L0_attn_bup, dtype=np.float32)
    wo = np.ascontiguousarray(np.asarray(L0_attn_Wout, dtype=np.float32))
    bout = np.asarray(L0_attn_bout, dtype=np.float32)
    wg = np.asarray(L0_ffn_w_gate, dtype=np.int32)
    wu = np.asarray(L0_ffn_w_up, dtype=np.int32)
    wd = np.asarray(L0_ffn_w_down, dtype=np.int32)

    wup4 = wup.reshape(DL, H, 2, DH)
    kcols = wup4[:, :, 0, :]
    vcols = wup4[:, :, 1, :]
    wup_p = np.ascontiguousarray(np.concatenate(
        [np.repeat(kcols, 2, axis=1).reshape(DL, 2048),
         vcols.reshape(DL, 1024)], axis=1))
    bup4 = bup.reshape(H, 2, DH)
    bup_p = np.concatenate([np.repeat(bup4[:, 0, :], 2, axis=0).reshape(2048),
                            bup4[:, 1, :].reshape(1024)])
    bq_fm = np.ascontiguousarray(bq.reshape(8, 128).T)
    bdn_fm = np.ascontiguousarray(bdn.reshape(2, 128).T)
    bup_fm = np.ascontiguousarray(bup_p.reshape(24, 128).T)
    bout_bc = np.ascontiguousarray(np.broadcast_to(bout, (128, D)))

    wgT = np.ascontiguousarray(wg.T)
    wuT = np.ascontiguousarray(wu.T)
    wdT = np.ascontiguousarray(wd.T)

    identb = np.eye(128, dtype=bfdt)
    ident3 = np.eye(128, dtype=np.float32)
    credc = np.zeros((128, 256), dtype=bfdt)
    credc[0:64, 127] = 1.0
    credc[64:128, 143] = 1.0
    idstk = np.zeros((128, 64), dtype=bfdt)
    idstk[np.arange(128), np.arange(128) % 64] = 1.0
    selc = np.zeros((128, 64, 128), dtype=bfdt)
    for si in range(64):
        selc[2 * si, si, 0:64] = 1.0
        selc[2 * si + 1, si, 64:128] = 1.0

    shards = np.split(x, NCORES, axis=0)
    common = {
        "wq_r": wq, "wdn_r": wdn, "wup_r": wup_p, "wo_r": wo,
        "wg_r": wgT, "wu_r": wuT, "wd_r": wdT,
        "cb_identb": identb, "cb_ident3": ident3, "cb_credc": credc,
        "cb_idstk": idstk, "cb_selc": selc, "cb_bq": bq_fm, "cb_bdn": bdn_fm,
        "cb_bup": bup_fm, "cb_bout": bout_bc,
        "cb_eps": np.full((128, 1), EPS, np.float32),
    }
    in_maps = [dict(common, x=np.ascontiguousarray(s)) for s in shards]
    nc = _get_nc()
    res = run_bass_kernel_spmd(nc, in_maps, core_ids=list(range(NCORES)))
    out = np.concatenate([r["y"] for r in res.results], axis=0)
    return out.reshape(4, 8192, D)


# version 25
# speedup vs baseline: 1.1578x; 1.0028x over previous
import numpy as np
import ml_dtypes
from contextlib import ExitStack

import concourse.bass as bass
import concourse.mybir as mybir
from concourse.bass_utils import run_bass_kernel_spmd

F32 = mybir.dt.float32
BF16 = mybir.dt.bfloat16
I32 = mybir.dt.int32
AF = mybir.ActivationFunctionType
ALU = mybir.AluOpType
AX = mybir.AxisListType
bfdt = ml_dtypes.bfloat16

NCORES = 8
D = 1024
H = 16
DH = 64
DL = 256
HID = 3072
NTOK = 4 * 8192
PER_CORE = NTOK // NCORES
NT = 512
SUB = NT // 128
TILES = PER_CORE // NT
EPS = 1e-5
QSCALE = 1.0 / 127.0
RING = 11
SLOTW = 1536

ENGINES = ("SP", "GP", "PE", "ACT", "DVE")


class Builder:

    SEMS = (["X", "P", "A", "V", "F", "G", "C0", "C1", "C2", "C3",
             "O0", "O1"]
            + ["W%d" % i for i in range(RING)]
            + ["AW%d" % i for i in range(16)])

    def __init__(self):
        self.ops = {e: [] for e in ENGINES}
        self.cnt = {s: 0 for s in self.SEMS}

    def emit(self, eng, fn, waits=(), inc=None, amt=1, extra_inc=None):
        ws = [w for w in waits if w is not None]
        self.ops[eng].append((fn, ws, inc, amt, extra_inc))
        ms = None
        if inc is not None:
            self.cnt[inc] += amt
            ms = (inc, self.cnt[inc])
        if extra_inc is not None:
            s2, a2 = extra_inc
            self.cnt[s2] += a2
        return ms


def build_program(n_tiles=TILES, taps=False, stop_after=None):
    nc = bass.Bass("TRN2", target_bir_lowering=False)

    x_in = nc.dram_tensor("x", [PER_CORE, D], F32, kind="ExternalInput").ap()
    wq_r = nc.dram_tensor("wq_r", [D, D], F32, kind="ExternalInput").ap()
    wdn_r = nc.dram_tensor("wdn_r", [D, DL], F32, kind="ExternalInput").ap()
    wup_r = nc.dram_tensor("wup_r", [DL, 3072], F32, kind="ExternalInput").ap()
    wo_r = nc.dram_tensor("wo_r", [D, D], F32, kind="ExternalInput").ap()
    wg_r = nc.dram_tensor("wg_r", [D, HID], I32, kind="ExternalInput").ap()
    wu_r = nc.dram_tensor("wu_r", [D, HID], I32, kind="ExternalInput").ap()
    wd_r = nc.dram_tensor("wd_r", [HID, D], I32, kind="ExternalInput").ap()
    cb_identb = nc.dram_tensor("cb_identb", [128, 128], BF16, kind="ExternalInput").ap()
    cb_ident3 = nc.dram_tensor("cb_ident3", [128, 128], F32, kind="ExternalInput").ap()
    cb_credc = nc.dram_tensor("cb_credc", [128, 256], BF16, kind="ExternalInput").ap()
    cb_idstk = nc.dram_tensor("cb_idstk", [128, 64], BF16, kind="ExternalInput").ap()
    cb_selc = nc.dram_tensor("cb_selc", [128, 64, 128], BF16, kind="ExternalInput").ap()
    cb_bq = nc.dram_tensor("cb_bq", [128, 8], F32, kind="ExternalInput").ap()
    cb_bdn = nc.dram_tensor("cb_bdn", [128, 2], F32, kind="ExternalInput").ap()
    cb_bup = nc.dram_tensor("cb_bup", [128, 24], F32, kind="ExternalInput").ap()
    cb_bout = nc.dram_tensor("cb_bout", [128, D], F32, kind="ExternalInput").ap()
    cb_eps = nc.dram_tensor("cb_eps", [128, 1], F32, kind="ExternalInput").ap()
    y_out = nc.dram_tensor("y", [PER_CORE, D], F32, kind="ExternalOutput").ap()

    wq_s = nc.dram_tensor("wq_s", [8, 128, 1024], BF16, kind="Internal").ap()
    wdn_s = nc.dram_tensor("wdn_s", [8, 128, 256], BF16, kind="Internal").ap()
    wup_s = nc.dram_tensor("wup_s", [6, 128, 1024], BF16, kind="Internal").ap()
    wo_s = nc.dram_tensor("wo_s", [8, 128, 1024], BF16, kind="Internal").ap()
    wg_s = nc.dram_tensor("wg_s", [16, 128, 1536], BF16, kind="Internal").ap()
    wu_s = nc.dram_tensor("wu_s", [16, 128, 1536], BF16, kind="Internal").ap()
    wd_s = nc.dram_tensor("wd_s", [24, 128, 1024], BF16, kind="Internal").ap()

    bld = Builder()

    with ExitStack() as st:
        sb = lambda name, shape, dt: st.enter_context(nc.sbuf_tensor(name, shape, dt))
        c_identb = sb("c_identb", [128, 128], BF16)
        c_ident3 = sb("c_ident3", [128, 128], F32)
        c_credc = sb("c_credc", [128, 256], BF16)
        c_idstk = sb("c_idstk", [128, 64], BF16)
        c_selc = sb("c_selc", [128, 64, 128], BF16)
        c_bq = sb("c_bq", [128, 8], F32)
        c_bdn = sb("c_bdn", [128, 2], F32)
        c_bup = sb("c_bup", [128, 24], F32)
        c_bout = sb("c_bout", [128, D], F32)
        c_eps = sb("c_eps", [128, 1], F32)
        xres = sb("xres", [128, SUB, D], F32)
        ybuf = sb("ybuf", [128, SUB, D], F32)
        xmbuf = sb("xmbuf", [128, 4, D], F32)
        stt = sb("stt", [128, 32], F32)
        xln = sb("xln", [128, SUB, D], BF16)
        xlnT = sb("xlnT", [128, 8, NT], BF16)
        qfm = sb("qfm", [128, 8, NT], BF16)
        lat = sb("lat", [128, 2, NT], BF16)
        kdup = sb("kdup", [128, 16, NT], BF16)
        vcmp = sb("vcmp", [128, 8, NT], BF16)
        scev = sb("scev", [128, 2, NT], F32)
        smax = sb("smax", [128, SUB, 16], F32)
        etm = sb("etm", [128, SUB, 256], F32)
        atts = sb("atts", [128, 2048], BF16)
        ebc = sb("ebc", [128, D], BF16)
        pbf = sb("pbf", [128, 8, NT], BF16)
        aoT = sb("aoT", [128, 8, NT], BF16)
        hbuf = sb("hbuf", [128, 24, NT], BF16)
        wring = sb("wring", [128, RING, SLOTW], BF16)
        ps = st.enter_context(nc.psum_tensor("ps", [128, 8, 512], F32))

        sems = {}
        for s in Builder.SEMS:
            sems[s] = st.enter_context(nc.semaphore(s))
        blk = st.enter_context(nc.Block())

        psap = ps.ap()
        psbf = psap.rearrange("p b f -> p (b f)").bitcast(BF16).rearrange(
            "p (b f) -> p b f", b=8)

        hflat_b = hbuf.ap().rearrange("p a b -> p (a b)")
        cin_i32 = hflat_b.bitcast(I32)
        cin_f32 = hflat_b.bitcast(F32)
        aflat_b = aoT.ap().rearrange("p a b -> p (a b)")

        E = bld.emit

        for src, dst in [(cb_identb, c_identb), (cb_ident3, c_ident3),
                         (cb_credc, c_credc), (cb_idstk, c_idstk),
                         (cb_selc, c_selc), (cb_eps, c_eps),
                         (cb_bq, c_bq), (cb_bdn, c_bdn), (cb_bup, c_bup),
                         (cb_bout, c_bout)]:
            E("GP", lambda g, s=src, d=dst: g.dma_start(d.ap(), s),
              inc="X", amt=16)

        xin_ms = [None] * (TILES + 1)

        def emit_xin(t, extra_waits=()):
            dst = (xres if t % 2 == 0 else ybuf).ap()
            srcv = x_in[t * NT:(t + 1) * NT, :].rearrange(
                "(s p) d -> p s d", p=128)
            xin_ms[t] = E("GP", lambda g, s_=srcv, d=dst: g.dma_start(d, s_),
                          waits=extra_waits, inc="X", amt=16)

        emit_xin(0)

        tile_chunks = ([("wq", c, 1024) for c in range(8)]
                       + [("wdn", c, 256) for c in range(8)]
                       + [("wup", i, 1024) for i in range(6)]
                       + [("wo", c, 1024) for c in range(8)]
                       + [("wg", i, 1536) for i in range(8)]
                       + [("wu", i, 1536) for i in range(8)]
                       + [("wd", kk, 1024) for kk in range(24)])
        NCH = len(tile_chunks)
        alt_aps = ([("kdup", kdup, k * 1536) for k in range(5)]
                   + [("vcmp", vcmp, k * 1536) for k in range(2)]
                   + [("qfm", qfm, k * 1536) for k in range(2)]
                   + [("pbf", pbf, k * 1536) for k in range(2)]
                   + [("etm", etm, 0), ("aoT", aoT, 0), ("aoT", aoT, 1536),
                      ("scev", scev, 0), ("atts", atts, 0)])
        assert len(alt_aps) == 16

        def alt_ap(k):
            nm, tens, off = alt_aps[k]
            fl = tens.ap().rearrange("p a b -> p (a b)") if tens.ap().ndim == 3                 else tens.ap()
            if fl.dtype != BF16:
                fl = fl.bitcast(BF16)
            return fl[:, off:off + 1536]

        def alt_w(k, t):
            return ("AW%d" % k, 16 * (t + 1))
        scr = {"wq": wq_s, "wdn": wdn_s, "wup": wup_s, "wo": wo_s,
               "wg": wg_s, "wu": wu_s, "wd": wd_s}

        chunk_free_val = []

        def w_ms(gi):
            return ("W%d" % (gi % RING), 16 * (gi // RING + 1))

        def chunk_ap(gi, width):
            return wring.ap()[:, gi % RING, 0:width]

        STOP_CHUNKS = {"ln1": 0, "q": 8, "lat": 16, "kv": 22, "scores": 22,
                       "soft": 22, "attv": 22, "wout": 30, "ln2": 30,
                       "scores_noevac": 22, "gate": 38, "up": 46, "dn": 70}

        def _ring_load(gi):
            tag, idx, width = tile_chunks[gi % NCH]
            slot = gi % RING
            waits = [scratch_ready[(tag, idx)]]
            if gi >= RING:
                waits.append(("F", chunk_free_val[gi - RING]))
            E("SP", lambda e, d=wring.ap()[:, slot, 0:width],
              s=scr[tag][idx]: e.dma_start(d, s),
              waits=waits, inc="W%d" % slot, amt=16)

        def emit_tile_loads(t, alt_free_ms=None):
            base = t * NCH
            upto = STOP_CHUNKS.get(stop_after, NCH) if stop_after else NCH
            start_li = RING if (t == 0 and _t0_pre) else 0
            for li in range(start_li, upto):
                _ring_load(base + li)

        def emit_alt_loads(ks, free_ms):
            for k in ks:
                tag = "wg" if k < 8 else "wu"
                idx = 8 + (k % 8)
                E("GP", lambda e, d=alt_ap(k), s=scr[tag][idx]:
                  e.dma_start(d, s),
                  waits=[scratch_ready[(tag, idx)], free_ms],
                  inc="AW%d" % k, amt=16)


        conv = []
        for c in range(8):
            conv.append(("wq", c, wq_r[c * 128:(c + 1) * 128, :], wq_s[c], 1024, F32))
        for c in range(8):
            conv.append(("wdn", c, wdn_r[c * 128:(c + 1) * 128, :], wdn_s[c], 256, F32))
        for i in range(6):
            k2, mh = i // 3, i % 3
            conv.append(("wup", i,
                         wup_r[k2 * 128:(k2 + 1) * 128, mh * 1024:(mh + 1) * 1024],
                         wup_s[i], 1024, F32))
        for c in range(8):
            conv.append(("wo", c, wo_r[c * 128:(c + 1) * 128, :], wo_s[c], 1024, F32))
        for i in range(16):
            hh, c = i // 8, i % 8
            conv.append(("wg", i,
                         wg_r[c * 128:(c + 1) * 128, hh * 1536:(hh + 1) * 1536],
                         wg_s[i], 1536, I32))
        for i in range(16):
            hh, c = i // 8, i % 8
            conv.append(("wu", i,
                         wu_r[c * 128:(c + 1) * 128, hh * 1536:(hh + 1) * 1536],
                         wu_s[i], 1536, I32))
        for kk in range(24):
            conv.append(("wd", kk, wd_r[kk * 128:(kk + 1) * 128, :], wd_s[kk], 1024, I32))

        scratch_ready = {}
        _t0_pre = []
        cast_ms = [None] * 4
        outdma_ms = [None, None]
        for i, (tag, idx, src, dst, width, sdt) in enumerate(conv):
            lpar = i % 4
            opar = i % 2
            cin = (cin_i32 if sdt == I32 else cin_f32)
            cin_v = cin[:, lpar * 1536:lpar * 1536 + width]
            cout_v = aflat_b[:, opar * 1536:opar * 1536 + width]
            ld = E("SP", lambda g, s=src, d=cin_v: g.dma_start(d, s),
                   waits=[cast_ms[lpar]], inc="C%d" % lpar, amt=16)
            if i % 2 == 0:
                cast = E("DVE", lambda v, d=cout_v, s=cin_v: v.tensor_copy(d, s),
                         waits=[ld, outdma_ms[opar]], inc="V")
            else:
                cast = E("ACT", lambda a, d=cout_v, s=cin_v: a.copy(d, s),
                         waits=[ld, outdma_ms[opar]], inc="A")
            cast_ms[lpar] = cast
            od = E("GP", lambda g, s=cout_v, d=dst: g.dma_start(d, s),
                   waits=[cast], inc="O%d" % opar, amt=16)
            outdma_ms[opar] = od
            scratch_ready[(tag, idx)] = od
            if i == 29 and not stop_after:
                _t0_pre.append(0)
                for li in range(RING):
                    _ring_load(li)
        conv_end2 = [("O0", bld.cnt["O0"]), ("O1", bld.cnt["O1"])]

        bank_w = [None] * 8
        bank_r = [None] * 8

        def ln_phase(specs, fin_waits=()):
            n = len(specs)
            rsm = [None] * n; mn = [None] * n; sub = [None] * n
            sq = [None] * n; var = [None] * n; sd = [None] * n; rs = [None] * n
            fins = [None] * n
            for i, (s, src_ap, ok_, sm, sp, res_ap) in enumerate(specs):
                so = s * 8
                if sp is None:
                    rsm[i] = E("DVE", lambda v, o=stt.ap()[:, so:so + 1],
                               i_=src_ap:
                               v.reduce_sum(o, i_, AX.X), waits=list(sm),
                               inc="V")
                else:
                    rsm[i] = sp
            for i, (s, src_ap, ok_, sm, sp, res_ap) in enumerate(specs):
                so = s * 8
                sub[i] = E("DVE", lambda v, o=xmbuf.ap()[:, s, :], i_=src_ap,
                           sb2=stt.ap()[:, so:so + 1].broadcast_to([128, D]):
                           v.scalar_tensor_tensor(o, sb2, -1.0 / D, i_,
                                                  ALU.mult, ALU.add),
                           waits=[rsm[i]] + list(sm), inc="V")
            for i, (s, src_ap, ok_, sm, sp, res_ap) in enumerate(specs):
                so = s * 8
                sq[i] = E("ACT", lambda a,
                          o=ebc.ap(),
                          i_=xmbuf.ap()[:, s, :],
                          ac=stt.ap()[:, so + 2:so + 3]:
                          a.activation(o, i_, AF.Square, accum_out=ac),
                          waits=[sub[i]], inc="A")
            for i, (s, src_ap, ok_, sm, sp, res_ap) in enumerate(specs):
                so = s * 8
                sd[i] = E("ACT", lambda a, o=stt.ap()[:, so + 4:so + 5],
                          i_=stt.ap()[:, so + 2:so + 3]:
                          a.activation(o, i_, AF.Ln, scale=1.0 / D,
                                       bias=c_eps.ap()),
                          waits=[sq[i]], inc="A")
            for i, (s, src_ap, ok_, sm, sp, res_ap) in enumerate(specs):
                so = s * 8
                rs[i] = E("ACT", lambda a, o=stt.ap()[:, so + 5:so + 6],
                          i_=stt.ap()[:, so + 4:so + 5]:
                          a.activation(o, i_, AF.Exp, scale=-0.5),
                          waits=[sd[i]], inc="A")
            for i, (s, src_ap, ok_, sm, sp, res_ap) in enumerate(specs):
                so = s * 8
                if ok_ == "xln":
                    fins[i] = E("DVE", lambda v, o=xln.ap()[:, s, :],
                                i_=xmbuf.ap()[:, s, :],
                                sc=stt.ap()[:, so + 5:so + 6]:
                                v.tensor_scalar_mul(o, i_, sc),
                                waits=[rs[i]] + list(fin_waits), inc="V")
                elif ok_ == "resid":
                    fins[i] = E("DVE", lambda v, o=res_ap, i_=xmbuf.ap()[:, s, :],
                                i1=res_ap, sc=stt.ap()[:, so + 5:so + 6]:
                                v.scalar_tensor_tensor(o, i_, sc, i1,
                                                       ALU.mult, ALU.add),
                                waits=[rs[i]], inc="V")
                else:
                    fins[i] = E("DVE", lambda v, o=res_ap, i_=xmbuf.ap()[:, s, :],
                                i1=res_ap, sc=stt.ap()[:, so + 5:so + 6],
                                ac=stt.ap()[:, so:so + 1]:
                                v.scalar_tensor_tensor(o, i_, sc, i1,
                                                       ALU.mult, ALU.add,
                                                       accum_out=ac),
                                waits=[rs[i]], inc="V")
            return fins, sub

        def emit_transposes(ln_ms):
            evs = []
            tms = []
            for s in range(SUB):
                w = [ln_ms[s], bank_r[s]]
                tm = None
                for c in range(8):
                    tm = E("PE", lambda e, o=psbf[:, s, c * 128:(c + 1) * 128],
                           i=xln.ap()[:, s, c * 128:(c + 1) * 128]:
                           e.transpose(o, i, c_identb.ap()),
                           waits=(w if c == 0 else []),
                           inc=("P" if c == 7 else None))
                bank_w[s] = tm
                ev = E("DVE", lambda v,
                       o=xlnT.ap()[:, :, s * 128:(s + 1) * 128],
                       i=psbf[:, s, :].rearrange("p (c q) -> p c q", c=8):
                       v.tensor_copy(o, i), waits=[tm], inc="V")
                bank_r[s] = ev
                evs.append(ev)
                tms.append(tm)
            return evs[-1], tms[-1]

        prev_out_ms = [None]
        for t in range(n_tiles):
            first_tile = (t == 0)
            base = t * NCH
            res_b, scr_b = (xres, ybuf) if t % 2 == 0 else (ybuf, xres)
            if t + 1 < n_tiles:
                emit_xin(t + 1, extra_waits=[prev_out_ms[0]])

            def gq(c): return base + c
            def gdn(c): return base + 8 + c
            def gup(i): return base + 16 + i
            def gwo(c): return base + 22 + c
            def gwg(i): return base + 30 + i
            def gwu(i): return base + 38 + i
            def gwd(kk): return base + 46 + kk

            if t == 0:
                ln1, _ = ln_phase([(s, res_b.ap()[:, s, :], "xln",
                                    [xin_ms[0]], None, None)
                                   for s in range(SUB)])
                xlnT_ready, _ = emit_transposes(ln1)
            else:
                xlnT_ready = pending_xlnT
            if taps and t == 0:
                d1 = nc.dram_tensor("d_xlnT1", [128, 8, NT], BF16,
                                    kind="ExternalOutput").ap()
                E("GP", lambda g, d=d1: g.dma_start(d, xlnT.ap()),
                  waits=[xlnT_ready], inc="X", amt=16)
                d2 = nc.dram_tensor("d_xln1", [128, SUB, D], BF16,
                                    kind="ExternalOutput").ap()
                E("GP", lambda g, d=d2: g.dma_start(d, xln.ap()),
                  waits=[ln1[3]], inc="X", amt=16)
            if stop_after == "ln1":
                for nm2, buf2 in [("d_xres", xres), ("d_stt", stt),
                                  ("d_xm", xmbuf)]:
                    shp2 = list(buf2.ap().shape)
                    dr2 = nc.dram_tensor(nm2, shp2, buf2.ap().dtype,
                                         kind="ExternalOutput").ap()
                    E("GP", lambda g, d=dr2, s=buf2: g.dma_start(d, s.ap()),
                      waits=[("P", bld.cnt["P"]), ("A", bld.cnt["A"]),
                             ("V", bld.cnt["V"])], inc="X", amt=16)
                emit_tile_loads(t, None); continue

            qfm_ev = []
            for m in range(8):
                b = m % 4
                w = [xlnT_ready, bank_r[b]]
                mm = None
                for k in range(8):
                    mm = E("PE", lambda e, o=psap[:, b, :],
                           l=chunk_ap(gq(k), 1024), m_=m, k_=k:
                           e.matmul(o, l[:, m_ * 128:(m_ + 1) * 128],
                                    xlnT.ap()[:, k_, :],
                                    start=(k_ == 0), stop=(k_ == 7)),
                           waits=(w + [w_ms(gq(k))] if k == 0
                                  else [w_ms(gq(k))]),
                           inc=(("F" if m == 7 else "P") if k == 7 else None))
                bank_w[b] = mm
                ev = E("ACT", lambda a, o=qfm.ap()[:, m, :], i=psap[:, b, :],
                       bi=c_bq.ap()[:, m:m + 1]:
                       a.activation(o, i, AF.Identity, bias=bi), waits=[mm], inc="A")
                bank_r[b] = ev
                qfm_ev.append(ev)
            chunk_free_val.extend([bld.cnt["F"]] * 8)
            if stop_after == "q":
                emit_tile_loads(t, None); continue

            lat_ev = []
            for m in range(2):
                b = 4 + m
                w = [xlnT_ready, bank_r[b]]
                mm = None
                for k in range(8):
                    mm = E("PE", lambda e, o=psap[:, b, :],
                           l=chunk_ap(gdn(k), 256), m_=m, k_=k:
                           e.matmul(o, l[:, m_ * 128:(m_ + 1) * 128],
                                    xlnT.ap()[:, k_, :],
                                    start=(k_ == 0), stop=(k_ == 7)),
                           waits=(w + [w_ms(gdn(k))] if k == 0
                                  else [w_ms(gdn(k))]),
                           inc=(("F" if m == 1 else "P") if k == 7 else None))
                bank_w[b] = mm
                ev = E("ACT", lambda a, o=lat.ap()[:, m, :], i=psap[:, b, :],
                       bi=c_bdn.ap()[:, m:m + 1]:
                       a.activation(o, i, AF.Identity, bias=bi), waits=[mm], inc="A")
                bank_r[b] = ev
                lat_ev.append(ev)
            chunk_free_val.extend([bld.cnt["F"]] * 8)
            if stop_after == "lat":
                emit_tile_loads(t, None); continue

            kv_ev = [None] * 24
            for g in range(24):
                b = g % 4
                w = [lat_ev[-1], bank_r[b]]
                mm = None
                for k2 in range(2):
                    mh = g // 8
                    mloc = g % 8
                    ci = gup(k2 * 3 + mh)
                    mm = E("PE", lambda e, o=psap[:, b, :],
                           l=chunk_ap(ci, 1024), ml=mloc, k_=k2:
                           e.matmul(o, l[:, ml * 128:(ml + 1) * 128],
                                    lat.ap()[:, k_, :],
                                    start=(k_ == 0), stop=(k_ == 1)),
                           waits=(w + [w_ms(ci)] if k2 == 0 else [w_ms(ci)]),
                           inc=(("F" if g == 23 else "P") if k2 == 1 else None))
                bank_w[b] = mm
                dst = kdup.ap()[:, g, :] if g < 16 else vcmp.ap()[:, g - 16, :]
                ev = E("ACT", lambda a, o=dst, i=psap[:, b, :],
                       bi=c_bup.ap()[:, g:g + 1]:
                       a.activation(o, i, AF.Identity, bias=bi), waits=[mm], inc="A")
                bank_r[b] = ev
                kv_ev[g] = ev
            chunk_free_val.extend([bld.cnt["F"]] * 6)
            if stop_after == "kv":
                emit_tile_loads(t, None); continue

            sred_ms = [None] * 128
            idx = 0
            pair_i = 0
            for c in range(8):
                for gp in range(8):
                    g = 2 * gp
                    w = [qfm_ev[c], kv_ev[g + 1]]
                    if idx >= 8:
                        w.append(sred_ms[idx - 7])
                    peng = "DVE"
                    sl = idx % 8
                    pm = E(peng, lambda v, o=pbf.ap()[:, sl:sl + 2, :],
                           a=qfm.ap()[:, c:c + 1, :].broadcast_to([128, 2, NT]),
                           b2=kdup.ap()[:, g:g + 2, :]:
                           v.tensor_tensor(o, a, b2, ALU.mult),
                           waits=w, inc=("V" if peng == "DVE" else "G"))
                    pair_i += 1
                    for q2 in range(2):
                        bank = 6 + c // 4
                        r1 = ((2 * c) % 8) * 16 + g + q2
                        first = (c % 4 == 0 and g + q2 == 0)
                        last = (c % 4 == 3 and g + q2 == 15)
                        w2 = [pm]
                        if first:
                            w2.append(bank_r[bank])
                        mm = E("PE", lambda e, o=psap[:, bank, :],
                               l=c_credc.ap()[:, 127 - r1:255 - r1],
                               pi=idx % 8, f=first, la=last:
                               e.matmul(o, l, pbf.ap()[:, pi, :],
                                        start=f, stop=la),
                               waits=(w2 if q2 == 0 else []), inc="P")
                        sred_ms[idx] = mm
                        if last:
                            bank_w[bank] = mm
                        idx += 1

            if stop_after == "scores_noevac":
                emit_tile_loads(t, None); continue
            sev = []
            for c2 in range(2):
                ev = E("ACT", lambda a, o=scev.ap()[:, c2, :],
                       i=psap[:, 6 + c2, :]: a.copy(o, i),
                       waits=[bank_w[6 + c2]], inc="A")
                bank_r[6 + c2] = ev
                sev.append(ev)
            strn_last = None
            for s in range(SUB):
                b = 4 + s // 2
                w = [sev[-1], bank_r[4], bank_r[5]]
                for c2 in range(2):
                    strn_last = E("PE", lambda e,
                                  o=psap[:, b, (s % 2) * 256 + c2 * 128:
                                         (s % 2) * 256 + (c2 + 1) * 128],
                                  i=scev.ap()[:, c2, s * 128:(s + 1) * 128]:
                                  e.transpose(o, i, c_ident3.ap()),
                                  waits=(w if (s == 0 and c2 == 0) else []),
                                  inc=("P" if (s == 3 and c2 == 1) else None))
            bank_w[4] = strn_last
            bank_w[5] = strn_last

            tmv_all = psap[:, 4:6, :].rearrange("p b (x h g) -> p (b x h) g",
                                                x=2, h=16)
            smax_f = smax.ap().rearrange("p s h -> p (s h)")
            etm_f = etm.ap().rearrange("p s q -> p (s q)")
            mx = E("DVE", lambda v, o=smax_f, i=tmv_all:
                   v.reduce_max(o, i, AX.X),
                   waits=[strn_last], inc="V")
            sb_ = E("DVE", lambda v,
                    o=etm_f.rearrange("p (a g) -> p a g", g=16),
                    i=tmv_all,
                    m=smax_f.unsqueeze(2).broadcast_to([128, 64, 16]):
                    v.tensor_tensor(o, i, m, ALU.subtract),
                    waits=[mx], inc="V")
            ex = E("ACT", lambda a, o=etm_f, i=etm_f:
                   a.activation(o, i, AF.Exp, scale=0.125),
                   waits=[sb_], inc="A")
            zs = E("DVE", lambda v, o=smax_f,
                   i=etm_f.rearrange("p (a g) -> p a g", g=16):
                   v.reduce_sum(o, i, AX.X), waits=[ex], inc="V")
            rc = E("DVE", lambda v, o=smax_f, i=smax_f: v.reciprocal(o, i),
                   waits=[zs], inc="V")
            nm = E("DVE", lambda v,
                   o=atts.ap()[:, 1024:2048].rearrange("p (a g) -> p a g", g=16),
                   i=etm_f.rearrange("p (a g) -> p a g", g=16),
                   m=smax_f.unsqueeze(2).broadcast_to([128, 64, 16]):
                   v.tensor_tensor(o, i, m, ALU.mult),
                   waits=[rc], inc="V")
            soft = [nm] * SUB
            bank_r[4] = nm
            bank_r[5] = nm

            atrn = None
            w0 = soft + [bank_r[6]]
            for c2 in range(2):
                for s in range(SUB):
                    atrn = E("PE", lambda e,
                             o=psbf[:, 6, c2 * 512 + s * 128:c2 * 512 + (s + 1) * 128],
                             i=atts.ap()[:, 1024 + s * 256 + c2 * 128:1024 + s * 256 + (c2 + 1) * 128]:
                             e.transpose(o, i, c_identb.ap()),
                             waits=(w0 if (c2 == 0 and s == 0) else []),
                             inc=("P" if (c2 == 1 and s == 3) else None))
            bank_w[6] = atrn
            attT_ev = []
            for c2 in range(2):
                ev = E("DVE", lambda v, o=atts.ap()[:, c2 * 512:(c2 + 1) * 512],
                       i=psbf[:, 6, c2 * 512:(c2 + 1) * 512]:
                       v.tensor_copy(o, i), waits=[atrn], inc="V")
                attT_ev.append(ev)
            bank_r[6] = attT_ev[-1]

            idred_ms = [None] * 128
            bc_ms = [None] * 8
            pp_ms = [None] * 4
            ao_ev = [None] * 8
            pidx0 = 0

            def emit_bcast(h, j):
                eb = j % 4
                si = (h % 8) * 8 + j
                w = [attT_ev[h // 8], bank_r[eb]]
                bc_ms[j] = E("PE", lambda e, o=psap[:, eb, :],
                             l=c_selc.ap()[:, si, :], ti=h // 8:
                             e.matmul(o, l, atts.ap()[:, ti * 512:(ti + 1) * 512],
                                      start=True, stop=True),
                             waits=w, inc="P")
                bank_w[eb] = bc_ms[j]

            def emit_prodpair(h, jp, pidx):
                b0 = (2 * jp) % 4
                w2 = [bc_ms[2 * jp + 1], kv_ev[16 + 2 * jp + 1]]
                if pidx >= 8:
                    w2.append(idred_ms[pidx - 7])
                else:
                    w2.append(sred_ms[120 + pidx + 1])
                sl = pidx % 8
                pp_ms[jp] = E("DVE", lambda v, o=pbf.ap()[:, sl:sl + 2, :],
                              a=psap[:, b0:b0 + 2, :],
                              b2=vcmp.ap()[:, 2 * jp:2 * jp + 2, :]:
                              v.tensor_tensor(o, a, b2, ALU.mult),
                              waits=w2, inc="V")
                bank_r[b0] = pp_ms[jp]
                bank_r[b0 + 1] = pp_ms[jp]

            def emit_idred(h, j, pidx, ob):
                w3 = [pp_ms[j // 2]]
                if j == 0:
                    w3.append(bank_r[ob])
                mm = E("PE", lambda e,
                       o=psap[(h % 2) * 64:(h % 2) * 64 + 64, ob, :],
                       pi=pidx % 8, f=(j == 0), la=(j == 7):
                       e.matmul(o, c_idstk.ap(), pbf.ap()[:, pi, :],
                                start=f, stop=la),
                       waits=w3, inc="P")
                idred_ms[pidx] = mm
                return mm

            st_bc = [[None] * 8, [None] * 8]
            st_pp = [[None] * 4, [None] * 4]

            def unitA(h, jp, pidx0):
                par = h % 2
                bc_ms[:] = st_bc[par]
                pp_ms[:] = st_pp[par]
                emit_bcast(h, 2 * jp)
                emit_bcast(h, 2 * jp + 1)
                emit_prodpair(h, jp, pidx0 + 2 * jp)
                st_bc[par][:] = bc_ms
                st_pp[par][:] = pp_ms

            def unitB(h, jp, pidx0, ob):
                par = h % 2
                pp_ms[:] = st_pp[par]
                last = emit_idred(h, 2 * jp, pidx0 + 2 * jp, ob)
                last = emit_idred(h, 2 * jp + 1, pidx0 + 2 * jp + 1, ob)
                return last

            for jp in range(4):
                unitA(0, jp, 0)
            for h in range(H):
                pair = h // 2
                ob = 7 - (pair % 2)
                last = None
                for jp in range(4):
                    last = unitB(h, jp, 8 * h, ob)
                    if h + 1 < H:
                        unitA(h + 1, jp, 8 * (h + 1))
                if h % 2 == 1:
                    bank_w[ob] = last
                    w4 = [bank_w[ob]]
                    if first_tile and pair == 0:
                        w4.extend(conv_end2)
                    ev = E("ACT", lambda a, o=aoT.ap()[:, pair, :],
                           i=psap[:, ob, :]: a.copy(o, i), waits=w4, inc="A")
                    bank_r[ob] = ev
                    ao_ev[pair] = ev

            attv_last = idred_ms[127]
            if stop_after is None:
                emit_alt_loads([k for k in range(16) if k not in (12, 13)],
                               attv_last)
            if stop_after == "attv":
                emit_tile_loads(t, None); continue
            wout_ev = [None] * SUB
            wout_mm = [None] * SUB
            attn_fin = [None] * SUB
            ln2 = [None] * SUB
            for pr in (0, 1):
                ss = (2 * pr, 2 * pr + 1)
                for s in ss:
                    sb0 = 4 + (s % 2) * 2
                    w = [bank_r[sb0], bank_r[sb0 + 1]] + ao_ev
                    mm = None
                    for c in range(8):
                        for half in range(2):
                            wk = [w_ms(gwo(c))] if half == 0 else []
                            mm = E("PE", lambda e, o=psap[:, sb0 + half, :],
                                   l=aoT.ap()[:, c, s * 128:(s + 1) * 128],
                                   wch=chunk_ap(gwo(c), 1024), hf=half, c_=c:
                                   e.matmul(o, l,
                                            wch[:, hf * 512:(hf + 1) * 512],
                                            start=(c_ == 0), stop=(c_ == 7)),
                                   waits=(w + wk if (c == 0 and half == 0)
                                          else wk),
                                   inc=(("F" if s == 3 else "P")
                                        if (c == 7 and half == 1) else None))
                    bank_w[sb0] = mm
                    bank_w[sb0 + 1] = mm
                    wout_mm[s] = mm
                for s in ss:
                    sb0 = 4 + (s % 2) * 2
                    ev = E("DVE", lambda v, o=xmbuf.ap()[:, s, :],
                           i=psap[:, sb0:sb0 + 2, :].rearrange(
                               "p b f -> p (b f)"),
                           ac=stt.ap()[:, s * 8:s * 8 + 1]:
                           v.scalar_tensor_tensor(o, i, 1.0, c_bout.ap(),
                                                  ALU.mult, ALU.add,
                                                  accum_out=ac),
                           waits=[wout_mm[s]], inc="V")
                    bank_r[sb0] = ev
                    bank_r[sb0 + 1] = ev
                    wout_ev[s] = ev
                af, _ = ln_phase(
                    [(s, xmbuf.ap()[:, s, :], "resid_accum", [wout_ev[s]],
                      wout_ev[s], res_b.ap()[:, s, :]) for s in ss])
                attn_fin[2 * pr:2 * pr + 2] = af
                l2, _ = ln_phase([(s, res_b.ap()[:, s, :], "xln",
                                   [attn_fin[s]], attn_fin[s], None)
                                  for s in ss])
                ln2[2 * pr:2 * pr + 2] = l2
            chunk_free_val.extend([bld.cnt["F"]] * 8)
            wout_last = wout_mm[3]
            if stop_after is None:
                emit_alt_loads([12, 13], wout_last)
            if stop_after == "wout":
                emit_tile_loads(t, None); continue

            xlnT_ready, t2_tm = emit_transposes(ln2)
            ln1_next = [None] * SUB
            if t + 1 < n_tiles:
                ln1_next, _ = ln_phase(
                    [(s, scr_b.ap()[:, s, :], "xln", [xin_ms[t + 1]],
                      None, None) for s in range(SUB)],
                    fin_waits=[("P", t2_tm[1])])
                pending_ln1 = ln1_next

            silu_ms = [None] * 24
            for hh in range(2):
                for mmi in range(12):
                    m = hh * 12 + mmi
                    b = 2 * (m % 2)
                    w = [xlnT_ready, bank_r[b]]
                    mm = None
                    for k in range(8):
                        if hh == 0:
                            lap = chunk_ap(gwg(k), 1536)
                            wk = [w_ms(gwg(k))]
                        else:
                            lap = alt_ap(k)
                            wk = [alt_w(k, t)]
                        mm = E("PE", lambda e, o=psap[:, b, :],
                               l=lap, ml=mmi, k_=k:
                               e.matmul(o, l[:, ml * 128:(ml + 1) * 128],
                                        xlnT.ap()[:, k_, :],
                                        start=(k_ == 0), stop=(k_ == 7)),
                               waits=(w + wk if k == 0 else wk),
                               inc=(("F" if (mmi == 11 and hh == 0) else "P")
                                    if k == 7 else None))
                    bank_w[b] = mm
                    w2 = [mm]
                    if first_tile and m == 0:
                        w2.extend(conv_end2)
                    ev = E("ACT", lambda a, o=hbuf.ap()[:, m, :], i=psap[:, b, :]:
                           a.activation(o, i, AF.Silu, scale=QSCALE),
                           waits=w2, inc="A")
                    bank_r[b] = ev
                    silu_ms[m] = ev
                if hh == 0:
                    chunk_free_val.extend([bld.cnt["F"]] * 8)

            stt_ms = [None] * 24
            for hh in range(2):
                for mmi in range(12):
                    m = hh * 12 + mmi
                    b = 2 * (m % 2) + 1
                    w = [xlnT_ready, bank_r[b]]
                    mm = None
                    for k in range(8):
                        if hh == 0:
                            lap = chunk_ap(gwu(k), 1536)
                            wk = [w_ms(gwu(k))]
                        else:
                            lap = alt_ap(8 + k)
                            wk = [alt_w(8 + k, t)]
                        mm = E("PE", lambda e, o=psap[:, b, :],
                               l=lap, ml=mmi, k_=k:
                               e.matmul(o, l[:, ml * 128:(ml + 1) * 128],
                                        xlnT.ap()[:, k_, :],
                                        start=(k_ == 0), stop=(k_ == 7)),
                               waits=(w + wk if k == 0 else wk),
                               inc=(("F" if (mmi == 11 and hh == 0) else "P")
                                    if k == 7 else None))
                    bank_w[b] = mm
                    ev = E("DVE", lambda v, o=hbuf.ap()[:, m, :], i=psap[:, b, :],
                           i1=hbuf.ap()[:, m, :]:
                           v.scalar_tensor_tensor(o, i, QSCALE, i1,
                                                  ALU.mult, ALU.mult),
                           waits=[mm, silu_ms[m]], inc="V")
                    bank_r[b] = ev
                    stt_ms[m] = ev
                if hh == 0:
                    chunk_free_val.extend([bld.cnt["F"]] * 8)

            for kk in range(24):
                for s in range(SUB):
                    for half in range(2):
                        b = s * 2 + half
                        w = [w_ms(gwd(kk)), stt_ms[kk]]
                        if kk == 0:
                            w.append(bank_r[b])
                        mm = E("PE", lambda e, o=psap[:, b, :],
                               l=hbuf.ap()[:, kk, s * 128:(s + 1) * 128],
                               wch=chunk_ap(gwd(kk), 1024), hf=half, k_=kk:
                               e.matmul(o, l, wch[:, hf * 512:(hf + 1) * 512],
                                        start=(k_ == 0), stop=(k_ == 23)),
                               waits=w,
                               inc=(("F" if (s == 3 and half == 1) else
                                     ("P" if kk == 23 else None))))
                        if kk == 23:
                            bank_w[b] = mm
                chunk_free_val.append(bld.cnt["F"])

            dn_ev = [None] * SUB
            for s in range(SUB):
                ev = E("DVE", lambda v, o=xmbuf.ap()[:, s, :],
                       i=psap[:, s * 2:s * 2 + 2, :].rearrange("p b f -> p (b f)"),
                       ac=stt.ap()[:, s * 8:s * 8 + 1]:
                       v.tensor_scalar(o, i, 1.0, 0.0, ALU.mult, ALU.add,
                                       accum_out=ac),
                       waits=[bank_w[s * 2], bank_w[s * 2 + 1],
                              ln1_next[s]], inc="V")
                bank_r[s * 2] = ev
                bank_r[s * 2 + 1] = ev
                dn_ev[s] = ev
            if t + 1 < n_tiles:
                pending_xlnT, _ = emit_transposes(ln1_next)
            dn_fin, _ = ln_phase(
                [(s, xmbuf.ap()[:, s, :], "resid", [dn_ev[s]],
                  dn_ev[s], res_b.ap()[:, s, :]) for s in range(SUB)])
            dst = y_out[t * NT:(t + 1) * NT, :].rearrange("(s p) d -> p s d", p=128)
            out_ms = E("GP", lambda g, d=dst, sb_=res_b: g.dma_start(d, sb_.ap()),
                       waits=dn_fin, inc="X", amt=16)
            prev_out_ms[0] = out_ms

            emit_tile_loads(t)

        if taps:
            tap_list = [
                ("d_xlnT", xlnT), ("d_qfm", qfm), ("d_lat", lat),
                ("d_kdup", kdup), ("d_vcmp", vcmp), ("d_scev", scev),
                ("d_atts", atts), ("d_aoT", aoT),
                ("d_hbuf", hbuf), ("d_ybuf", ybuf), ("d_etm", etm),
            ]
            for nm, dr_s, shp in [("d_wqs", wq_s, [8, 128, 1024]),
                                  ("d_wds", wd_s, [24, 128, 1024])]:
                drt = nc.dram_tensor(nm, shp, BF16, kind="ExternalOutput").ap()
                E("GP", lambda g, d=drt, s=dr_s: g.dma_start(d, s),
                  waits=[("X", bld.cnt["X"])], inc="X", amt=16)
            allw = [("P", bld.cnt["P"]), ("A", bld.cnt["A"]),
                    ("V", bld.cnt["V"]), ("X", bld.cnt["X"])]
            for nm, buf in tap_list:
                shp = list(buf.ap().shape)
                dt = buf.ap().dtype
                dr = nc.dram_tensor(nm, shp, dt, kind="ExternalOutput").ap()
                E("GP", lambda g, d=dr, s=buf: g.dma_start(d, s.ap()),
                  waits=allw, inc="X", amt=16)

        E("GP", lambda g: g.engine_nop(), waits=[("X", bld.cnt["X"])])

        def render(eng_name, deco):
            @deco
            def _(e):
                waited = {}
                for fn, waits, inc, amt, extra in bld.ops[eng_name]:
                    for (snm, val) in waits:
                        if waited.get(snm, -1) < val:
                            e.wait_ge(sems[snm], val)
                            waited[snm] = val
                    inst = fn(e)
                    if inc is not None:
                        inst = inst.then_inc(sems[inc], amt)
                    if extra is not None:
                        inst.then_inc(sems[extra[0]], extra[1])

        render("SP", blk.sync)
        render("GP", blk.gpsimd)
        render("PE", blk.tensor)
        render("ACT", blk.scalar)
        render("DVE", blk.vector)

    nc.finalize()
    return nc


_NC_CACHE = None


def _get_nc():
    global _NC_CACHE
    if _NC_CACHE is None:
        _NC_CACHE = build_program()
    return _NC_CACHE


def kernel(L0_attn_x, L0_attn_Wq, L0_attn_bq, L0_attn_Wdown, L0_attn_bdown,
           L0_attn_Wup, L0_attn_bup, L0_attn_Wout, L0_attn_bout,
           L0_ffn_w_gate, L0_ffn_w_up, L0_ffn_w_down):
    x = np.ascontiguousarray(np.asarray(L0_attn_x, dtype=np.float32).reshape(NTOK, D))
    wq = np.ascontiguousarray(np.asarray(L0_attn_Wq, dtype=np.float32))
    bq = np.asarray(L0_attn_bq, dtype=np.float32)
    wdn = np.ascontiguousarray(np.asarray(L0_attn_Wdown, dtype=np.float32))
    bdn = np.asarray(L0_attn_bdown, dtype=np.float32)
    wup = np.asarray(L0_attn_Wup, dtype=np.float32)
    bup = np.asarray(L0_attn_bup, dtype=np.float32)
    wo = np.ascontiguousarray(np.asarray(L0_attn_Wout, dtype=np.float32))
    bout = np.asarray(L0_attn_bout, dtype=np.float32)
    wg = np.asarray(L0_ffn_w_gate, dtype=np.int32)
    wu = np.asarray(L0_ffn_w_up, dtype=np.int32)
    wd = np.asarray(L0_ffn_w_down, dtype=np.int32)

    wup4 = wup.reshape(DL, H, 2, DH)
    kcols = wup4[:, :, 0, :]
    vcols = wup4[:, :, 1, :]
    wup_p = np.ascontiguousarray(np.concatenate(
        [np.repeat(kcols, 2, axis=1).reshape(DL, 2048),
         vcols.reshape(DL, 1024)], axis=1))
    bup4 = bup.reshape(H, 2, DH)
    bup_p = np.concatenate([np.repeat(bup4[:, 0, :], 2, axis=0).reshape(2048),
                            bup4[:, 1, :].reshape(1024)])
    bq_fm = np.ascontiguousarray(bq.reshape(8, 128).T)
    bdn_fm = np.ascontiguousarray(bdn.reshape(2, 128).T)
    bup_fm = np.ascontiguousarray(bup_p.reshape(24, 128).T)
    bout_bc = np.ascontiguousarray(np.broadcast_to(bout, (128, D)))

    wgT = np.ascontiguousarray(wg.T)
    wuT = np.ascontiguousarray(wu.T)
    wdT = np.ascontiguousarray(wd.T)

    identb = np.eye(128, dtype=bfdt)
    ident3 = np.eye(128, dtype=np.float32)
    credc = np.zeros((128, 256), dtype=bfdt)
    credc[0:64, 127] = 1.0
    credc[64:128, 143] = 1.0
    idstk = np.zeros((128, 64), dtype=bfdt)
    idstk[np.arange(128), np.arange(128) % 64] = 1.0
    selc = np.zeros((128, 64, 128), dtype=bfdt)
    for si in range(64):
        selc[2 * si, si, 0:64] = 1.0
        selc[2 * si + 1, si, 64:128] = 1.0

    shards = np.split(x, NCORES, axis=0)
    common = {
        "wq_r": wq, "wdn_r": wdn, "wup_r": wup_p, "wo_r": wo,
        "wg_r": wgT, "wu_r": wuT, "wd_r": wdT,
        "cb_identb": identb, "cb_ident3": ident3, "cb_credc": credc,
        "cb_idstk": idstk, "cb_selc": selc, "cb_bq": bq_fm, "cb_bdn": bdn_fm,
        "cb_bup": bup_fm, "cb_bout": bout_bc,
        "cb_eps": np.full((128, 1), EPS, np.float32),
    }
    in_maps = [dict(common, x=np.ascontiguousarray(s)) for s in shards]
    nc = _get_nc()
    res = run_bass_kernel_spmd(nc, in_maps, core_ids=list(range(NCORES)))
    out = np.concatenate([r["y"] for r in res.results], axis=0)
    return out.reshape(4, 8192, D)


# version 26
# speedup vs baseline: 1.1786x; 1.0180x over previous
import numpy as np
import ml_dtypes
from contextlib import ExitStack

import concourse.bass as bass
import concourse.mybir as mybir
from concourse.bass_utils import run_bass_kernel_spmd

F32 = mybir.dt.float32
BF16 = mybir.dt.bfloat16
I32 = mybir.dt.int32
AF = mybir.ActivationFunctionType
ALU = mybir.AluOpType
AX = mybir.AxisListType
bfdt = ml_dtypes.bfloat16

NCORES = 8
D = 1024
H = 16
DH = 64
DL = 256
HID = 3072
NTOK = 4 * 8192
PER_CORE = NTOK // NCORES
NT = 512
SUB = NT // 128
TILES = PER_CORE // NT
EPS = 1e-5
QSCALE = 1.0 / 127.0
RING = 11
SLOTW = 1536

ENGINES = ("SP", "GP", "PE", "ACT", "DVE")


class Builder:

    SEMS = (["X", "P", "A", "V", "F", "G", "C0", "C1", "C2", "C3",
             "O0", "O1"]
            + ["W%d" % i for i in range(RING)]
            + ["AW%d" % i for i in range(16)])

    def __init__(self):
        self.ops = {e: [] for e in ENGINES}
        self.cnt = {s: 0 for s in self.SEMS}

    def emit(self, eng, fn, waits=(), inc=None, amt=1, extra_inc=None):
        ws = [w for w in waits if w is not None]
        self.ops[eng].append((fn, ws, inc, amt, extra_inc))
        ms = None
        if inc is not None:
            self.cnt[inc] += amt
            ms = (inc, self.cnt[inc])
        if extra_inc is not None:
            s2, a2 = extra_inc
            self.cnt[s2] += a2
        return ms


def build_program(n_tiles=TILES, taps=False, stop_after=None):
    nc = bass.Bass("TRN2", target_bir_lowering=False)

    x_in = nc.dram_tensor("x", [PER_CORE, D], F32, kind="ExternalInput").ap()
    wq_r = nc.dram_tensor("wq_r", [D, D], F32, kind="ExternalInput").ap()
    wdn_r = nc.dram_tensor("wdn_r", [D, DL], F32, kind="ExternalInput").ap()
    wup_r = nc.dram_tensor("wup_r", [DL, 3072], F32, kind="ExternalInput").ap()
    wo_r = nc.dram_tensor("wo_r", [D, D], F32, kind="ExternalInput").ap()
    wg_r = nc.dram_tensor("wg_r", [D, HID], I32, kind="ExternalInput").ap()
    wu_r = nc.dram_tensor("wu_r", [D, HID], I32, kind="ExternalInput").ap()
    wd_r = nc.dram_tensor("wd_r", [HID, D], I32, kind="ExternalInput").ap()
    cb_identb = nc.dram_tensor("cb_identb", [128, 128], BF16, kind="ExternalInput").ap()
    cb_ident3 = nc.dram_tensor("cb_ident3", [128, 128], F32, kind="ExternalInput").ap()
    cb_credc = nc.dram_tensor("cb_credc", [128, 256], BF16, kind="ExternalInput").ap()
    cb_idstk = nc.dram_tensor("cb_idstk", [128, 64], BF16, kind="ExternalInput").ap()
    cb_selc = nc.dram_tensor("cb_selc", [128, 64, 128], BF16, kind="ExternalInput").ap()
    cb_bq = nc.dram_tensor("cb_bq", [128, 8], F32, kind="ExternalInput").ap()
    cb_bdn = nc.dram_tensor("cb_bdn", [128, 2], F32, kind="ExternalInput").ap()
    cb_bup = nc.dram_tensor("cb_bup", [128, 24], F32, kind="ExternalInput").ap()
    cb_bout = nc.dram_tensor("cb_bout", [128, D], F32, kind="ExternalInput").ap()
    cb_eps = nc.dram_tensor("cb_eps", [128, 1], F32, kind="ExternalInput").ap()
    y_out = nc.dram_tensor("y", [PER_CORE, D], F32, kind="ExternalOutput").ap()

    wq_s = nc.dram_tensor("wq_s", [8, 128, 1024], BF16, kind="Internal").ap()
    wdn_s = nc.dram_tensor("wdn_s", [8, 128, 256], BF16, kind="Internal").ap()
    wup_s = nc.dram_tensor("wup_s", [6, 128, 1024], BF16, kind="Internal").ap()
    wo_s = nc.dram_tensor("wo_s", [8, 128, 1024], BF16, kind="Internal").ap()
    wg_s = nc.dram_tensor("wg_s", [16, 128, 1536], BF16, kind="Internal").ap()
    wu_s = nc.dram_tensor("wu_s", [16, 128, 1536], BF16, kind="Internal").ap()
    wd_s = nc.dram_tensor("wd_s", [24, 128, 1024], BF16, kind="Internal").ap()

    bld = Builder()

    with ExitStack() as st:
        sb = lambda name, shape, dt: st.enter_context(nc.sbuf_tensor(name, shape, dt))
        c_identb = sb("c_identb", [128, 128], BF16)
        c_ident3 = sb("c_ident3", [128, 128], F32)
        c_credc = sb("c_credc", [128, 256], BF16)
        c_idstk = sb("c_idstk", [128, 64], BF16)
        c_selc = sb("c_selc", [128, 64, 128], BF16)
        c_bq = sb("c_bq", [128, 8], F32)
        c_bdn = sb("c_bdn", [128, 2], F32)
        c_bup = sb("c_bup", [128, 24], F32)
        c_bout = sb("c_bout", [128, D], F32)
        c_eps = sb("c_eps", [128, 1], F32)
        xres = sb("xres", [128, SUB, D], F32)
        ybuf = sb("ybuf", [128, SUB, D], F32)
        xmbuf = sb("xmbuf", [128, 4, D], F32)
        stt = sb("stt", [128, 32], F32)
        xln = sb("xln", [128, SUB, D], BF16)
        xlnT = sb("xlnT", [128, 8, NT], BF16)
        qfm = sb("qfm", [128, 8, NT], BF16)
        lat = sb("lat", [128, 2, NT], BF16)
        kdup = sb("kdup", [128, 16, NT], BF16)
        vcmp = sb("vcmp", [128, 8, NT], BF16)
        scev = sb("scev", [128, 2, NT], F32)
        smax = sb("smax", [128, SUB, 16], F32)
        etm = sb("etm", [128, SUB, 256], F32)
        atts = sb("atts", [128, 2048], BF16)
        ebc = sb("ebc", [128, D], BF16)
        pbf = sb("pbf", [128, 8, NT], BF16)
        aoT = sb("aoT", [128, 8, NT], BF16)
        hbuf = sb("hbuf", [128, 24, NT], BF16)
        wring = sb("wring", [128, RING, SLOTW], BF16)
        ps = st.enter_context(nc.psum_tensor("ps", [128, 8, 512], F32))

        sems = {}
        for s in Builder.SEMS:
            sems[s] = st.enter_context(nc.semaphore(s))
        blk = st.enter_context(nc.Block())

        psap = ps.ap()
        psbf = psap.rearrange("p b f -> p (b f)").bitcast(BF16).rearrange(
            "p (b f) -> p b f", b=8)

        hflat_b = hbuf.ap().rearrange("p a b -> p (a b)")
        cin_i32 = hflat_b.bitcast(I32)
        cin_f32 = hflat_b.bitcast(F32)
        aflat_b = aoT.ap().rearrange("p a b -> p (a b)")

        E = bld.emit

        for src, dst in [(cb_identb, c_identb), (cb_ident3, c_ident3),
                         (cb_credc, c_credc), (cb_idstk, c_idstk),
                         (cb_selc, c_selc), (cb_eps, c_eps),
                         (cb_bq, c_bq), (cb_bdn, c_bdn), (cb_bup, c_bup),
                         (cb_bout, c_bout)]:
            E("GP", lambda g, s=src, d=dst: g.dma_start(d.ap(), s),
              inc="X", amt=16)

        xin_ms = [None] * (TILES + 1)

        def emit_xin(t, extra_waits=()):
            dst = (xres if t % 2 == 0 else ybuf).ap()
            srcv = x_in[t * NT:(t + 1) * NT, :].rearrange(
                "(s p) d -> p s d", p=128)
            xin_ms[t] = E("GP", lambda g, s_=srcv, d=dst: g.dma_start(d, s_),
                          waits=extra_waits, inc="X", amt=16)

        emit_xin(0)

        tile_chunks = ([("wq", c, 1024) for c in range(8)]
                       + [("wdn", c, 256) for c in range(8)]
                       + [("wup", i, 1024) for i in range(6)]
                       + [("wo", c, 1024) for c in range(8)]
                       + [("wg", i, 1536) for i in range(8)]
                       + [("wu", i, 1536) for i in range(8)]
                       + [("wd", kk, 1024) for kk in range(24)])
        NCH = len(tile_chunks)
        alt_aps = ([("kdup", kdup, k * 1536) for k in range(5)]
                   + [("vcmp", vcmp, k * 1536) for k in range(2)]
                   + [("qfm", qfm, k * 1536) for k in range(2)]
                   + [("pbf", pbf, k * 1536) for k in range(2)]
                   + [("etm", etm, 0), ("aoT", aoT, 0), ("aoT", aoT, 1536),
                      ("scev", scev, 0), ("atts", atts, 0)])
        assert len(alt_aps) == 16

        def alt_ap(k):
            nm, tens, off = alt_aps[k]
            fl = tens.ap().rearrange("p a b -> p (a b)") if tens.ap().ndim == 3                 else tens.ap()
            if fl.dtype != BF16:
                fl = fl.bitcast(BF16)
            return fl[:, off:off + 1536]

        def alt_w(k, t):
            return ("AW%d" % k, 16 * (t + 1))
        scr = {"wq": wq_s, "wdn": wdn_s, "wup": wup_s, "wo": wo_s,
               "wg": wg_s, "wu": wu_s, "wd": wd_s}

        chunk_free_val = []

        def w_ms(gi):
            return ("W%d" % (gi % RING), 16 * (gi // RING + 1))

        def chunk_ap(gi, width):
            return wring.ap()[:, gi % RING, 0:width]

        STOP_CHUNKS = {"ln1": 0, "q": 8, "lat": 16, "kv": 22, "scores": 22,
                       "soft": 22, "attv": 22, "wout": 30, "ln2": 30,
                       "scores_noevac": 22, "gate": 38, "up": 46, "dn": 70}

        def _ring_load(gi):
            tag, idx, width = tile_chunks[gi % NCH]
            slot = gi % RING
            waits = [scratch_ready[(tag, idx)]]
            if gi >= RING:
                waits.append(("F", chunk_free_val[gi - RING]))
            E("SP", lambda e, d=wring.ap()[:, slot, 0:width],
              s=scr[tag][idx]: e.dma_start(d, s),
              waits=waits, inc="W%d" % slot, amt=16)

        def emit_tile_loads(t, alt_free_ms=None):
            base = t * NCH
            upto = STOP_CHUNKS.get(stop_after, NCH) if stop_after else NCH
            start_li = RING if (t == 0 and _t0_pre) else 0
            for li in range(start_li, upto):
                _ring_load(base + li)

        def emit_alt_loads(ks, free_ms):
            for k in ks:
                tag = "wg" if k < 8 else "wu"
                idx = 8 + (k % 8)
                E("GP", lambda e, d=alt_ap(k), s=scr[tag][idx]:
                  e.dma_start(d, s),
                  waits=[scratch_ready[(tag, idx)], free_ms],
                  inc="AW%d" % k, amt=16)


        conv = []
        for c in range(8):
            conv.append(("wq", c, wq_r[c * 128:(c + 1) * 128, :], wq_s[c], 1024, F32))
        for c in range(8):
            conv.append(("wdn", c, wdn_r[c * 128:(c + 1) * 128, :], wdn_s[c], 256, F32))
        for i in range(6):
            k2, mh = i // 3, i % 3
            conv.append(("wup", i,
                         wup_r[k2 * 128:(k2 + 1) * 128, mh * 1024:(mh + 1) * 1024],
                         wup_s[i], 1024, F32))
        for c in range(8):
            conv.append(("wo", c, wo_r[c * 128:(c + 1) * 128, :], wo_s[c], 1024, F32))
        for i in range(16):
            hh, c = i // 8, i % 8
            conv.append(("wg", i,
                         wg_r[c * 128:(c + 1) * 128, hh * 1536:(hh + 1) * 1536],
                         wg_s[i], 1536, I32))
        for i in range(16):
            hh, c = i // 8, i % 8
            conv.append(("wu", i,
                         wu_r[c * 128:(c + 1) * 128, hh * 1536:(hh + 1) * 1536],
                         wu_s[i], 1536, I32))
        for kk in range(24):
            conv.append(("wd", kk, wd_r[kk * 128:(kk + 1) * 128, :], wd_s[kk], 1024, I32))

        scratch_ready = {}
        _t0_pre = []
        cast_ms = [None] * 4
        outdma_ms = [None, None]
        for i, (tag, idx, src, dst, width, sdt) in enumerate(conv):
            lpar = i % 4
            opar = i % 2
            cin = (cin_i32 if sdt == I32 else cin_f32)
            cin_v = cin[:, lpar * 1536:lpar * 1536 + width]
            cout_v = aflat_b[:, opar * 1536:opar * 1536 + width]
            ld = E("SP", lambda g, s=src, d=cin_v: g.dma_start(d, s),
                   waits=[cast_ms[lpar]], inc="C%d" % lpar, amt=16)
            if i % 2 == 0:
                cast = E("DVE", lambda v, d=cout_v, s=cin_v: v.tensor_copy(d, s),
                         waits=[ld, outdma_ms[opar]], inc="V")
            else:
                cast = E("ACT", lambda a, d=cout_v, s=cin_v: a.copy(d, s),
                         waits=[ld, outdma_ms[opar]], inc="A")
            cast_ms[lpar] = cast
            od = E("GP", lambda g, s=cout_v, d=dst: g.dma_start(d, s),
                   waits=[cast], inc="O%d" % opar, amt=16)
            outdma_ms[opar] = od
            scratch_ready[(tag, idx)] = od
            if i == 29 and not stop_after:
                _t0_pre.append(0)
                for li in range(RING):
                    _ring_load(li)
        conv_end2 = [("O0", bld.cnt["O0"]), ("O1", bld.cnt["O1"])]

        last_sq = [None]
        bank_w = [None] * 8
        bank_r = [None] * 8

        def ln_phase(specs, fin_waits=()):
            n = len(specs)
            rsm = [None] * n; mn = [None] * n; sub = [None] * n
            sq = [None] * n; var = [None] * n; sd = [None] * n; rs = [None] * n
            fins = [None] * n
            for i, (s, src_ap, ok_, sm, sp, res_ap) in enumerate(specs):
                so = s * 8
                if sp is None:
                    rsm[i] = E("DVE", lambda v, o=stt.ap()[:, so:so + 1],
                               i_=src_ap:
                               v.reduce_sum(o, i_, AX.X), waits=list(sm),
                               inc="V")
                else:
                    rsm[i] = sp
            for i, (s, src_ap, ok_, sm, sp, res_ap) in enumerate(specs):
                so = s * 8
                sub[i] = E("DVE", lambda v, o=xmbuf.ap()[:, s, :], i_=src_ap,
                           sb2=stt.ap()[:, so:so + 1].broadcast_to([128, D]):
                           v.scalar_tensor_tensor(o, sb2, -1.0 / D, i_,
                                                  ALU.mult, ALU.add),
                           waits=[rsm[i]] + list(sm), inc="V")
            for i, (s, src_ap, ok_, sm, sp, res_ap) in enumerate(specs):
                so = s * 8
                sq[i] = E("ACT", lambda a,
                          o=ebc.ap(),
                          i_=xmbuf.ap()[:, s, :],
                          ac=stt.ap()[:, so + 2:so + 3]:
                          a.activation(o, i_, AF.Square, accum_out=ac),
                          waits=[sub[i], last_sq[0]], inc="A")
                last_sq[0] = sq[i]
            for i, (s, src_ap, ok_, sm, sp, res_ap) in enumerate(specs):
                so = s * 8
                sd[i] = E("ACT", lambda a, o=stt.ap()[:, so + 4:so + 5],
                          i_=stt.ap()[:, so + 2:so + 3]:
                          a.activation(o, i_, AF.Ln, scale=1.0 / D,
                                       bias=c_eps.ap()),
                          waits=[sq[i]], inc="A")
            for i, (s, src_ap, ok_, sm, sp, res_ap) in enumerate(specs):
                so = s * 8
                rs[i] = E("ACT", lambda a, o=stt.ap()[:, so + 5:so + 6],
                          i_=stt.ap()[:, so + 4:so + 5]:
                          a.activation(o, i_, AF.Exp, scale=-0.5),
                          waits=[sd[i]], inc="A")
            for i, (s, src_ap, ok_, sm, sp, res_ap) in enumerate(specs):
                so = s * 8
                if ok_ == "xln":
                    fins[i] = E("DVE", lambda v, o=xln.ap()[:, s, :],
                                i_=xmbuf.ap()[:, s, :],
                                sc=stt.ap()[:, so + 5:so + 6]:
                                v.tensor_scalar_mul(o, i_, sc),
                                waits=[rs[i]] + list(fin_waits), inc="V")
                elif ok_ == "resid":
                    fins[i] = E("DVE", lambda v, o=res_ap, i_=xmbuf.ap()[:, s, :],
                                i1=res_ap, sc=stt.ap()[:, so + 5:so + 6]:
                                v.scalar_tensor_tensor(o, i_, sc, i1,
                                                       ALU.mult, ALU.add),
                                waits=[rs[i]], inc="V")
                else:
                    fins[i] = E("DVE", lambda v, o=res_ap, i_=xmbuf.ap()[:, s, :],
                                i1=res_ap, sc=stt.ap()[:, so + 5:so + 6],
                                ac=stt.ap()[:, so:so + 1]:
                                v.scalar_tensor_tensor(o, i_, sc, i1,
                                                       ALU.mult, ALU.add,
                                                       accum_out=ac),
                                waits=[rs[i]], inc="V")
            return fins, sub

        def emit_transposes(ln_ms):
            evs = []
            tms = []
            for s in range(SUB):
                w = [ln_ms[s], bank_r[s]]
                tm = None
                for c in range(8):
                    tm = E("PE", lambda e, o=psbf[:, s, c * 128:(c + 1) * 128],
                           i=xln.ap()[:, s, c * 128:(c + 1) * 128]:
                           e.transpose(o, i, c_identb.ap()),
                           waits=(w if c == 0 else []),
                           inc=("P" if c == 7 else None))
                bank_w[s] = tm
                ev = E("DVE", lambda v,
                       o=xlnT.ap()[:, :, s * 128:(s + 1) * 128],
                       i=psbf[:, s, :].rearrange("p (c q) -> p c q", c=8):
                       v.tensor_copy(o, i), waits=[tm], inc="V")
                bank_r[s] = ev
                evs.append(ev)
                tms.append(tm)
            return evs[-1], tms[-1]

        prev_out_ms = [None]
        for t in range(n_tiles):
            first_tile = (t == 0)
            base = t * NCH
            res_b, scr_b = (xres, ybuf) if t % 2 == 0 else (ybuf, xres)
            if t + 1 < n_tiles:
                emit_xin(t + 1, extra_waits=[prev_out_ms[0]])

            def gq(c): return base + c
            def gdn(c): return base + 8 + c
            def gup(i): return base + 16 + i
            def gwo(c): return base + 22 + c
            def gwg(i): return base + 30 + i
            def gwu(i): return base + 38 + i
            def gwd(kk): return base + 46 + kk

            if t == 0:
                ln1, _ = ln_phase([(s, res_b.ap()[:, s, :], "xln",
                                    [xin_ms[0]], None, None)
                                   for s in range(SUB)])
                xlnT_ready, _ = emit_transposes(ln1)
            else:
                xlnT_ready = pending_xlnT
            if taps and t == 0:
                d1 = nc.dram_tensor("d_xlnT1", [128, 8, NT], BF16,
                                    kind="ExternalOutput").ap()
                E("GP", lambda g, d=d1: g.dma_start(d, xlnT.ap()),
                  waits=[xlnT_ready], inc="X", amt=16)
                d2 = nc.dram_tensor("d_xln1", [128, SUB, D], BF16,
                                    kind="ExternalOutput").ap()
                E("GP", lambda g, d=d2: g.dma_start(d, xln.ap()),
                  waits=[ln1[3]], inc="X", amt=16)
            if stop_after == "ln1":
                for nm2, buf2 in [("d_xres", xres), ("d_stt", stt),
                                  ("d_xm", xmbuf)]:
                    shp2 = list(buf2.ap().shape)
                    dr2 = nc.dram_tensor(nm2, shp2, buf2.ap().dtype,
                                         kind="ExternalOutput").ap()
                    E("GP", lambda g, d=dr2, s=buf2: g.dma_start(d, s.ap()),
                      waits=[("P", bld.cnt["P"]), ("A", bld.cnt["A"]),
                             ("V", bld.cnt["V"])], inc="X", amt=16)
                emit_tile_loads(t, None); continue

            qfm_ev = []
            for m in range(8):
                b = m % 4
                w = [xlnT_ready, bank_r[b]]
                mm = None
                for k in range(8):
                    mm = E("PE", lambda e, o=psap[:, b, :],
                           l=chunk_ap(gq(k), 1024), m_=m, k_=k:
                           e.matmul(o, l[:, m_ * 128:(m_ + 1) * 128],
                                    xlnT.ap()[:, k_, :],
                                    start=(k_ == 0), stop=(k_ == 7)),
                           waits=(w + [w_ms(gq(k))] if k == 0
                                  else [w_ms(gq(k))]),
                           inc=(("F" if m == 7 else "P") if k == 7 else None))
                bank_w[b] = mm
                ev = E("ACT", lambda a, o=qfm.ap()[:, m, :], i=psap[:, b, :],
                       bi=c_bq.ap()[:, m:m + 1]:
                       a.activation(o, i, AF.Identity, bias=bi), waits=[mm], inc="A")
                bank_r[b] = ev
                qfm_ev.append(ev)
            chunk_free_val.extend([bld.cnt["F"]] * 8)
            if stop_after == "q":
                emit_tile_loads(t, None); continue

            lat_ev = []
            for m in range(2):
                b = 4 + m
                w = [xlnT_ready, bank_r[b]]
                mm = None
                for k in range(8):
                    mm = E("PE", lambda e, o=psap[:, b, :],
                           l=chunk_ap(gdn(k), 256), m_=m, k_=k:
                           e.matmul(o, l[:, m_ * 128:(m_ + 1) * 128],
                                    xlnT.ap()[:, k_, :],
                                    start=(k_ == 0), stop=(k_ == 7)),
                           waits=(w + [w_ms(gdn(k))] if k == 0
                                  else [w_ms(gdn(k))]),
                           inc=(("F" if m == 1 else "P") if k == 7 else None))
                bank_w[b] = mm
                ev = E("ACT", lambda a, o=lat.ap()[:, m, :], i=psap[:, b, :],
                       bi=c_bdn.ap()[:, m:m + 1]:
                       a.activation(o, i, AF.Identity, bias=bi), waits=[mm], inc="A")
                bank_r[b] = ev
                lat_ev.append(ev)
            chunk_free_val.extend([bld.cnt["F"]] * 8)
            if stop_after == "lat":
                emit_tile_loads(t, None); continue

            kv_ev = [None] * 24
            for g in range(24):
                b = g % 4
                w = [lat_ev[-1], bank_r[b]]
                mm = None
                for k2 in range(2):
                    mh = g // 8
                    mloc = g % 8
                    ci = gup(k2 * 3 + mh)
                    mm = E("PE", lambda e, o=psap[:, b, :],
                           l=chunk_ap(ci, 1024), ml=mloc, k_=k2:
                           e.matmul(o, l[:, ml * 128:(ml + 1) * 128],
                                    lat.ap()[:, k_, :],
                                    start=(k_ == 0), stop=(k_ == 1)),
                           waits=(w + [w_ms(ci)] if k2 == 0 else [w_ms(ci)]),
                           inc=(("F" if g == 23 else "P") if k2 == 1 else None))
                bank_w[b] = mm
                dst = kdup.ap()[:, g, :] if g < 16 else vcmp.ap()[:, g - 16, :]
                ev = E("ACT", lambda a, o=dst, i=psap[:, b, :],
                       bi=c_bup.ap()[:, g:g + 1]:
                       a.activation(o, i, AF.Identity, bias=bi), waits=[mm], inc="A")
                bank_r[b] = ev
                kv_ev[g] = ev
            chunk_free_val.extend([bld.cnt["F"]] * 6)
            if stop_after == "kv":
                emit_tile_loads(t, None); continue

            sred_ms = [None] * 128
            idx = 0
            pair_i = 0
            for c in range(8):
                for gp in range(8):
                    g = 2 * gp
                    w = [qfm_ev[c], kv_ev[g + 1]]
                    if idx >= 8:
                        w.append(sred_ms[idx - 7])
                    peng = "DVE"
                    sl = idx % 8
                    pm = E(peng, lambda v, o=pbf.ap()[:, sl:sl + 2, :],
                           a=qfm.ap()[:, c:c + 1, :].broadcast_to([128, 2, NT]),
                           b2=kdup.ap()[:, g:g + 2, :]:
                           v.tensor_tensor(o, a, b2, ALU.mult),
                           waits=w, inc=("V" if peng == "DVE" else "G"))
                    pair_i += 1
                    for q2 in range(2):
                        bank = 6 + c // 4
                        r1 = ((2 * c) % 8) * 16 + g + q2
                        first = (c % 4 == 0 and g + q2 == 0)
                        last = (c % 4 == 3 and g + q2 == 15)
                        w2 = [pm]
                        if first:
                            w2.append(bank_r[bank])
                        mm = E("PE", lambda e, o=psap[:, bank, :],
                               l=c_credc.ap()[:, 127 - r1:255 - r1],
                               pi=idx % 8, f=first, la=last:
                               e.matmul(o, l, pbf.ap()[:, pi, :],
                                        start=f, stop=la),
                               waits=(w2 if q2 == 0 else []), inc="P")
                        sred_ms[idx] = mm
                        if last:
                            bank_w[bank] = mm
                        idx += 1

            if stop_after == "scores_noevac":
                emit_tile_loads(t, None); continue
            sev = []
            for c2 in range(2):
                ev = E("ACT", lambda a, o=scev.ap()[:, c2, :],
                       i=psap[:, 6 + c2, :]: a.copy(o, i),
                       waits=[bank_w[6 + c2]], inc="A")
                bank_r[6 + c2] = ev
                sev.append(ev)
            strn_last = None
            for s in range(SUB):
                b = 4 + s // 2
                w = [sev[-1], bank_r[4], bank_r[5]]
                for c2 in range(2):
                    strn_last = E("PE", lambda e,
                                  o=psap[:, b, (s % 2) * 256 + c2 * 128:
                                         (s % 2) * 256 + (c2 + 1) * 128],
                                  i=scev.ap()[:, c2, s * 128:(s + 1) * 128]:
                                  e.transpose(o, i, c_ident3.ap()),
                                  waits=(w if (s == 0 and c2 == 0) else []),
                                  inc=("P" if (s == 3 and c2 == 1) else None))
            bank_w[4] = strn_last
            bank_w[5] = strn_last

            tmv_all = psap[:, 4:6, :].rearrange("p b (x h g) -> p (b x h) g",
                                                x=2, h=16)
            smax_f = smax.ap().rearrange("p s h -> p (s h)")
            etm_f = etm.ap().rearrange("p s q -> p (s q)")
            mx = E("DVE", lambda v, o=smax_f, i=tmv_all:
                   v.reduce_max(o, i, AX.X),
                   waits=[strn_last], inc="V")
            sb_ = E("DVE", lambda v,
                    o=etm_f.rearrange("p (a g) -> p a g", g=16),
                    i=tmv_all,
                    m=smax_f.unsqueeze(2).broadcast_to([128, 64, 16]):
                    v.tensor_tensor(o, i, m, ALU.subtract),
                    waits=[mx], inc="V")
            ex = E("ACT", lambda a, o=etm_f, i=etm_f:
                   a.activation(o, i, AF.Exp, scale=0.125),
                   waits=[sb_], inc="A")
            zs = E("DVE", lambda v, o=smax_f,
                   i=etm_f.rearrange("p (a g) -> p a g", g=16):
                   v.reduce_sum(o, i, AX.X), waits=[ex], inc="V")
            rc = E("DVE", lambda v, o=smax_f, i=smax_f: v.reciprocal(o, i),
                   waits=[zs], inc="V")
            nm = E("DVE", lambda v,
                   o=atts.ap()[:, 1024:2048].rearrange("p (a g) -> p a g", g=16),
                   i=etm_f.rearrange("p (a g) -> p a g", g=16),
                   m=smax_f.unsqueeze(2).broadcast_to([128, 64, 16]):
                   v.tensor_tensor(o, i, m, ALU.mult),
                   waits=[rc], inc="V")
            soft = [nm] * SUB
            bank_r[4] = nm
            bank_r[5] = nm

            atrn = None
            w0 = soft + [bank_r[6]]
            for c2 in range(2):
                for s in range(SUB):
                    atrn = E("PE", lambda e,
                             o=psbf[:, 6, c2 * 512 + s * 128:c2 * 512 + (s + 1) * 128],
                             i=atts.ap()[:, 1024 + s * 256 + c2 * 128:1024 + s * 256 + (c2 + 1) * 128]:
                             e.transpose(o, i, c_identb.ap()),
                             waits=(w0 if (c2 == 0 and s == 0) else []),
                             inc=("P" if (c2 == 1 and s == 3) else None))
            bank_w[6] = atrn
            attT_ev = []
            for c2 in range(2):
                ev = E("DVE", lambda v, o=atts.ap()[:, c2 * 512:(c2 + 1) * 512],
                       i=psbf[:, 6, c2 * 512:(c2 + 1) * 512]:
                       v.tensor_copy(o, i), waits=[atrn], inc="V")
                attT_ev.append(ev)
            bank_r[6] = attT_ev[-1]

            idred_ms = [None] * 128
            bc_ms = [None] * 8
            pp_ms = [None] * 4
            ao_ev = [None] * 8
            pidx0 = 0

            def emit_bcast(h, j):
                eb = j % 4
                si = (h % 8) * 8 + j
                w = [attT_ev[h // 8], bank_r[eb]]
                bc_ms[j] = E("PE", lambda e, o=psap[:, eb, :],
                             l=c_selc.ap()[:, si, :], ti=h // 8:
                             e.matmul(o, l, atts.ap()[:, ti * 512:(ti + 1) * 512],
                                      start=True, stop=True),
                             waits=w, inc="P")
                bank_w[eb] = bc_ms[j]

            def emit_prodpair(h, jp, pidx):
                b0 = (2 * jp) % 4
                w2 = [bc_ms[2 * jp + 1], kv_ev[16 + 2 * jp + 1]]
                if pidx >= 8:
                    w2.append(idred_ms[pidx - 7])
                else:
                    w2.append(sred_ms[120 + pidx + 1])
                sl = pidx % 8
                pp_ms[jp] = E("DVE", lambda v, o=pbf.ap()[:, sl:sl + 2, :],
                              a=psap[:, b0:b0 + 2, :],
                              b2=vcmp.ap()[:, 2 * jp:2 * jp + 2, :]:
                              v.tensor_tensor(o, a, b2, ALU.mult),
                              waits=w2, inc="V")
                bank_r[b0] = pp_ms[jp]
                bank_r[b0 + 1] = pp_ms[jp]

            def emit_idred(h, j, pidx, ob):
                w3 = [pp_ms[j // 2]]
                if j == 0:
                    w3.append(bank_r[ob])
                mm = E("PE", lambda e,
                       o=psap[(h % 2) * 64:(h % 2) * 64 + 64, ob, :],
                       pi=pidx % 8, f=(j == 0), la=(j == 7):
                       e.matmul(o, c_idstk.ap(), pbf.ap()[:, pi, :],
                                start=f, stop=la),
                       waits=w3, inc="P")
                idred_ms[pidx] = mm
                return mm

            st_bc = [[None] * 8, [None] * 8]
            st_pp = [[None] * 4, [None] * 4]

            def unitA(h, jp, pidx0):
                par = h % 2
                bc_ms[:] = st_bc[par]
                pp_ms[:] = st_pp[par]
                emit_bcast(h, 2 * jp)
                emit_bcast(h, 2 * jp + 1)
                emit_prodpair(h, jp, pidx0 + 2 * jp)
                st_bc[par][:] = bc_ms
                st_pp[par][:] = pp_ms

            def unitB(h, jp, pidx0, ob):
                par = h % 2
                pp_ms[:] = st_pp[par]
                last = emit_idred(h, 2 * jp, pidx0 + 2 * jp, ob)
                last = emit_idred(h, 2 * jp + 1, pidx0 + 2 * jp + 1, ob)
                return last

            for jp in range(4):
                unitA(0, jp, 0)
            for h in range(H):
                pair = h // 2
                ob = 7 - (pair % 2)
                last = None
                for jp in range(4):
                    last = unitB(h, jp, 8 * h, ob)
                    if h + 1 < H:
                        unitA(h + 1, jp, 8 * (h + 1))
                if h % 2 == 1:
                    bank_w[ob] = last
                    w4 = [bank_w[ob]]
                    if first_tile and pair == 0:
                        w4.extend(conv_end2)
                    ev = E("ACT", lambda a, o=aoT.ap()[:, pair, :],
                           i=psap[:, ob, :]: a.copy(o, i), waits=w4, inc="A")
                    bank_r[ob] = ev
                    ao_ev[pair] = ev

            attv_last = idred_ms[127]
            if stop_after is None:
                emit_alt_loads([k for k in range(16) if k not in (12, 13)],
                               attv_last)
            if stop_after == "attv":
                emit_tile_loads(t, None); continue
            wout_ev = [None] * SUB
            wout_mm = [None] * SUB
            attn_fin = [None] * SUB
            ln2 = [None] * SUB
            for pr in (0, 1):
                ss = (2 * pr, 2 * pr + 1)
                for s in ss:
                    sb0 = 4 + (s % 2) * 2
                    w = [bank_r[sb0], bank_r[sb0 + 1]] + ao_ev
                    mm = None
                    for c in range(8):
                        for half in range(2):
                            wk = [w_ms(gwo(c))] if half == 0 else []
                            mm = E("PE", lambda e, o=psap[:, sb0 + half, :],
                                   l=aoT.ap()[:, c, s * 128:(s + 1) * 128],
                                   wch=chunk_ap(gwo(c), 1024), hf=half, c_=c:
                                   e.matmul(o, l,
                                            wch[:, hf * 512:(hf + 1) * 512],
                                            start=(c_ == 0), stop=(c_ == 7)),
                                   waits=(w + wk if (c == 0 and half == 0)
                                          else wk),
                                   inc=(("F" if s == 3 else "P")
                                        if (c == 7 and half == 1) else None))
                    bank_w[sb0] = mm
                    bank_w[sb0 + 1] = mm
                    wout_mm[s] = mm
                for s in ss:
                    sb0 = 4 + (s % 2) * 2
                    ev = E("DVE", lambda v, o=xmbuf.ap()[:, s, :],
                           i=psap[:, sb0:sb0 + 2, :].rearrange(
                               "p b f -> p (b f)"),
                           ac=stt.ap()[:, s * 8:s * 8 + 1]:
                           v.scalar_tensor_tensor(o, i, 1.0, c_bout.ap(),
                                                  ALU.mult, ALU.add,
                                                  accum_out=ac),
                           waits=[wout_mm[s]], inc="V")
                    bank_r[sb0] = ev
                    bank_r[sb0 + 1] = ev
                    wout_ev[s] = ev
                af, _ = ln_phase(
                    [(s, xmbuf.ap()[:, s, :], "resid_accum", [wout_ev[s]],
                      wout_ev[s], res_b.ap()[:, s, :]) for s in ss])
                attn_fin[2 * pr:2 * pr + 2] = af
                l2, _ = ln_phase([(s, res_b.ap()[:, s, :], "xln",
                                   [attn_fin[s]], attn_fin[s], None)
                                  for s in ss])
                ln2[2 * pr:2 * pr + 2] = l2
            chunk_free_val.extend([bld.cnt["F"]] * 8)
            wout_last = wout_mm[3]
            if stop_after is None:
                emit_alt_loads([12, 13], wout_last)
            if stop_after == "wout":
                emit_tile_loads(t, None); continue

            xlnT_ready, t2_tm = emit_transposes(ln2)
            ln1_next = [None] * SUB
            if t + 1 < n_tiles:
                ln1_next, _ = ln_phase(
                    [(s, scr_b.ap()[:, s, :], "xln", [xin_ms[t + 1]],
                      None, None) for s in range(SUB)],
                    fin_waits=[("P", t2_tm[1])])
                pending_ln1 = ln1_next

            silu_ms = [None] * 24
            for hh in range(2):
                for mmi in range(12):
                    m = hh * 12 + mmi
                    b = 2 * (m % 2)
                    w = [xlnT_ready, bank_r[b]]
                    mm = None
                    for k in range(8):
                        if hh == 0:
                            lap = chunk_ap(gwg(k), 1536)
                            wk = [w_ms(gwg(k))]
                        else:
                            lap = alt_ap(k)
                            wk = [alt_w(k, t)]
                        mm = E("PE", lambda e, o=psap[:, b, :],
                               l=lap, ml=mmi, k_=k:
                               e.matmul(o, l[:, ml * 128:(ml + 1) * 128],
                                        xlnT.ap()[:, k_, :],
                                        start=(k_ == 0), stop=(k_ == 7)),
                               waits=(w + wk if k == 0 else wk),
                               inc=(("F" if (mmi == 11 and hh == 0) else "P")
                                    if k == 7 else None))
                    bank_w[b] = mm
                    w2 = [mm]
                    if first_tile and m == 0:
                        w2.extend(conv_end2)
                    ev = E("ACT", lambda a, o=hbuf.ap()[:, m, :], i=psap[:, b, :]:
                           a.activation(o, i, AF.Silu, scale=QSCALE),
                           waits=w2, inc="A")
                    bank_r[b] = ev
                    silu_ms[m] = ev
                if hh == 0:
                    chunk_free_val.extend([bld.cnt["F"]] * 8)

            stt_ms = [None] * 24
            for hh in range(2):
                for mmi in range(12):
                    m = hh * 12 + mmi
                    b = 2 * (m % 2) + 1
                    w = [xlnT_ready, bank_r[b]]
                    mm = None
                    for k in range(8):
                        if hh == 0:
                            lap = chunk_ap(gwu(k), 1536)
                            wk = [w_ms(gwu(k))]
                        else:
                            lap = alt_ap(8 + k)
                            wk = [alt_w(8 + k, t)]
                        mm = E("PE", lambda e, o=psap[:, b, :],
                               l=lap, ml=mmi, k_=k:
                               e.matmul(o, l[:, ml * 128:(ml + 1) * 128],
                                        xlnT.ap()[:, k_, :],
                                        start=(k_ == 0), stop=(k_ == 7)),
                               waits=(w + wk if k == 0 else wk),
                               inc=(("F" if (mmi == 11 and hh == 0) else "P")
                                    if k == 7 else None))
                    bank_w[b] = mm
                    ev = E("DVE", lambda v, o=hbuf.ap()[:, m, :], i=psap[:, b, :],
                           i1=hbuf.ap()[:, m, :]:
                           v.scalar_tensor_tensor(o, i, QSCALE, i1,
                                                  ALU.mult, ALU.mult),
                           waits=[mm, silu_ms[m]], inc="V")
                    bank_r[b] = ev
                    stt_ms[m] = ev
                if hh == 0:
                    chunk_free_val.extend([bld.cnt["F"]] * 8)

            for kk in range(24):
                for s in range(SUB):
                    for half in range(2):
                        b = s * 2 + half
                        w = [w_ms(gwd(kk)), stt_ms[kk]]
                        if kk == 0:
                            w.append(bank_r[b])
                        mm = E("PE", lambda e, o=psap[:, b, :],
                               l=hbuf.ap()[:, kk, s * 128:(s + 1) * 128],
                               wch=chunk_ap(gwd(kk), 1024), hf=half, k_=kk:
                               e.matmul(o, l, wch[:, hf * 512:(hf + 1) * 512],
                                        start=(k_ == 0), stop=(k_ == 23)),
                               waits=w,
                               inc=(("F" if (s == 3 and half == 1) else
                                     ("P" if kk == 23 else None))))
                        if kk == 23:
                            bank_w[b] = mm
                chunk_free_val.append(bld.cnt["F"])

            dn_ev = [None] * SUB
            for s in range(SUB):
                ev = E("DVE", lambda v, o=xmbuf.ap()[:, s, :],
                       i=psap[:, s * 2:s * 2 + 2, :].rearrange("p b f -> p (b f)"),
                       ac=stt.ap()[:, s * 8:s * 8 + 1]:
                       v.tensor_scalar(o, i, 1.0, 0.0, ALU.mult, ALU.add,
                                       accum_out=ac),
                       waits=[bank_w[s * 2], bank_w[s * 2 + 1],
                              ln1_next[s]], inc="V")
                bank_r[s * 2] = ev
                bank_r[s * 2 + 1] = ev
                dn_ev[s] = ev
            if t + 1 < n_tiles:
                pending_xlnT, _ = emit_transposes(ln1_next)
            dn_fin, _ = ln_phase(
                [(s, xmbuf.ap()[:, s, :], "resid", [dn_ev[s]],
                  dn_ev[s], res_b.ap()[:, s, :]) for s in range(SUB)])
            dst = y_out[t * NT:(t + 1) * NT, :].rearrange("(s p) d -> p s d", p=128)
            out_ms = E("GP", lambda g, d=dst, sb_=res_b: g.dma_start(d, sb_.ap()),
                       waits=dn_fin, inc="X", amt=16)
            prev_out_ms[0] = out_ms

            emit_tile_loads(t)

        if taps:
            tap_list = [
                ("d_xlnT", xlnT), ("d_qfm", qfm), ("d_lat", lat),
                ("d_kdup", kdup), ("d_vcmp", vcmp), ("d_scev", scev),
                ("d_atts", atts), ("d_aoT", aoT),
                ("d_hbuf", hbuf), ("d_ybuf", ybuf), ("d_etm", etm),
            ]
            for nm, dr_s, shp in [("d_wqs", wq_s, [8, 128, 1024]),
                                  ("d_wds", wd_s, [24, 128, 1024])]:
                drt = nc.dram_tensor(nm, shp, BF16, kind="ExternalOutput").ap()
                E("GP", lambda g, d=drt, s=dr_s: g.dma_start(d, s),
                  waits=[("X", bld.cnt["X"])], inc="X", amt=16)
            allw = [("P", bld.cnt["P"]), ("A", bld.cnt["A"]),
                    ("V", bld.cnt["V"]), ("X", bld.cnt["X"])]
            for nm, buf in tap_list:
                shp = list(buf.ap().shape)
                dt = buf.ap().dtype
                dr = nc.dram_tensor(nm, shp, dt, kind="ExternalOutput").ap()
                E("GP", lambda g, d=dr, s=buf: g.dma_start(d, s.ap()),
                  waits=allw, inc="X", amt=16)

        E("GP", lambda g: g.engine_nop(), waits=[("X", bld.cnt["X"])])

        def render(eng_name, deco):
            @deco
            def _(e):
                waited = {}
                for fn, waits, inc, amt, extra in bld.ops[eng_name]:
                    for (snm, val) in waits:
                        if waited.get(snm, -1) < val:
                            e.wait_ge(sems[snm], val)
                            waited[snm] = val
                    inst = fn(e)
                    if inc is not None:
                        inst = inst.then_inc(sems[inc], amt)
                    if extra is not None:
                        inst.then_inc(sems[extra[0]], extra[1])

        render("SP", blk.sync)
        render("GP", blk.gpsimd)
        render("PE", blk.tensor)
        render("ACT", blk.scalar)
        render("DVE", blk.vector)

    nc.finalize()
    return nc


_NC_CACHE = None


def _get_nc():
    global _NC_CACHE
    if _NC_CACHE is None:
        _NC_CACHE = build_program()
    return _NC_CACHE


def kernel(L0_attn_x, L0_attn_Wq, L0_attn_bq, L0_attn_Wdown, L0_attn_bdown,
           L0_attn_Wup, L0_attn_bup, L0_attn_Wout, L0_attn_bout,
           L0_ffn_w_gate, L0_ffn_w_up, L0_ffn_w_down):
    x = np.ascontiguousarray(np.asarray(L0_attn_x, dtype=np.float32).reshape(NTOK, D))
    wq = np.ascontiguousarray(np.asarray(L0_attn_Wq, dtype=np.float32))
    bq = np.asarray(L0_attn_bq, dtype=np.float32)
    wdn = np.ascontiguousarray(np.asarray(L0_attn_Wdown, dtype=np.float32))
    bdn = np.asarray(L0_attn_bdown, dtype=np.float32)
    wup = np.asarray(L0_attn_Wup, dtype=np.float32)
    bup = np.asarray(L0_attn_bup, dtype=np.float32)
    wo = np.ascontiguousarray(np.asarray(L0_attn_Wout, dtype=np.float32))
    bout = np.asarray(L0_attn_bout, dtype=np.float32)
    wg = np.asarray(L0_ffn_w_gate, dtype=np.int32)
    wu = np.asarray(L0_ffn_w_up, dtype=np.int32)
    wd = np.asarray(L0_ffn_w_down, dtype=np.int32)

    wup4 = wup.reshape(DL, H, 2, DH)
    kcols = wup4[:, :, 0, :]
    vcols = wup4[:, :, 1, :]
    wup_p = np.ascontiguousarray(np.concatenate(
        [np.repeat(kcols, 2, axis=1).reshape(DL, 2048),
         vcols.reshape(DL, 1024)], axis=1))
    bup4 = bup.reshape(H, 2, DH)
    bup_p = np.concatenate([np.repeat(bup4[:, 0, :], 2, axis=0).reshape(2048),
                            bup4[:, 1, :].reshape(1024)])
    bq_fm = np.ascontiguousarray(bq.reshape(8, 128).T)
    bdn_fm = np.ascontiguousarray(bdn.reshape(2, 128).T)
    bup_fm = np.ascontiguousarray(bup_p.reshape(24, 128).T)
    bout_bc = np.ascontiguousarray(np.broadcast_to(bout, (128, D)))

    wgT = np.ascontiguousarray(wg.T)
    wuT = np.ascontiguousarray(wu.T)
    wdT = np.ascontiguousarray(wd.T)

    identb = np.eye(128, dtype=bfdt)
    ident3 = np.eye(128, dtype=np.float32)
    credc = np.zeros((128, 256), dtype=bfdt)
    credc[0:64, 127] = 1.0
    credc[64:128, 143] = 1.0
    idstk = np.zeros((128, 64), dtype=bfdt)
    idstk[np.arange(128), np.arange(128) % 64] = 1.0
    selc = np.zeros((128, 64, 128), dtype=bfdt)
    for si in range(64):
        selc[2 * si, si, 0:64] = 1.0
        selc[2 * si + 1, si, 64:128] = 1.0

    shards = np.split(x, NCORES, axis=0)
    common = {
        "wq_r": wq, "wdn_r": wdn, "wup_r": wup_p, "wo_r": wo,
        "wg_r": wgT, "wu_r": wuT, "wd_r": wdT,
        "cb_identb": identb, "cb_ident3": ident3, "cb_credc": credc,
        "cb_idstk": idstk, "cb_selc": selc, "cb_bq": bq_fm, "cb_bdn": bdn_fm,
        "cb_bup": bup_fm, "cb_bout": bout_bc,
        "cb_eps": np.full((128, 1), EPS, np.float32),
    }
    in_maps = [dict(common, x=np.ascontiguousarray(s)) for s in shards]
    nc = _get_nc()
    res = run_bass_kernel_spmd(nc, in_maps, core_ids=list(range(NCORES)))
    out = np.concatenate([r["y"] for r in res.results], axis=0)
    return out.reshape(4, 8192, D)
